# version 10
# baseline (speedup 1.0000x reference)
import numpy as np
import concourse.bass as bass
import concourse.mybir as mybir
from concourse.bass_utils import run_bass_kernel_spmd

F32 = mybir.dt.float32
BF16 = mybir.dt.bfloat16
AF = mybir.ActivationFunctionType
ALU = mybir.AluOpType
X = mybir.AxisListType.X

D = 1024
SEQ = 4096
HALF = 2048
NH = 16
HD = 64
SCALE = HD ** -0.5


ENGINE_NAMES = ("tensor", "vector", "scalar", "gpsimd", "sync")
CHAIN_ALL_QUEUES = True


class _Op:
    __slots__ = ("idx", "engine", "emit", "reads", "writes", "queue", "tag",
                 "deps", "need_inc", "inc_val", "waits", "hazard", "inc_amt")

    def __init__(self, idx, engine, emit, reads, writes, queue, tag, hazard=False):
        self.idx = idx
        self.engine = engine
        self.emit = emit
        self.reads = reads
        self.writes = writes
        self.queue = queue
        self.tag = tag
        self.hazard = hazard
        self.deps = []
        self.need_inc = False
        self.inc_val = None
        self.waits = []


class _Queue:
    __slots__ = ("name", "engine", "count")

    def __init__(self, name, engine):
        self.name = name
        self.engine = engine
        self.count = 0


class Prog:
    def __init__(self, nc):
        self.nc = nc
        self.ops = []
        self.queues = {}
        self._last_write = {}
        self._readers = {}
        self._barrier = []
        self._last_dma = {}
        self.psum_prev = {}
        self.pg_scr = None
        self.qsplit = {}
        self._rr = {}
        self.chain = set()

    def palloc(self, nc, nm, shape, dtype, key=None):
        b0 = nc.psum_base
        g = nc.psum_tensor(nm, shape, dtype)
        t = g.__enter__()
        b1 = nc.psum_base
        key = key if key is not None else (nm,)
        prev = set()
        for b in range(b0, b1):
            pk = self.psum_prev.get(b)
            if pk is not None and pk != key:
                prev.add(pk)
            self.psum_prev[b] = key
        if prev and self.pg_scr is not None:
            scr = self.pg_scr
            self.op("vector", lambda e: e.memset(scr[0:1, 0:2], 0.0),
                    writes=sorted(prev) + [key], tag=f"pg_{nm}")
        return g, t

    def barrier(self):
        last = {}
        for o in self.ops:
            key = ("q", o.queue) if o.queue is not None else ("e", o.engine)
            last[key] = o.idx
        self._barrier = sorted(last.values())

    def dma_queue(self, name, engine):
        q = _Queue(name, engine)
        self.queues[name] = q
        return name

    def op(self, engine, emit, reads=(), writes=(), tag="", hazard=False):
        o = _Op(len(self.ops), engine, emit, tuple(reads), tuple(writes), None,
                tag, hazard=hazard)
        self.ops.append(o)
        self._track(o)
        return o

    def dma(self, queue, emit, reads=(), writes=(), tag="", inc=16):
        if queue in self.qsplit:
            lst = self.qsplit[queue]
            i = self._rr.get(queue, 0)
            self._rr[queue] = (i + 1) % len(lst)
            queue = lst[i]
        q = self.queues[queue]
        o = _Op(len(self.ops), q.engine, emit, tuple(reads), tuple(writes),
                queue, tag)
        o.inc_amt = inc
        self.ops.append(o)
        self._track(o)
        if q.engine == "gpsimd" or queue in self.chain or CHAIN_ALL_QUEUES:
            prev = self._last_dma.get(queue)
            if prev is not None and prev not in o.deps:
                o.deps.append(prev)
                self.ops[prev].need_inc = True
        self._last_dma[queue] = o.idx
        return o

    def _track(self, o):
        deps = set()
        for r in o.reads:
            w = self._last_write.get(r)
            if w is not None:
                deps.add(w)
            self._readers.setdefault(r, []).append(o.idx)
        for w in o.writes:
            pw = self._last_write.get(w)
            if pw is not None:
                deps.add(pw)
            for rd in self._readers.get(w, ()):
                deps.add(rd)
            self._last_write[w] = o.idx
            self._readers[w] = []
        deps.discard(o.idx)
        deps.update(self._barrier)
        deps.discard(o.idx)
        real = []
        for d in sorted(deps):
            pr = self.ops[d]
            if (pr.queue is None and pr.engine == o.engine
                    and pr.engine == "tensor" and not pr.hazard):
                continue
            real.append(d)
        o.deps = real
        for d in real:
            self.ops[d].need_inc = True

    def emit(self, final_engine="sync"):
        nc = self.nc
        eng_count = {e: 0 for e in ENGINE_NAMES}
        for o in self.ops:
            if o.queue is not None:
                q = self.queues[o.queue]
                q.count += getattr(o, "inc_amt", 16)
                o.inc_val = q.count
                o.need_inc = True
            elif o.need_inc:
                eng_count[o.engine] += 1
                o.inc_val = eng_count[o.engine]
        hw = {}
        for o in self.ops:
            for d in o.deps:
                pr = self.ops[d]
                if pr.queue is not None:
                    sem_key = ("q", pr.queue)
                else:
                    sem_key = ("e", pr.engine)
                    if pr.inc_val is None:
                        continue
                k = (o.engine, sem_key)
                if hw.get(k, 0) < pr.inc_val:
                    hw[k] = pr.inc_val
                    o.waits.append((sem_key, pr.inc_val))
        sems = {}
        for e in ENGINE_NAMES:
            if eng_count[e] > 0:
                sems[("e", e)] = nc.alloc_semaphore(f"sem_{e}")
        for qn, q in self.queues.items():
            if q.count > 0:
                sems[("q", qn)] = nc.alloc_semaphore(f"sem_q_{qn}")

        per_engine = {e: [] for e in ENGINE_NAMES}
        for o in self.ops:
            per_engine[o.engine].append(o)

        final_waits = []
        for qn, q in self.queues.items():
            if q.count > 0:
                final_waits.append((("q", qn), q.count))
        for e in ENGINE_NAMES:
            if eng_count[e] > 0 and e != final_engine:
                final_waits.append((("e", e), eng_count[e]))

        with nc.Block() as block:
            for e in ENGINE_NAMES:
                ops = per_engine[e]
                if not ops and e != final_engine:
                    continue

                def make_body(eng_name, eng_ops):
                    def body(eng):
                        for o in eng_ops:
                            for sem_key, val in o.waits:
                                eng.wait_ge(sems[sem_key], val)
                            inst = o.emit(eng)
                            if o.queue is not None:
                                inst.then_inc(sems[("q", o.queue)],
                                              getattr(o, "inc_amt", 16))
                            elif o.inc_val is not None:
                                inst.then_inc(sems[("e", eng_name)], 1)
                        if eng_name == final_engine:
                            for sem_key, val in final_waits:
                                eng.wait_ge(sems[sem_key], val)
                    return body

                getattr(block, e)(make_body(e, ops))

    def stats(self):
        n_inc = sum(1 for o in self.ops if o.inc_val is not None)
        n_wait = sum(len(o.waits) for o in self.ops)
        per_eng = {}
        for o in self.ops:
            per_eng[o.engine] = per_eng.get(o.engine, 0) + 1
        return dict(n_ops=len(self.ops), n_inc=n_inc, n_wait=n_wait,
                    per_engine=per_eng)


class KeyPrefix:

    def __init__(self, p, pfx, exempt=()):
        self.p = p
        self.pfx = pfx
        self.exempt = set(exempt)

    def _k(self, keys):
        out = []
        for k in keys:
            if k[0] in self.exempt:
                out.append(k)
            else:
                out.append((self.pfx + k[0],) + tuple(k[1:]))
        return out

    def op(self, engine, emit, reads=(), writes=(), **kw):
        return self.p.op(engine, emit, self._k(reads), self._k(writes), **kw)

    def dma(self, queue, emit, reads=(), writes=(), **kw):
        return self.p.dma(queue, emit, self._k(reads), self._k(writes), **kw)

    def palloc(self, nc, nm, shape, dtype, key=None):
        key = self._k([key if key is not None else (nm,)])[0]
        return self.p.palloc(nc, nm, shape, dtype, key=key)

    def barrier(self):
        self.p.barrier()

    @property
    def queues(self):
        return self.p.queues


class NPfx:

    def __init__(self, nc, pfx):
        self._nc = nc
        self._pfx = pfx

    def alloc_sbuf_tensor(self, name, shape, dtype, **kw):
        return self._nc.alloc_sbuf_tensor(self._pfx + name, shape, dtype, **kw)

    def sbuf_tensor(self, name, shape, dtype, **kw):
        return self._nc.sbuf_tensor(self._pfx + name, shape, dtype, **kw)

    def psum_tensor(self, name, shape, dtype, **kw):
        return self._nc.psum_tensor(self._pfx + name, shape, dtype, **kw)

    def dram_tensor(self, name, *a, **kw):
        return self._nc.dram_tensor(self._pfx + name, *a, **kw)

    def __getattr__(self, a):
        return getattr(self._nc, a)




def make_identity(nc, ident):
    nc.gpsimd.memset(ident, 0.0)
    return nc.gpsimd.affine_select(
        out=ident, in_=ident, compare_op=ALU.not_equal,
        fill=1.0, base=0, pattern=[[-1, ident.shape[0]]], channel_multiplier=1)


class Common:
    def __init__(self, nc, p):
        self.identf = nc.alloc_sbuf_tensor("c_identf", [128, 128], F32)
        self.identb = nc.alloc_sbuf_tensor("c_identb", [128, 128], BF16)
        self.onesf = nc.alloc_sbuf_tensor("c_onesf", [128, 128], F32)
        self.onesb = nc.alloc_sbuf_tensor("c_onesb", [128, 128], BF16)
        self.hmask = nc.alloc_sbuf_tensor("c_hmask", [128, 2], BF16)
        self.pgscr = nc.alloc_sbuf_tensor("c_pgscr", [1, 2], F32)
        p.pg_scr = self.pgscr
        for t, k in ((self.identf, "c_identf"), (self.identb, "c_identb")):
            p.op("gpsimd", lambda e, t=t: e.memset(t[:], 0.0),
                 writes=[(k,)], hazard=True)
            p.op("gpsimd", lambda e, t=t: e.affine_select(
                out=t[:], in_=t[:], compare_op=ALU.not_equal,
                fill=1.0, base=0, pattern=[[-1, 128]], channel_multiplier=1),
                reads=[(k,)], writes=[(k,)], hazard=True)
        p.op("vector", lambda e: e.memset(self.onesf[:], 1.0), writes=[("c_onesf",)])
        p.op("vector", lambda e: e.memset(self.onesb[:], 1.0), writes=[("c_onesb",)])
        p.op("vector", lambda e: e.memset(self.hmask[:], 0.0), writes=[("c_hmask",)])
        p.op("vector", lambda e: e.memset(self.hmask[0:64, 0:1], 1.0),
             writes=[("c_hmask",)])
        p.op("vector", lambda e: e.memset(self.hmask[64:128, 1:2], 1.0),
             writes=[("c_hmask",)])


def ln_tiles(nc, p, cm, name, x_ap, n_tiles, own_tiles, xnT_own, xnT_par,
             pT, qdma, invn=None, eps=1e-5, out_tm=None, src_rk=None):
    guards = []

    def ra(nm, shape, dtype):
        g = nc.sbuf_tensor(nm, shape, dtype)
        t = g.__enter__()
        guards.append(g)
        return t

    xt = [ra(f"{name}_xt{j}", [128, D], F32) for j in range(2)]
    xm = [ra(f"{name}_xm{j}", [128, D], BF16) for j in range(2)]
    dmy = ra(f"{name}_dmy", [128, D], F32)
    st = ra(f"{name}_st", [128, 8], F32)
    SK = (f"{name}_st",)
    for i in range(n_tiles):
        j = i % 2
        xk = (f"{name}_xt", j)
        p.dma(qdma, lambda e, i=i, j=j: e.dma_start(
            out=xt[j][:], in_=x_ap[i * 128:(i + 1) * 128, :]),
            reads=(src_rk(i) if src_rk else []), writes=[xk])
        p.op("vector", lambda e, j=j: e.reduce_sum(st[:, 0:1], xt[j][:], axis=X),
             reads=[xk], writes=[SK], hazard=True)
        p.op("scalar", lambda e, j=j: e.activation(dmy[:], xt[j][:], AF.Square,
                                                   accum_out=st[:, 1:2]),
             reads=[xk], writes=[SK, (f"{name}_dmy",)], hazard=True)
        p.op("vector", lambda e: e.tensor_scalar_mul(st[:, 2:3], st[:, 0:1],
                                                     1.0 / D),
             reads=[SK], writes=[SK])
        p.op("vector", lambda e: e.scalar_tensor_tensor(
            st[:, 6:7], st[:, 2:3], 1.0, st[:, 2:3], ALU.mult, ALU.mult),
            reads=[SK], writes=[SK])
        p.op("vector", lambda e: e.scalar_tensor_tensor(
            st[:, 3:4], st[:, 1:2], 1.0 / D, st[:, 6:7], ALU.mult, ALU.subtract),
            reads=[SK], writes=[SK])
        p.op("vector", lambda e: e.tensor_scalar_add(st[:, 6:7], st[:, 3:4], eps),
             reads=[SK], writes=[SK])
        p.op("scalar", lambda e: e.activation(st[:, 7:8], st[:, 6:7], AF.Sqrt),
             reads=[SK], writes=[SK], hazard=True)
        p.op("vector", lambda e: e.reciprocal(st[:, 4:5], st[:, 7:8]),
             reads=[SK], writes=[SK], hazard=True)
        p.op("vector", lambda e: e.scalar_tensor_tensor(
            st[:, 5:6], st[:, 2:3], -1.0, st[:, 4:5], ALU.mult, ALU.mult),
            reads=[SK], writes=[SK])
        p.op("scalar", lambda e, j=j: e.activation(
            xm[j][:], xt[j][:], AF.Identity, bias=st[:, 5:6], scale=st[:, 4:5]),
            reads=[xk, SK], writes=[(f"{name}_xm", j)])
        if invn is not None and i < own_tiles:
            p.op("vector", lambda e: e.tensor_scalar_mul(st[:, 7:8], st[:, 3:4],
                                                         float(D)),
                 reads=[SK], writes=[SK])
            p.op("vector", lambda e: e.reciprocal(st[:, 7:8], st[:, 7:8]),
                 reads=[SK], writes=[SK], hazard=True)
            p.op("vector", lambda e: e.tensor_mul(st[:, 7:8], st[:, 7:8],
                                                  st[:, 6:7]),
                 reads=[SK], writes=[SK])
            p.op("scalar", lambda e, i=i: e.activation(invn[:, i:i + 1],
                                                       st[:, 7:8], AF.Sqrt),
                 reads=[SK], writes=[("invn",)], hazard=True)
        if out_tm is not None:
            p.op("vector", lambda e, j=j, i=i: e.tensor_copy(
                out=out_tm[:, i * D:(i + 1) * D], in_=xm[j][:]),
                reads=[(f"{name}_xm", j)], writes=[("xtm", i // 4)])
        dst = xnT_own if i < own_tiles else xnT_par
        dk = "xno" if i < own_tiles else "xnp"
        loc = i if i < own_tiles else i - own_tiles
        for c in range(8):
            s = c % 2
            p.op("tensor", lambda e, j=j, c=c, s=s: e.transpose(
                pT[s][:], xm[j][:, c * 128:(c + 1) * 128], cm.identb[:]),
                reads=[(f"{name}_xm", j), ("c_identb",)], writes=[("pT", s)])
            p.op("vector", lambda e, c=c, s=s, loc=loc, dst=dst: e.tensor_copy(
                out=dst[:, c * HALF + loc * 128:c * HALF + loc * 128 + 128],
                in_=pT[s][:]),
                reads=[("pT", s)], writes=[(dk, loc // 4)])
    return guards


def free_guards(guards):
    for g in reversed(guards):
        g.__exit__(None, None, None)




def emit_l0(nc, p, cm, tapmap, ins, y1, stop_after="full"):
    _stops = ["consts", "ln1t", "ln", "pair_k", "pair_q", "pair_v", "pair_a", "pairs", "proj", "gating", "ffn", "full"]
    def _past(ph):
        return _stops.index(stop_after) <= _stops.index(ph)
    x = ins["x"]
    w_qkv = ins["w_qkv"]
    w_proj = ins["w_proj"]
    b_proj = ins["b_proj"]
    gate_w = ins["gate_w"]
    w_gate = ins["w_gate"]
    w_up = ins["w_up"]
    w_down = ins["w_down"]
    sw_gate = ins["sw_gate"]
    sw_up = ins["sw_up"]
    sw_down = ins["sw_down"]
    qx, qw, qm = "qx", "qw", "qm"

    g_dacc = nc.sbuf_tensor("dacc", [128, 8 * HALF], F32, side="right")
    dacc = g_dacc.__enter__()
    g_xnT_own = nc.sbuf_tensor("xnT_own", [128, 8 * HALF], BF16, side="right")
    xnT_own = g_xnT_own.__enter__()
    g_attnoT = nc.sbuf_tensor("attnoT", [128, 8 * HALF], BF16, side="right")
    attnoT = g_attnoT.__enter__()
    g_xnT_par = nc.sbuf_tensor("xnT_par", [128, 8 * HALF], BF16, side="right")
    xnT_par = g_xnT_par.__enter__()
    invn = nc.alloc_sbuf_tensor("invn", [128, 16], F32)
    negM = nc.alloc_sbuf_tensor("negM", [128, 16], F32)
    bpr = nc.alloc_sbuf_tensor("bpr", [128, 8], F32)

    def palloc(nm, shape, dtype, key=None):
        return p.palloc(nc, nm, shape, dtype, key=key)

    g_pT0, pT0 = palloc("pT0", [128, 128], BF16, key=("pT", 0))
    g_pT1, pT1 = palloc("pT1", [128, 128], BF16, key=("pT", 1))

    for c in range(8):
        p.dma(qx, lambda e, c=c: e.dma_start(
            out=bpr[:, c:c + 1],
            in_=b_proj[c * 128:(c + 1) * 128].rearrange("(a b) -> a b", b=1)),
            writes=[("bpr",)])

    if _past("consts"):
        p.dma(qx, lambda e: e.dma_start(out=y1[0:128, 0:8], in_=bpr[:]),
              reads=[("bpr",)])
        return True

    tapmap["xnT_own"] = (xnT_own, [("xno", i) for i in range(4)])
    tapmap["xnT_par"] = (xnT_par, [("xnp", i) for i in range(4)])
    tapmap["invn"] = (invn, [("invn",)])
    tapmap["negM"] = (negM, [("negM",)])

    n_ln_tiles = 1 if stop_after == "ln1t" else 32
    lng = ln_tiles(nc, p, cm, "ln0", x, n_ln_tiles, 16, xnT_own, xnT_par,
                   [pT0, pT1], qx, invn=invn)
    free_guards(lng)
    p.barrier()
    if _past("ln1t"):
        p.dma(qx, lambda e: e.dma_start(out=y1[0:128, 0:8], in_=bpr[:]),
              reads=[("bpr",)])
        return True

    if _past("ln"):
        p.dma(qx, lambda e: e.dma_start(out=y1[0:128, 0:8], in_=bpr[:]),
              reads=[("bpr",)])
        return True

    g_pT1.__exit__(None, None, None)
    g_pT0.__exit__(None, None, None)
    g_pKV0, pKV0 = palloc("pKV0", [128, 512], F32)
    g_pKV1, pKV1 = palloc("pKV1", [128, 512], F32)
    g_pS0, pS0 = palloc("pS0", [128, 512], F32)
    g_pS1, pS1 = palloc("pS1", [128, 512], F32)
    g_pO, pO = palloc("pO", [128, 512], F32)
    g_pB, pB = palloc("pB", [64, 512], F32)


    def xn_rhs(dc, t0, w):
        if t0 < HALF:
            return (xnT_own[:, dc * HALF + t0:dc * HALF + t0 + w],
                    ("xno", t0 // 512))
        t = t0 - HALF
        return (xnT_par[:, dc * HALF + t:dc * HALF + t + w], ("xnp", t // 512))

    pair_guards = []

    def rpa(nm, shape, dtype):
        g = nc.sbuf_tensor(nm, shape, dtype, side="right")
        t = g.__enter__()
        pair_guards.append(g)
        return t

    sqs = rpa("sqs", [128, 512], BF16)
    nsc = rpa("nsc", [2, 16], F32)
    diag = rpa("diag", [2, 2], F32)
    pts = [rpa(f"pts{j}", [128, 512], BF16) for j in range(4)]
    rown = rpa("rown", [128, 512], F32)
    bcs = rpa("bcs", [64, 512], F32)
    ostg = rpa("ostg", [64, 512], BF16)
    wq_g = rpa("wq_g", [128, 8 * 128], BF16)
    wk_g = rpa("wk_g", [128, 8 * 128], BF16)
    wv_g = rpa("wv_g", [128, 8 * 128], BF16)
    kT = rpa("kT", [128, SEQ], BF16)
    qT = rpa("qT", [128, HALF], BF16)
    vt = rpa("vt", [128, 32 * 256], BF16)

    tapmap["kT"] = (kT, [("kT", i) for i in range(4)])
    tapmap["qT"] = (qT, [("qT", i) for i in range(4)])
    tapmap["vt"] = (vt, [("vt", i) for i in range(32)] + [("vt_ones",)])
    vt_r = vt[:].rearrange("p (i hh c) -> p i hh c", i=32, hh=2)
    p.op("vector", lambda e: e.memset(vt[:], 0.0), writes=[("vt_ones",)])
    p.op("vector", lambda e: e.memset(vt_r[:, :, :, 64:65], 1.0),
         reads=[("vt_ones",)], writes=[("vt_ones",)], hazard=True)

    import os as _os
    n_pairs = 1 if stop_after.startswith("pair_") else 8
    for kc in range(n_pairs):
        g_pN, pN = palloc(f"pN{kc}", [2, 512], F32)
        g_pM, pM = palloc(f"pM{kc}", [128, 2], F32)
        for wt, wn, col0 in ((wq_g, "wq_g", kc * 128),
                             (wk_g, "wk_g", D + kc * 128),
                             (wv_g, "wv_g", 2 * D + kc * 128)):
            p.dma(qw, lambda e, wt=wt, col0=col0: e.dma_start(
                out=wt[:].rearrange("p (dc n) -> p dc n", dc=8),
                in_=w_qkv[:, col0:col0 + 128].rearrange("(dc p) n -> p dc n",
                                                        p=128)),
                writes=[(wn,)], tag=f"load_{wn}{kc}")

        p.op("vector", lambda e: e.memset(nsc[:, 0:2], 0.0), writes=[("nsc",)])
        for t in range(int(_os.environ.get("KWIN", "8"))):
            ps, pk = (pKV0, ("pKV0",)) if t % 2 == 0 else (pKV1, ("pKV1",))
            _kdc = int(_os.environ.get("KDC", "8"))
            for dc in range(_kdc):
                rhs, rk = xn_rhs(dc, t * 512, 512)
                p.op("tensor", lambda e, dc=dc, rhs=rhs, ps=ps, _kdc=_kdc: e.matmul(
                    ps[:], wk_g[:, dc * 128:(dc + 1) * 128],
                    rhs, start=(dc == 0), stop=(dc == _kdc - 1)),
                    reads=[("wk_g",), rk], writes=[pk])
            _kp = int(_os.environ.get("KPART", "5"))
            if _kp >= 2:
                p.op("scalar", lambda e, t=t, ps=ps: e.copy(
                    out=kT[:, t * 512:t * 512 + 512], in_=ps[:]),
                    reads=[pk], writes=[("kT", t // 2)])
            if _kp >= 3:
                p.op("vector", lambda e, t=t: e.tensor_mul(
                    sqs[:], kT[:, t * 512:t * 512 + 512],
                    kT[:, t * 512:t * 512 + 512]),
                    reads=[("kT", t // 2)], writes=[("sqs",)])
            if _kp >= 4:
                p.op("tensor", lambda e, pN=pN: e.matmul(
                    pN[:], cm.hmask[:], sqs[:], start=True, stop=True),
                    reads=[("sqs",), ("c_hmask",)], writes=[(f"pN{kc}",)])
            if _kp >= 5:
                p.op("vector", lambda e, pN=pN: e.reduce_max(
                    nsc[:, 2:3], pN[:], axis=X),
                    reads=[(f"pN{kc}",)], writes=[("nsc",)], hazard=True)
                p.op("vector", lambda e: e.tensor_max(nsc[:, 1:2], nsc[:, 1:2],
                                                      nsc[:, 2:3]),
                     reads=[("nsc",)], writes=[("nsc",)])
        for t in range(4 if not _past("pair_k") else 0):
            ps, pk = (pKV0, ("pKV0",)) if t % 2 == 0 else (pKV1, ("pKV1",))
            for dc in range(8):
                rhs, rk = xn_rhs(dc, t * 512, 512)
                p.op("tensor", lambda e, dc=dc, rhs=rhs, ps=ps: e.matmul(
                    ps[:], wq_g[:, dc * 128:(dc + 1) * 128],
                    rhs, start=(dc == 0), stop=(dc == 7)),
                    reads=[("wq_g",), rk], writes=[pk])
            p.op("scalar", lambda e, t=t, ps=ps: e.copy(
                out=qT[:, t * 512:t * 512 + 512], in_=ps[:]),
                reads=[pk], writes=[("qT", t)])
            p.op("vector", lambda e, t=t: e.tensor_mul(
                sqs[:], qT[:, t * 512:t * 512 + 512],
                qT[:, t * 512:t * 512 + 512]),
                reads=[("qT", t)], writes=[("sqs",)])
            p.op("tensor", lambda e, pN=pN: e.matmul(pN[:], cm.hmask[:], sqs[:],
                                              start=True, stop=True),
                 reads=[("sqs",), ("c_hmask",)], writes=[(f"pN{kc}",)])
            p.op("vector", lambda e, pN=pN: e.reduce_max(nsc[:, 3:4], pN[:], axis=X),
                 reads=[(f"pN{kc}",)], writes=[("nsc",)], hazard=True)
            p.op("vector", lambda e: e.tensor_max(nsc[:, 0:1], nsc[:, 0:1],
                                                  nsc[:, 3:4]),
                 reads=[("nsc",)], writes=[("nsc",)])
        if _past("pair_q"):
            continue
        p.op("vector", lambda e: e.tensor_mul(nsc[:, 4:5], nsc[:, 0:1],
                                              nsc[:, 1:2]),
             reads=[("nsc",)], writes=[("nsc",)])
        p.op("scalar", lambda e: e.activation(nsc[:, 5:6], nsc[:, 4:5], AF.Sqrt),
             reads=[("nsc",)], writes=[("nsc",)], hazard=True)
        p.op("vector", lambda e: e.tensor_scalar_mul(nsc[:, 5:6], nsc[:, 5:6],
                                                     -SCALE),
             reads=[("nsc",)], writes=[("nsc",)], hazard=True)
        p.op("vector", lambda e: e.tensor_mul(
            diag[:], nsc[:, 5:6].to_broadcast([2, 2]), cm.identf[0:2, 0:2]),
            reads=[("nsc",), ("c_identf",)], writes=[("diag",)])
        h0 = 2 * kc
        p.op("tensor", lambda e: e.matmul(
            pM[:], cm.onesf[0:2, 0:128], diag[:], start=True, stop=True),
            reads=[("diag",), ("c_onesf",)], writes=[(f"pM{kc}",)])
        p.op("vector", lambda e, h0=h0, pM=pM, kc=kc: e.tensor_copy(
            out=negM[:, h0:h0 + 2], in_=pM[:]),
            reads=[(f"pM{kc}",)], writes=[("negM",)])
        g_pM.__exit__(None, None, None)
        g_pN.__exit__(None, None, None)
        g_pOB, pOB = palloc(f"pOB{kc}", [128, 512], F32)

        for i in range(32 if not _past("pair_v") else 0):
            ps, pk = (pKV0, ("pKV0",)) if i % 2 == 0 else (pKV1, ("pKV1",))
            for dc in range(8):
                lhs, rk = xn_rhs(dc, i * 128, 128)
                p.op("tensor", lambda e, lhs=lhs, dc=dc, ps=ps: e.matmul(
                    ps[:, 0:128], lhs, wv_g[:, dc * 128:(dc + 1) * 128],
                    start=(dc == 0), stop=(dc == 7)),
                    reads=[("wv_g",), rk], writes=[pk])
            p.op("vector", lambda e, i=i, ps=ps: e.tensor_copy(
                out=vt_r[:, i, :, 0:64],
                in_=ps[:, 0:128].rearrange("p (hh c) -> p hh c", hh=2)),
                reads=[pk], writes=[("vt", i)])

        if _past("pair_a"):
            g_pOB.__exit__(None, None, None)
            continue
        pSa = (pS0, pS1)
        pSak = (("pS0",), ("pS1",))
        pSb = (pKV0, pKV1)
        pSbk = (("pKV0",), ("pKV1",))
        pOk = ("pO",)
        pOBk = (f"pOB{kc}",)

        def emit_S2(tc, q0, qb):
            sa, sak = pSa[tc % 2], pSak[tc % 2]
            sb, sbk = pSb[tc % 2], pSbk[tc % 2]
            p.op("tensor", lambda e, tc=tc, q0=q0, sa=sa: e.matmul(
                sa[:], kT[0:64, tc * 128:tc * 128 + 128],
                qT[0:64, q0:q0 + 512],
                start=True, stop=True, tile_position=(0, 0)),
                reads=[("kT", tc // 8), ("qT", qb)], writes=[sak])
            p.op("tensor", lambda e, tc=tc, q0=q0, sb=sb: e.matmul(
                sb[:], kT[64:128, tc * 128:tc * 128 + 128],
                qT[64:128, q0:q0 + 512],
                start=True, stop=True, tile_position=(64, 0)),
                reads=[("kT", tc // 8), ("qT", qb)], writes=[sbk])

        hA, hB = 2 * kc, 2 * kc + 1
        for qb in range(4):
            q0 = qb * 512
            emit_S2(0, q0, qb)
            for tc in range(32):
                sa, sak = pSa[tc % 2], pSak[tc % 2]
                sb, sbk = pSb[tc % 2], pSbk[tc % 2]
                pa, pak = pts[tc % 2], ("pts", tc % 2)
                pb_, pbk = pts[2 + tc % 2], ("pts", 2 + tc % 2)
                p.op("scalar", lambda e, sa=sa, pa=pa, hA=hA: e.activation(
                    pa[:], sa[:], AF.Exp, bias=negM[:, hA:hA + 1], scale=SCALE),
                    reads=[sak, ("negM",)], writes=[pak])
                p.op("scalar", lambda e, sb=sb, pb_=pb_, hB=hB: e.activation(
                    pb_[:], sb[:], AF.Exp, bias=negM[:, hB:hB + 1], scale=SCALE),
                    reads=[sbk, ("negM",)], writes=[pbk])
                if tc + 1 < 32:
                    emit_S2(tc + 1, q0, qb)
                p.op("tensor", lambda e, tc=tc, pa=pa: e.matmul(
                    pO[:], vt[:, tc * 256:tc * 256 + 128],
                    pa[:], start=(tc == 0), stop=(tc == 31)),
                    reads=[pak, ("vt", tc), ("vt_ones",)], writes=[pOk])
                p.op("tensor", lambda e, tc=tc, pb_=pb_, pOB=pOB: e.matmul(
                    pOB[:], vt[:, tc * 256 + 128:tc * 256 + 256],
                    pb_[:], start=(tc == 0), stop=(tc == 31)),
                    reads=[pbk, ("vt", tc), ("vt_ones",)], writes=[pOBk])
            p.op("scalar", lambda e: e.copy(out=rown[64:65, :],
                                            in_=pO[64:65, :]),
                 reads=[pOk], writes=[("rown",)])
            p.op("vector", lambda e: e.reciprocal(rown[64:65, :],
                                                  rown[64:65, :]),
                 reads=[("rown",)], writes=[("rown",)], hazard=True)
            p.op("tensor", lambda e: e.matmul(
                pB[:], cm.onesf[64:65, 0:64], rown[64:65, :],
                start=True, stop=True, tile_position=(64, 0)),
                reads=[("rown",), ("c_onesf",)], writes=[("pB",)])
            p.op("scalar", lambda e: e.copy(out=bcs[:], in_=pB[:]),
                 reads=[("pB",)], writes=[("bcs",)])
            p.op("vector", lambda e, q0=q0, kc=kc: e.tensor_mul(
                attnoT[0:64, kc * HALF + q0:kc * HALF + q0 + 512],
                pO[0:64, :], bcs[:]),
                reads=[pOk, ("bcs",)], writes=[("aT", kc, 0, q0)])
            p.op("scalar", lambda e, pOB=pOB: e.copy(out=rown[64:65, :],
                                                     in_=pOB[64:65, :]),
                 reads=[pOBk, ("aT", kc, 0, q0)], writes=[("rown",)])
            p.op("vector", lambda e: e.reciprocal(rown[64:65, :],
                                                  rown[64:65, :]),
                 reads=[("rown",)], writes=[("rown",)], hazard=True)
            p.op("tensor", lambda e: e.matmul(
                pB[:], cm.onesf[64:65, 0:64], rown[64:65, :],
                start=True, stop=True, tile_position=(64, 0)),
                reads=[("rown",), ("c_onesf",)], writes=[("pB",)])
            p.op("scalar", lambda e: e.copy(out=bcs[:], in_=pB[:]),
                 reads=[("pB",)], writes=[("bcs",)])
            p.op("vector", lambda e, pOB=pOB: e.tensor_mul(
                ostg[:], pOB[0:64, :], bcs[:]),
                reads=[pOBk, ("bcs",)], writes=[("ostg",)])
            p.dma(qm, lambda e, q0=q0, kc=kc: e.dma_start(
                out=attnoT[64:128, kc * HALF + q0:kc * HALF + q0 + 512],
                in_=ostg[:]),
                reads=[("ostg",)], writes=[("aT", kc, 1, q0)])
        g_pOB.__exit__(None, None, None)

    if _past("pairs"):
        p.dma(qx, lambda e: e.dma_start(out=y1[0:128, 0:8], in_=bpr[:]),
              reads=[("bpr",)])
        return True

    if _past("pair_a") and not _past("pairs"):
        pass
    if stop_after.startswith("pair_"):
        p.dma(qx, lambda e: e.dma_start(out=y1[0:128, 0:8], in_=bpr[:]),
              reads=[("bpr",)])
        return True

    for g in (g_pB, g_pO, g_pS1, g_pS0, g_pKV1, g_pKV0):
        g.__exit__(None, None, None)
    free_guards(pair_guards)
    if not _os.environ.get("KEEP_PAR"):
        g_xnT_par.__exit__(None, None, None)
    p.barrier()
    g_pP0, pP0 = palloc("pP0", [128, 512], F32)
    g_pP1, pP1 = palloc("pP1", [128, 512], F32)
    tapmap["dacc"] = (dacc, [("dacc", c, w) for c in range(8) for w in range(4)])
    g_wpj = nc.sbuf_tensor("wpj", [128, 8 * 1024], BF16, side="right")
    wpj = g_wpj.__enter__()
    p.dma(qw, lambda e: e.dma_start(
        out=wpj[:].rearrange("p (c n) -> p c n", c=8),
        in_=w_proj.rearrange("(c p) n -> p c n", p=128)),
        writes=[("wpj",)])
    aT_keys = [("aT", c, par, q0) for c in range(8) for par in range(2)
               for q0 in (0, 512, 1024, 1536)]
    tapmap["attnoT"] = (attnoT, aT_keys)
    for qb in range(4):
        q0 = qb * 512
        for dc8 in range(8):
            ps, pk = (pP0, ("pP0",)) if dc8 % 2 == 0 else (pP1, ("pP1",))
            rks = [("aT", c, par, q0) for c in range(8) for par in range(2)]
            for c in range(8):
                p.op("tensor", lambda e, c=c, dc8=dc8, q0=q0, ps=ps: e.matmul(
                    ps[:], wpj[:, c * 1024 + dc8 * 128:c * 1024 + dc8 * 128 + 128],
                    attnoT[:, c * HALF + q0:c * HALF + q0 + 512],
                    start=(c == 0), stop=(c == 7)),
                    reads=[("wpj",)] + rks, writes=[pk])
            p.op("scalar", lambda e, dc8=dc8, q0=q0, ps=ps: e.activation(
                dacc[:, dc8 * HALF + q0:dc8 * HALF + q0 + 512], ps[:], AF.Identity,
                bias=bpr[:, dc8:dc8 + 1]),
                reads=[pk, ("bpr",)], writes=[("dacc", dc8, qb)])

    g_wpj.__exit__(None, None, None)
    if not _os.environ.get("KEEP_PAR"):
        g_attnoT.__exit__(None, None, None)
    p.barrier()

    if _past("proj"):
        p.dma(qx, lambda e: e.dma_start(out=y1[0:128, 0:8], in_=bpr[:]),
              reads=[("bpr",)])
        return True

    g_pP1.__exit__(None, None, None)
    g_pP0.__exit__(None, None, None)
    g_pG, pG = palloc("pG", [128, 16], F32)
    g_pGW, pGW = palloc("pGW", [128, 16], BF16)
    g_pCT, pCT = palloc("pCT", [16, 128], BF16)
    gat_guards = []

    def gga(nm, shape, dtype):
        g = nc.sbuf_tensor(nm, shape, dtype, side="right")
        t = g.__enter__()
        gat_guards.append(g)
        return t

    gst = gga("gst", [16, 4], F32)
    gwn = gga("gwn", [16, D], BF16)
    gwnT = gga("gwnT", [128, 128], BF16)
    scor = gga("scor", [128, 16], F32)
    m8 = gga("m8", [128, 8], F32)
    s8 = gga("s8", [128, 1], F32)
    r8 = gga("r8", [128, 1], F32)
    zap = gga("zap", [128, 16], F32)
    cmbf = gga("cmbf", [128, 16], BF16)
    combT = gga("combT", [16, HALF], BF16)
    tapmap["combT"] = (combT, [("combT", i) for i in range(4)])
    sel = gga("sel", [16, 16 * 128], BF16)
    g_gwf = nc.sbuf_tensor("gwf", [16, D], F32, side="right")
    gwf = g_gwf.__enter__()
    g_gsq = nc.sbuf_tensor("gsq", [16, D], F32, side="right")
    gsq = g_gsq.__enter__()

    p.dma(qx, lambda e: e.dma_start(out=gwf[:], in_=gate_w), writes=[("gwf",)])
    p.op("scalar", lambda e: e.activation(gsq[:], gwf[:], AF.Square,
                                          accum_out=gst[:, 0:1]),
         reads=[("gwf",)], writes=[("gsq",), ("gst",)], hazard=True)
    p.op("vector", lambda e: e.reciprocal(gst[:, 1:2], gst[:, 0:1]),
         reads=[("gst",)], writes=[("gst",)], hazard=True)
    p.op("scalar", lambda e: e.activation(gst[:, 2:3], gst[:, 1:2], AF.Sqrt),
         reads=[("gst",)], writes=[("gst",)], hazard=True)
    p.op("vector", lambda e: e.tensor_scalar_mul(gwn[:], gwf[:], gst[:, 2:3]),
         reads=[("gwf",), ("gst",)], writes=[("gwn",)])
    for c in range(8):
        p.op("tensor", lambda e, c=c: e.transpose(
            pGW[:], gwn[0:16, c * 128:(c + 1) * 128], cm.identb[0:16, 0:16]),
            reads=[("gwn",), ("c_identb",)], writes=[("pGW",)])
        p.op("vector", lambda e, c=c: e.tensor_copy(
            out=gwnT[:, c * 16:(c + 1) * 16], in_=pGW[:]),
            reads=[("pGW",)], writes=[("gwnT",)])
    g_gsq.__exit__(None, None, None)
    g_gwf.__exit__(None, None, None)
    p.barrier()

    def make_sel(e):
        nc.gpsimd.memset(sel[:], 0.0)
        return nc.gpsimd.affine_select(
            out=sel[:].rearrange("p (eb c) -> p eb c", eb=16),
            in_=sel[:].rearrange("p (eb c) -> p eb c", eb=16),
            compare_op=ALU.not_equal, fill=1.0, base=0,
            pattern=[[1, 16], [0, 128]], channel_multiplier=-1)
    p.op("gpsimd", lambda e: make_sel(e), writes=[("sel",)])

    for i in range(16):
        for dc in range(8):
            p.op("tensor", lambda e, i=i, dc=dc: e.matmul(
                pG[:], xnT_own[:, dc * HALF + i * 128:dc * HALF + i * 128 + 128],
                gwnT[:, dc * 16:(dc + 1) * 16],
                start=(dc == 0), stop=(dc == 7)),
                reads=[("gwnT",), ("xno", i // 4)], writes=[("pG",)])
        p.op("scalar", lambda e, i=i: e.activation(
            scor[:], pG[:], AF.Sigmoid, scale=invn[:, i:i + 1]),
            reads=[("pG",), ("invn",)], writes=[("scor",)])
        p.op("vector", lambda e: e.max(out=m8[:], in_=scor[:]),
             reads=[("scor",)], writes=[("m8",)], hazard=True)
        p.op("vector", lambda e: e.reduce_sum(s8[:], m8[:], axis=X),
             reads=[("m8",)], writes=[("s8",)], hazard=True)
        p.op("vector", lambda e: e.reciprocal(r8[:], s8[:]),
             reads=[("s8",)], writes=[("r8",)], hazard=True)
        p.op("vector", lambda e: e.match_replace(out=zap[:], in_to_replace=m8[:],
                                                 in_values=scor[:], imm_value=0.0),
             reads=[("scor",), ("m8",)], writes=[("zap",), ("m8",)],
             hazard=True)
        p.op("vector", lambda e: e.tensor_sub(zap[:], scor[:], zap[:]),
             reads=[("scor",), ("zap",)], writes=[("zap",)])
        p.op("vector", lambda e: e.tensor_scalar_mul(cmbf[:], zap[:], r8[:]),
             reads=[("zap",), ("r8",)], writes=[("cmbf",)])
        p.op("tensor", lambda e: e.transpose(
            pCT[:], cmbf[:], cm.identb[:]),
            reads=[("cmbf",), ("c_identb",)], writes=[("pCT",)])
        p.op("vector", lambda e, i=i: e.tensor_copy(
            out=combT[0:16, i * 128:(i + 1) * 128], in_=pCT[:]),
            reads=[("pCT",)], writes=[("combT", i // 4)])

    if _past("gating"):
        p.dma(qx, lambda e: e.dma_start(out=y1[0:128, 0:8], in_=bpr[:]),
              reads=[("bpr",)])
        return True

    p.barrier()
    g_pCT.__exit__(None, None, None)
    g_pGW.__exit__(None, None, None)
    g_pG.__exit__(None, None, None)
    g_pGU0, pGU0 = palloc("pGU0", [128, 512], F32)
    g_pGU1, pGU1 = palloc("pGU1", [128, 512], F32)
    g_pBC, pBC = palloc("pBC", [128, 512], F32)
    _psRg = [palloc(f"pR{j}", [128, 512], F32, key=("psR", j))
             for j in range(4)]
    psR = [t for _, t in _psRg]
    ffn_guards = []

    def ffa(nm, shape, dtype):
        g = nc.sbuf_tensor(nm, shape, dtype, side="right")
        t = g.__enter__()
        ffn_guards.append(g)
        return t

    wg = [ffa(f"wg{j}", [128, 8 * 512], BF16) for j in range(2)]
    wu = [ffa(f"wu{j}", [128, 8 * 512], BF16) for j in range(2)]
    wd = [ffa(f"wd{j}", [128, 4 * 1024], BF16) for j in range(2)]
    fbc = ffa("fbc", [128, 512], BF16)
    gsil = ffa("gsil", [128, 4 * 512], BF16)
    ub = ffa("ub", [128, 4 * 512], BF16)
    hb = ffa("hb", [128, 4 * 512], BF16)

    def load_expert(e_, j):
        if e_ < 16:
            srcs = (w_gate[e_], w_up[e_], w_down[e_])
        else:
            srcs = (sw_gate, sw_up, sw_down)
        p.dma(qw, lambda e, s=srcs[0], j=j: e.dma_start(
            out=wg[j][:].rearrange("p (dc n) -> p dc n", dc=8),
            in_=s.rearrange("(dc p) n -> p dc n", p=128)),
            writes=[("wg", j)])
        p.dma(qw, lambda e, s=srcs[1], j=j: e.dma_start(
            out=wu[j][:].rearrange("p (dc n) -> p dc n", dc=8),
            in_=s.rearrange("(dc p) n -> p dc n", p=128)),
            writes=[("wu", j)])
        p.dma(qw, lambda e, s=srcs[2], j=j: e.dma_start(
            out=wd[j][:].rearrange("p (hc n) -> p hc n", hc=4),
            in_=s.rearrange("(hc p) n -> p hc n", p=128)),
            writes=[("wd", j)])

    load_expert(0, 0)
    for ex in range(17):
        j = ex % 2
        if ex + 1 < 17:
            load_expert(ex + 1, (ex + 1) % 2)
        for w4 in range(4):
            t0 = w4 * 512
            if ex < 16:
                p.op("tensor", lambda e, ex=ex, t0=t0: e.matmul(
                    pBC[:], sel[:, ex * 128:(ex + 1) * 128],
                    combT[0:16, t0:t0 + 512], start=True, stop=True),
                    reads=[("sel",), ("combT", w4)], writes=[("pBC",)])
                p.op("scalar", lambda e: e.copy(out=fbc[:], in_=pBC[:]),
                     reads=[("pBC",)], writes=[("fbc",)])
            for hc in range(4):
                for dc in range(8):
                    p.op("tensor", lambda e, j=j, hc=hc, dc=dc, t0=t0: e.matmul(
                        pGU0[:], wg[j][:, dc * 512 + hc * 128:dc * 512 + hc * 128 + 128],
                        xnT_own[:, dc * HALF + t0:dc * HALF + t0 + 512],
                        start=(dc == 0), stop=(dc == 7)),
                        reads=[("wg", j), ("xno", w4)], writes=[("pGU0",)])
                p.op("scalar", lambda e, hc=hc: e.activation(
                    gsil[:, hc * 512:(hc + 1) * 512], pGU0[:], AF.Silu),
                    reads=[("pGU0",)], writes=[("gsil", hc)])
                for dc in range(8):
                    p.op("tensor", lambda e, j=j, hc=hc, dc=dc, t0=t0: e.matmul(
                        pGU1[:], wu[j][:, dc * 512 + hc * 128:dc * 512 + hc * 128 + 128],
                        xnT_own[:, dc * HALF + t0:dc * HALF + t0 + 512],
                        start=(dc == 0), stop=(dc == 7)),
                        reads=[("wu", j), ("xno", w4)], writes=[("pGU1",)])
                if ex < 16:
                    p.op("vector", lambda e, hc=hc: e.tensor_mul(
                        ub[:, hc * 512:(hc + 1) * 512], pGU1[:], fbc[:]),
                        reads=[("pGU1",), ("fbc",)], writes=[("ub", hc)])
                else:
                    p.op("scalar", lambda e, hc=hc: e.copy(
                        out=ub[:, hc * 512:(hc + 1) * 512], in_=pGU1[:]),
                        reads=[("pGU1",)], writes=[("ub", hc)])
                p.op("vector", lambda e, hc=hc: e.tensor_mul(
                    hb[:, hc * 512:(hc + 1) * 512],
                    gsil[:, hc * 512:(hc + 1) * 512],
                    ub[:, hc * 512:(hc + 1) * 512]),
                    reads=[("gsil", hc), ("ub", hc)], writes=[("hb", hc)])
            for half in range(2):
                for d4 in range(4):
                    dc8 = half * 4 + d4
                    for hc in range(4):
                        p.op("tensor", lambda e, j=j, hc=hc, dc8=dc8, d4=d4:
                             e.matmul(
                            psR[d4][:],
                            wd[j][:, hc * 1024 + dc8 * 128:hc * 1024 + dc8 * 128 + 128],
                            hb[:, hc * 512:(hc + 1) * 512],
                            start=(hc == 0), stop=(hc == 3)),
                            reads=[("wd", j), ("hb", hc)], writes=[("psR", d4)])
                for d4 in range(4):
                    dc8 = half * 4 + d4
                    p.op("vector", lambda e, dc8=dc8, t0=t0, d4=d4: e.tensor_add(
                        dacc[:, dc8 * HALF + t0:dc8 * HALF + t0 + 512],
                        dacc[:, dc8 * HALF + t0:dc8 * HALF + t0 + 512],
                        psR[d4][:]),
                        reads=[("psR", d4), ("dacc", dc8, w4)],
                        writes=[("dacc", dc8, w4)])

    if _past("ffn"):
        p.dma(qx, lambda e: e.dma_start(out=y1[0:128, 0:8], in_=bpr[:]),
              reads=[("bpr",)])
        return True

    for _g, _t in reversed(_psRg):
        _g.__exit__(None, None, None)
    g_pBC.__exit__(None, None, None)
    g_pGU1.__exit__(None, None, None)
    g_pGU0.__exit__(None, None, None)
    for g in reversed(ffn_guards):
        g.__exit__(None, None, None)
    for g in reversed(gat_guards):
        g.__exit__(None, None, None)
    g_xnT_own.__exit__(None, None, None)
    p.barrier()
    g_pX0, pX0 = palloc("pX0", [128, 128], F32)
    g_pX1, pX1 = palloc("pX1", [128, 128], F32)
    x1g = [nc.sbuf_tensor(f"x1t{j}", [128, D], F32, side="right") for j in range(2)]
    x1t = [g.__enter__() for g in x1g]
    y1g = [nc.sbuf_tensor(f"y1t{j}", [128, D], F32, side="right") for j in range(2)]
    y1t = [g.__enter__() for g in y1g]
    for i in range(16):
        j = i % 2
        p.dma(qx, lambda e, i=i, j=j: e.dma_start(
            out=x1t[j][:], in_=x[i * 128:(i + 1) * 128, :]),
            writes=[("x1t", j)])
        for c in range(8):
            s = c % 2
            px = pX0 if s == 0 else pX1
            pxk = ("pX0",) if s == 0 else ("pX1",)
            p.op("tensor", lambda e, c=c, i=i, px=px: e.transpose(
                px[:], dacc[:, c * HALF + i * 128:c * HALF + i * 128 + 128],
                cm.identf[:]),
                reads=[("dacc", c, i // 4), ("c_identf",)], writes=[pxk])
            p.op("vector", lambda e, c=c, j=j, px=px, pxk=pxk: e.tensor_add(
                y1t[j][:, c * 128:(c + 1) * 128], x1t[j][:, c * 128:(c + 1) * 128],
                px[:]),
                reads=[pxk, ("x1t", j)], writes=[("y1t", j)])
        p.dma(qx, lambda e, i=i, j=j: e.dma_start(
            out=y1[i * 128:(i + 1) * 128, :], in_=y1t[j][:]),
            reads=[("y1t", j)], writes=[("y1out", i)])

    g_pX1.__exit__(None, None, None)
    g_pX0.__exit__(None, None, None)
    for g in reversed(y1g):
        g.__exit__(None, None, None)
    for g in reversed(x1g):
        g.__exit__(None, None, None)
    g_dacc.__exit__(None, None, None)
    p.barrier()
    return False



def _mk_prog(nc):
    p = Prog(nc)
    p.dma_queue("qx", "sync")
    p.dma_queue("qw", "gpsimd")
    p.dma_queue("qwB", "gpsimd")
    p.dma_queue("qm", "scalar")
    p.dma_queue("qb", "sync")
    p.chain = {"qb"}
    p.qsplit = {"qw": ["qw", "qwB"]}
    return p


def _do_finish(nc, p, tapmap, taps):
    for tname in taps:
        if tname not in tapmap:
            continue
        t, keys = tapmap[tname]
        o = nc.dram_tensor("tap_" + tname, list(t.shape), t.dtype,
                           kind="ExternalOutput").ap()
        p.dma("qx", lambda e, t=t, o=o: e.dma_start(out=o, in_=t[:]),
              reads=keys)
    p.emit()
    return nc, p.stats()


def l0_dram_ins(nc):
    return dict(
        x=nc.dram_tensor("x", [SEQ, D], F32, kind="ExternalInput").ap(),
        w_qkv=nc.dram_tensor("w_qkv", [D, 3 * D], F32, kind="ExternalInput").ap(),
        w_proj=nc.dram_tensor("w_proj", [D, D], F32, kind="ExternalInput").ap(),
        b_proj=nc.dram_tensor("b_proj", [D], F32, kind="ExternalInput").ap(),
        gate_w=nc.dram_tensor("gate_w", [16, D], F32, kind="ExternalInput").ap(),
        w_gate=nc.dram_tensor("w_gate", [16, D, 512], F32,
                              kind="ExternalInput").ap(),
        w_up=nc.dram_tensor("w_up", [16, D, 512], F32, kind="ExternalInput").ap(),
        w_down=nc.dram_tensor("w_down", [16, 512, D], F32,
                              kind="ExternalInput").ap(),
        sw_gate=nc.dram_tensor("sw_gate", [D, 512], F32,
                               kind="ExternalInput").ap(),
        sw_up=nc.dram_tensor("sw_up", [D, 512], F32, kind="ExternalInput").ap(),
        sw_down=nc.dram_tensor("sw_down", [512, D], F32,
                               kind="ExternalInput").ap())


def build_l0(taps=(), stop_after="full"):
    nc = bass.Bass("TRN2", target_bir_lowering=False)
    ins = l0_dram_ins(nc)
    y1 = nc.dram_tensor("y1", [HALF, D], F32, kind="ExternalOutput").ap()
    p = _mk_prog(nc)
    tapmap = {}
    cm = Common(nc, p)
    emit_l0(nc, p, cm, tapmap, ins, y1, stop_after=stop_after)
    return _do_finish(nc, p, tapmap, taps)




def l1_dram_ins(nc, own_only=False):
    d = dict(
        Wq=nc.dram_tensor("Wq", [D, D], F32, kind="ExternalInput").ap())
    for nm, shape in (("bq", [D]), ("bk", [D]), ("bv", [D]),
                      ("b1", [512]), ("b2", [64]), ("tau", [1]),
                      ("bih", [3 * D]), ("bhh", [3 * D]), ("bq1", [256]),
                      ("bk1", [256]), ("bv1", [D]), ("ln_g", [D]),
                      ("ln_b", [D]), ("bvp", [D]), ("bo", [D])):
        d[nm] = nc.dram_tensor(nm, shape, F32, kind="ExternalInput").ap()
    for nm, shape in (("Wk", [D, D]), ("Wv", [D, D]), ("slot_init", [64, D]),
                      ("w1", [2 * D, 512]), ("w2", [512, 64]),
                      ("Wih", [3 * D, D]), ("Whh", [3 * D, D]),
                      ("wq1", [D, 256]), ("wk1", [D, 256]), ("wv1", [D, D]),
                      ("wvp", [D, D]), ("wo", [D, D])):
        d[nm] = nc.dram_tensor(nm, shape, F32, kind="ExternalInput").ap()
    return d


def emit_l1(nc, p, cm, tapmap, ins, x1, y, own_only=False, x1_rk=None,
            cc=None):
    (Wq, bq, Wk, bk, Wv, bv, slot_init, w1, b1, w2, b2, tau, Wih, Whh, bih,
     bhh, wq1, bq1, wk1, bk1, wv1, bv1, ln_g, ln_b, wvp, bvp, wo, bo) = (
        ins["Wq"], ins["bq"], ins["Wk"], ins["bk"], ins["Wv"], ins["bv"],
        ins["slot_init"], ins["w1"], ins["b1"], ins["w2"], ins["b2"],
        ins["tau"], ins["Wih"], ins["Whh"], ins["bih"], ins["bhh"],
        ins["wq1"], ins["bq1"], ins["wk1"], ins["bk1"], ins["wv1"],
        ins["bv1"], ins["ln_g"], ins["ln_b"], ins["wvp"], ins["bvp"],
        ins["wo"], ins["bo"])
    qx, qw = "qx", "qw"
    NLT = 16 if own_only else 32
    NWND = 4 if own_only else 8

    def palloc(nm, shape, dtype, key=None):
        return p.palloc(nc, nm, shape, dtype, key=key)

    bstg = [nc.alloc_sbuf_tensor(f"bstg{j}", [24, 128], F32) for j in range(2)]
    g_pBC1, pBC1 = palloc("pBC1", [128, 512], F32)
    _lc_n = [0]

    def load_cols(dst, src_1d, n_chunks, key):
        j = _lc_n[0] & 1
        _lc_n[0] += 1
        sk = (f"bstg{j}",)
        p.dma("qb", lambda e, src_1d=src_1d, n_chunks=n_chunks, j=j: e.dma_start(
            out=bstg[j][0:n_chunks, :],
            in_=src_1d.rearrange("(c p) -> c p", p=128)),
            writes=[sk])
        p.op("tensor", lambda e, n_chunks=n_chunks, j=j: e.matmul(
            pBC1[:, 0:n_chunks], bstg[j][0:n_chunks, :],
            cm.identf[0:n_chunks, 0:n_chunks], start=True, stop=True),
            reads=[sk, ("c_identf",)], writes=[("pBC1",)])
        p.op("vector", lambda e, dst=dst, n_chunks=n_chunks: e.tensor_copy(
            out=dst[:, 0:n_chunks], in_=pBC1[:, 0:n_chunks]),
            reads=[("pBC1",)], writes=[key])

    bq_s = nc.alloc_sbuf_tensor("bq_s", [128, 8], F32)
    bk_s = nc.alloc_sbuf_tensor("bk_s", [128, 8], F32)
    b1_s = nc.alloc_sbuf_tensor("b1_s", [128, 4], F32)
    bih_s = nc.alloc_sbuf_tensor("bih_s", [128, 24], F32)
    bhh_s = nc.alloc_sbuf_tensor("bhh_s", [128, 24], F32)
    bq1_s = nc.alloc_sbuf_tensor("bq1_s", [128, 2], F32)
    bk1_s = nc.alloc_sbuf_tensor("bk1_s", [128, 2], F32)
    bv1_s = nc.alloc_sbuf_tensor("bv1_s", [128, 8], F32)
    lng_s = nc.alloc_sbuf_tensor("lng_s", [128, 8], F32)
    lnb_s = nc.alloc_sbuf_tensor("lnb_s", [128, 8], F32)
    bvp_s = nc.alloc_sbuf_tensor("bvp_s", [128, 8], F32)
    b2r = nc.alloc_sbuf_tensor("b2r", [1, 64], F32)
    taur = nc.alloc_sbuf_tensor("taur", [1, 4], F32)
    bvr = nc.alloc_sbuf_tensor("bvr", [1, D], F32)
    bor = nc.alloc_sbuf_tensor("bor", [1, D], F32)
    s0f = nc.alloc_sbuf_tensor("s0f", [64, D], F32)
    load_cols(bq_s, bq, 8, ("bq_s",))
    load_cols(bk_s, bk, 8, ("bk_s",))
    load_cols(b1_s, b1, 4, ("b1_s",))
    load_cols(bih_s, bih, 24, ("bih_s",))
    load_cols(bhh_s, bhh, 24, ("bhh_s",))
    load_cols(bq1_s, bq1, 2, ("bq1_s",))
    load_cols(bk1_s, bk1, 2, ("bk1_s",))
    load_cols(bv1_s, bv1, 8, ("bv1_s",))
    load_cols(lng_s, ln_g, 8, ("lng_s",))
    load_cols(lnb_s, ln_b, 8, ("lnb_s",))
    load_cols(bvp_s, bvp, 8, ("bvp_s",))
    p.dma("qb", lambda e: e.dma_start(out=b2r[:],
                                    in_=b2.rearrange("(a b) -> a b", a=1)),
          writes=[("b2r",)])
    p.dma("qb", lambda e: e.dma_start(out=taur[:, 0:1],
                                    in_=tau.rearrange("(a b) -> a b", a=1)),
          writes=[("taur",)])
    p.dma("qb", lambda e: e.dma_start(out=bvr[:],
                                    in_=bv.rearrange("(a b) -> a b", a=1)),
          writes=[("bvr",)])
    p.dma("qb", lambda e: e.dma_start(out=bor[:],
                                    in_=bo.rearrange("(a b) -> a b", a=1)),
          writes=[("bor",)])
    p.dma("qb", lambda e: e.dma_start(out=s0f[:], in_=slot_init), writes=[("s0f",)])

    bvb = nc.alloc_sbuf_tensor("bvb", [128, D], F32)
    bob = nc.alloc_sbuf_tensor("bob", [128, D], F32)
    b2b = nc.alloc_sbuf_tensor("b2b", [128, 64], F32)
    sbc = nc.alloc_sbuf_tensor("sbc", [128, 1], F32)
    for half in range(2):
        p.op("tensor", lambda e, half=half: e.matmul(
            pBC1[:], cm.onesf[0:1, 0:128], bvr[:, half * 512:(half + 1) * 512],
            start=True, stop=True),
            reads=[("bvr",), ("c_onesf",)], writes=[("pBC1",)])
        p.op("vector", lambda e, half=half: e.tensor_copy(
            out=bvb[:, half * 512:(half + 1) * 512], in_=pBC1[:]),
            reads=[("pBC1",)], writes=[("bvb",)])
        p.op("tensor", lambda e, half=half: e.matmul(
            pBC1[:], cm.onesf[0:1, 0:128], bor[:, half * 512:(half + 1) * 512],
            start=True, stop=True),
            reads=[("bor",), ("c_onesf",)], writes=[("pBC1",)])
        p.op("vector", lambda e, half=half: e.tensor_copy(
            out=bob[:, half * 512:(half + 1) * 512], in_=pBC1[:]),
            reads=[("pBC1",)], writes=[("bob",)])
    p.op("tensor", lambda e: e.matmul(
        pBC1[:, 0:64], cm.onesf[0:1, 0:128], b2r[:], start=True, stop=True),
        reads=[("b2r",), ("c_onesf",)], writes=[("pBC1",)])
    p.op("vector", lambda e: e.tensor_copy(out=b2b[:], in_=pBC1[:, 0:64]),
         reads=[("pBC1",)], writes=[("b2b",)])
    p.op("scalar", lambda e: e.activation(taur[:, 1:2], taur[:, 0:1], AF.Abs),
         reads=[("taur",)], writes=[("taur",)], hazard=True)
    p.op("vector", lambda e: e.tensor_scalar_add(taur[:, 2:3], taur[:, 1:2], 0.1),
         reads=[("taur",)], writes=[("taur",)], hazard=True)
    p.op("vector", lambda e: e.reciprocal(taur[:, 3:4], taur[:, 2:3]),
         reads=[("taur",)], writes=[("taur",)], hazard=True)
    p.op("tensor", lambda e: e.matmul(
        pBC1[:, 0:1], cm.onesf[0:1, 0:128], taur[:, 3:4], start=True, stop=True),
        reads=[("taur",), ("c_onesf",)], writes=[("pBC1",)])
    p.op("vector", lambda e: e.tensor_copy(out=sbc[:], in_=pBC1[:, 0:1]),
         reads=[("pBC1",)], writes=[("sbc",)])
    g_pBC1.__exit__(None, None, None)
    p.barrier()

    if own_only:
        g_x2acc = nc.sbuf_tensor("x2acc", [128, 16 * D], F32, side="right")
        x2acc = g_x2acc.__enter__()
    alphaTm = nc.alloc_sbuf_tensor("alphaTm", [128, 32 * 64], BF16)
    alphaT = nc.alloc_sbuf_tensor("alphaT", [64, SEQ], BF16)
    g_xn1T_own = nc.sbuf_tensor("xn1T_own", [128, 8 * HALF], BF16, side="right")
    xn1T_own = g_xn1T_own.__enter__()
    if not own_only:
        g_xn1T_par = nc.sbuf_tensor("xn1T_par", [128, 8 * HALF], BF16,
                                    side="right")
        xn1T_par = g_xn1T_par.__enter__()
    else:
        g_xn1T_par = xn1T_par = None
    g_xn1_tm = nc.sbuf_tensor("xn1_tm", [128, NLT * D], BF16, side="right")
    xn1_tm = g_xn1_tm.__enter__()

    g_pT0, pT0 = palloc("pT0", [128, 128], BF16, key=("pT", 0))
    g_pT1, pT1 = palloc("pT1", [128, 128], BF16, key=("pT", 1))
    lng = ln_tiles(nc, p, cm, "ln1", x1, NLT, 16, xn1T_own, xn1T_par,
                   [pT0, pT1], qx, invn=None, out_tm=xn1_tm, src_rk=x1_rk)
    free_guards(lng)
    g_pT1.__exit__(None, None, None)
    g_pT0.__exit__(None, None, None)
    p.barrier()
    tapmap["xn1T_own"] = (xn1T_own, [("xno", i) for i in range(4)])
    tapmap["xn1_tm"] = (xn1_tm, [("xtm", i) for i in range(8)])

    def xn1_rhs(dc, t0, w):
        if t0 < HALF:
            return (xn1T_own[:, dc * HALF + t0:dc * HALF + t0 + w],
                    ("xno", t0 // 512))
        t = t0 - HALF
        return (xn1T_par[:, dc * HALF + t:dc * HALF + t + w], ("xnp", t // 512))


    mid1 = []

    def m1a(nm, shape, dtype):
        g = nc.sbuf_tensor(nm, shape, dtype, side="right")
        t = g.__enter__()
        mid1.append(g)
        return t

    w1t = m1a("w1t", [128, 4 * 512], BF16)
    w1b = m1a("w1b", [128, 4 * 512], BF16)
    w2s = m1a("w2s", [128, 4 * 64], BF16)
    p.dma(qw, lambda e: e.dma_start(
        out=w1t[:].rearrange("p (dc n) -> p dc n", dc=4)
        .rearrange("p dc n -> p (dc n)")
        .rearrange("p (dc n) -> p dc n", dc=4),
        in_=w1[0:D, :].rearrange("(dc p) n -> p dc n", p=128)[:, 0:4, :])
        if False else e.dma_start(
        out=w1t[:].rearrange("p (dc n) -> p dc n", dc=4),
        in_=w1[0:512, :].rearrange("(dc p) n -> p dc n", p=128))
        if False else e.dma_start(
        out=w1t[:].rearrange("p (dc n) -> p dc n", dc=4),
        in_=w1[0:D, :].rearrange("(dc p) n -> p dc n", p=128)[:, 0:4, :]),
        writes=[("w1t",)])
    p.dma(qw, lambda e: e.dma_start(
        out=w1b[:].rearrange("p (dc n) -> p dc n", dc=4),
        in_=w1[D:2 * D, :].rearrange("(dc p) n -> p dc n", p=128)[:, 0:4, :]),
        writes=[("w1b",)])
    p.dma(qw, lambda e: e.dma_start(
        out=w2s[:].rearrange("p (hc n) -> p hc n", hc=4),
        in_=w2.rearrange("(hc p) n -> p hc n", p=128)),
        writes=[("w2s",)])

    smv = m1a("smv", [128, 8], BF16)
    hbias = m1a("hbias", [128, 4], F32)
    s0b = nc.alloc_sbuf_tensor("s0b", [64, D], BF16)
    o64 = nc.alloc_sbuf_tensor("o64", [64, 1], BF16)
    g_pSM, pSM = palloc("pSM", [128, 2], F32)
    p.op("vector", lambda e: e.memset(o64[:], 1.0 / 64.0), writes=[("o64",)])
    p.op("vector", lambda e: e.tensor_copy(out=s0b[:], in_=s0f[:]),
         reads=[("s0f",)], writes=[("s0b",)])
    for c in range(8):
        p.op("tensor", lambda e, c=c: e.matmul(
            pSM[:, 0:1], s0b[0:64, c * 128:(c + 1) * 128], o64[:],
            start=True, stop=True),
            reads=[("s0b",), ("o64",)], writes=[("pSM",)])
        p.op("vector", lambda e, c=c: e.tensor_copy(out=smv[:, c:c + 1],
                                                    in_=pSM[:, 0:1]),
             reads=[("pSM",)], writes=[("smv",)])
    for jc in range(4):
        for dc in range(8):
            p.op("tensor", lambda e, jc=jc, dc=dc: e.matmul(
                pSM[:, 1:2],
                w1b[:, dc * 512 + jc * 128:dc * 512 + jc * 128 + 128]
                if dc < 4 else
                w1b[:, (dc - 4) * 512 + jc * 128:(dc - 4) * 512 + jc * 128 + 128],
                smv[:, dc:dc + 1], start=(dc == 0), stop=(dc == 7)),
                reads=[("w1b",), ("smv",)], writes=[("pSM",)])
        p.op("vector", lambda e, jc=jc: e.tensor_add(
            hbias[:, jc:jc + 1], b1_s[:, jc:jc + 1], pSM[:, 1:2]),
            reads=[("pSM",), ("b1_s",)], writes=[("hbias",)])

    hT = m1a("hT", [128, 4 * 512], BF16)
    logt = m1a("logt", [128, 64], F32)
    m8L = m1a("m8L", [128, 8], F32)
    maskL = m1a("maskL", [128, 64], F32)
    expL = m1a("expL", [128, 64], F32)
    nbt = m1a("nbt", [128, 2], F32)
    g_pH0, pH0 = palloc("pH0", [128, 512], F32)
    g_pH1, pH1 = palloc("pH1", [128, 512], F32)
    g_pL, pL = palloc("pL", [128, 64], F32)
    g_pAT, pAT = palloc("pAT", [64, 128], BF16)

    for wnd in range(NWND):
        t0 = wnd * 512
        for hc in range(4):
            ph, phk = (pH0, ("pH0",)) if hc % 2 == 0 else (pH1, ("pH1",))
            for dc in range(8):
                rhs, rk = xn1_rhs(dc, t0, 512)
                wcol = dc * 512 + hc * 128 if dc < 4 else (dc - 4) * 512 + hc * 128
                p.op("tensor", lambda e, hc=hc, dc=dc, rhs=rhs, ph=ph: e.matmul(
                    ph[:], w1t[:, (dc % 4) * 512 + hc * 128:(dc % 4) * 512 + hc * 128 + 128]
                    if False else w1t[:, hc * 512 + 0:hc * 512 + 512][:, 0:128]
                    if False else w1t[:, hc * 128 + 0:hc * 128 + 128],
                    rhs, start=(dc == 0), stop=(dc == 7)),
                    reads=[("w1t",), rk], writes=[phk])
            p.op("scalar", lambda e, hc=hc, ph=ph: e.activation(
                hT[:, hc * 512:(hc + 1) * 512], ph[:], AF.Gelu,
                bias=hbias[:, hc:hc + 1]),
                reads=[phk, ("hbias",)], writes=[("hT", hc)])
        for it in range(4):
            ti = wnd * 4 + it
            for hc in range(4):
                p.op("tensor", lambda e, hc=hc, it=it: e.matmul(
                    pL[:], hT[:, hc * 512 + it * 128:hc * 512 + it * 128 + 128],
                    w2s[:, hc * 64:(hc + 1) * 64],
                    start=(hc == 0), stop=(hc == 3)),
                    reads=[("hT", hc), ("w2s",)], writes=[("pL",)])
            p.op("vector", lambda e: e.tensor_add(logt[:], pL[:], b2b[:]),
                 reads=[("pL",), ("b2b",)], writes=[("logt",)])
            p.op("vector", lambda e: e.memset(m8L[:], -1e30), writes=[("m8L",)])
            p.op("vector", lambda e: e.max(out=m8L[:], in_=logt[:]),
                 reads=[("logt",), ("m8L",)], writes=[("m8L",)], hazard=True)
            p.op("vector", lambda e: e.tensor_scalar(
                maskL[:], logt[:], m8L[:, 7:8], None, ALU.is_ge),
                reads=[("logt",), ("m8L",)], writes=[("maskL",)])
            p.op("vector", lambda e: e.scalar_tensor_tensor(
                nbt[:, 0:1], m8L[:, 0:1], -1.0, sbc[:], ALU.mult, ALU.mult),
                reads=[("m8L",), ("sbc",)], writes=[("nbt",)])
            p.op("scalar", lambda e: e.activation(
                expL[:], logt[:], AF.Exp, bias=nbt[:, 0:1], scale=sbc[:]),
                reads=[("logt",), ("nbt",), ("sbc",)], writes=[("expL",)])
            p.op("vector", lambda e: e.tensor_mul(expL[:], expL[:], maskL[:]),
                 reads=[("expL",), ("maskL",)], writes=[("expL",)], hazard=True)
            p.op("vector", lambda e: e.reduce_sum(nbt[:, 1:2], expL[:], axis=X),
                 reads=[("expL",)], writes=[("nbt",)], hazard=True)
            p.op("vector", lambda e: e.reciprocal(nbt[:, 1:2], nbt[:, 1:2]),
                 reads=[("nbt",)], writes=[("nbt",)], hazard=True)
            p.op("vector", lambda e, ti=ti: e.tensor_scalar_mul(
                alphaTm[:, ti * 64:(ti + 1) * 64], expL[:], nbt[:, 1:2]),
                reads=[("expL",), ("nbt",)], writes=[("aTm", ti)])
            p.op("tensor", lambda e, ti=ti: e.transpose(
                pAT[:], alphaTm[:, ti * 64:(ti + 1) * 64], cm.identb[:]),
                reads=[("aTm", ti), ("c_identb",)], writes=[("pAT",)])
            p.op("vector", lambda e, ti=ti: e.tensor_copy(
                out=alphaT[0:64, ti * 128:(ti + 1) * 128], in_=pAT[:]),
                reads=[("pAT",)], writes=[("aT1", ti)])
    tapmap["alphaTm"] = (alphaTm, [("aTm", t) for t in range(NLT)])

    slotin = nc.alloc_sbuf_tensor("slotin", [64, D], F32)
    denom = nc.alloc_sbuf_tensor("denom", [64, 2], F32)
    ob1 = nc.alloc_sbuf_tensor("ob1", [128, 1], BF16)
    g_pSI0, pSI0 = palloc("pSI0", [64, 512], F32)
    g_pSI1, pSI1 = palloc("pSI1", [64, 512], F32)
    p.op("vector", lambda e: e.memset(ob1[:], 1.0), writes=[("ob1",)])
    NST = NLT
    for dh in range(2):
        ps, psk = (pSI0, ("pSI0",)) if dh == 0 else (pSI1, ("pSI1",))
        for ti in range(NST):
            p.op("tensor", lambda e, dh=dh, ti=ti, ps=ps: e.matmul(
                ps[:], alphaTm[:, ti * 64:(ti + 1) * 64],
                xn1_tm[:, ti * D + dh * 512:ti * D + dh * 512 + 512],
                start=(ti == 0), stop=(ti == NST - 1)),
                reads=[("aTm", ti), ("xtm", ti // 4)], writes=[psk])
        p.op("vector", lambda e, dh=dh, ps=ps: e.tensor_copy(
            out=slotin[:, dh * 512:(dh + 1) * 512], in_=ps[:]),
            reads=[psk], writes=[("slotin",)])
    for ti in range(NST):
        p.op("tensor", lambda e, ti=ti: e.matmul(
            pSM[:, 0:1][0:64, :], alphaTm[:, ti * 64:(ti + 1) * 64], ob1[:],
            start=(ti == 0), stop=(ti == NST - 1)),
            reads=[("aTm", ti), ("ob1",)], writes=[("pSM",)])
    if own_only:
        ccstg = nc.alloc_sbuf_tensor("ccstg", [64, 1032], F32)
        p.op("vector", lambda e: e.tensor_copy(out=ccstg[:, 0:D],
                                               in_=slotin[:]),
             reads=[("slotin",)], writes=[("ccstg",)])
        p.op("vector", lambda e: e.tensor_copy(out=ccstg[:, D:D + 1],
                                               in_=pSM[0:64, 0:1]),
             reads=[("pSM",)], writes=[("ccstg",)])
        p.dma(qx, lambda e: e.dma_start(out=cc["cc_in"], in_=ccstg[:, 0:1025]),
              reads=[("ccstg",)], writes=[("cc_in",)])
        p.dma(cc["queue"], lambda e: e.collective_compute(
            "AllReduce", ALU.add, cc["groups"], [cc["cc_in"].opt()],
            [cc["cc_out"].opt()]),
            reads=[("cc_in",)], writes=[("cc_out",)], inc=1)
    else:
        p.op("vector", lambda e: e.tensor_scalar_add(denom[:, 0:1],
                                                     pSM[0:64, 0:1], 1e-8),
             reads=[("pSM",)], writes=[("denom",)], hazard=True)
        p.op("vector", lambda e: e.reciprocal(denom[:, 1:2], denom[:, 0:1]),
             reads=[("denom",)], writes=[("denom",)], hazard=True)
        p.op("vector", lambda e: e.tensor_scalar_mul(slotin[:], slotin[:],
                                                     denom[:, 1:2]),
             reads=[("slotin",), ("denom",)], writes=[("slotin",)],
             hazard=True)
    tapmap["slotin"] = (slotin, [("slotin",)])

    for g in (g_pSI1, g_pSI0, g_pAT, g_pL, g_pH1, g_pH0, g_pSM):
        g.__exit__(None, None, None)
    for g in reversed(mid1):
        g.__exit__(None, None, None)
    g_xn1_tm.__exit__(None, None, None)
    if g_xn1T_par is not None:
        g_xn1T_par.__exit__(None, None, None)
    p.barrier()

    if not own_only:
        g_x2acc = nc.sbuf_tensor("x2acc", [128, 16 * D], F32, side="right")
        x2acc = g_x2acc.__enter__()
    g_wqa = nc.sbuf_tensor("wqa", [128, 8 * D], BF16, side="right")
    wqa = g_wqa.__enter__()
    g_wka = nc.sbuf_tensor("wka", [128, 8 * D], BF16, side="right")
    wka = g_wka.__enter__()
    g_wva = nc.sbuf_tensor("wva", [128, 8 * D], BF16, side="right")
    wva = g_wva.__enter__()
    for wt, wn, wsrc in ((wqa, "wqa", Wq), (wka, "wka", Wk), (wva, "wva", Wv)):
        p.dma(qw, lambda e, wt=wt, wsrc=wsrc: e.dma_start(
            out=wt[:].rearrange("p (dc n) -> p dc n", dc=8),
            in_=wsrc.rearrange("(dc p) n -> p dc n", p=128)),
            writes=[(wn,)])
    att_guards = []

    def ata(nm, shape, dtype):
        g = nc.sbuf_tensor(nm, shape, dtype, side="right")
        t = g.__enter__()
        att_guards.append(g)
        return t

    q1T = ata("q1T", [128, 8 * 256], BF16)
    k1T = ata("k1T", [128, 8 * 256], BF16)
    v1t = ata("v1t", [128, 2 * 16 * 65], BF16)
    sq1 = ata("sq1", [128, 256], BF16)
    nrm1 = ata("nrm1", [1, 8], F32)
    negM1 = ata("negM1", [128, 1], F32)
    ptb1 = ata("ptb1", [128, 64], BF16)
    ssum1 = ata("ssum1", [128, 2], F32)
    g_pQ1, pQ1 = palloc("pQ1", [128, 512], F32)
    g_pK1, pK1 = palloc("pK1", [128, 512], F32)
    g_pV1a, pV1a = palloc("pV1a", [128, 512], F32)
    g_pV1b, pV1b = palloc("pV1b", [128, 512], F32)
    g_pN1, pN1 = palloc("pN1", [1, 256], F32)
    g_pM1, pM1 = palloc("pM1", [128, 1], F32)
    g_pSL, pSL = palloc("pSL", [128, 64], F32)
    g_pOL, pOL = palloc("pOL", [128, 65], F32)
    v1_r = v1t[:].rearrange("p (i hh c) -> p i hh c", i=2, hh=16)
    p.op("vector", lambda e: e.memset(v1_r[:, :, :, 64:65], 1.0),
         writes=[("v1o",)])

    for g4 in range(8):
        t0 = g4 * 256
        for tl in range(2):
            ti = g4 * 2 + tl
            p.dma(qx, lambda e, ti=ti: e.dma_start(
                out=x2acc[:, ti * D:(ti + 1) * D],
                in_=x1[ti * 128:(ti + 1) * 128, :]),
                reads=(x1_rk(ti) if x1_rk else []), writes=[("x2d", ti)])
            p.op("vector", lambda e, ti=ti: e.tensor_add(
                x2acc[:, ti * D:(ti + 1) * D], x2acc[:, ti * D:(ti + 1) * D],
                bob[:]),
                reads=[("x2d", ti), ("bob",)], writes=[("x2", ti)])
        for w_, wk_, ps_, psk_, dst, dk, bias_s, bk_ in (
                (wqa, "wqa", pQ1, ("pQ1",), q1T, "q1T", bq_s, "bq_s"),
                (wka, "wka", pK1, ("pK1",), k1T, "k1T", bk_s, "bk_s")):
            for oc in range(8):
                for dc in range(8):
                    rhs, rk = xn1_rhs(dc, t0, 256)
                    p.op("tensor", lambda e, oc=oc, dc=dc, rhs=rhs, w_=w_,
                         wk_=wk_, ps_=ps_: e.matmul(
                        ps_[:, 0:256],
                        w_[:, dc * D + oc * 128:dc * D + oc * 128 + 128],
                        rhs, start=(dc == 0), stop=(dc == 7)),
                        reads=[(wk_,), rk], writes=[psk_])
                p.op("scalar", lambda e, oc=oc, ps_=ps_, dst=dst, dk=dk,
                     bias_s=bias_s, bk_=bk_: e.activation(
                    dst[:, oc * 256:(oc + 1) * 256], ps_[:, 0:256], AF.Identity,
                    bias=bias_s[:, oc:oc + 1]),
                    reads=[psk_, (bk_,)], writes=[(dk, oc)])
        for tl in range(2):
            ti = g4 * 2 + tl
            for half in range(2):
                pv, pvk = (pV1a, ("pV1a",)) if half == 0 else (pV1b, ("pV1b",))
                for dc in range(8):
                    lhs, rk = xn1_rhs(dc, ti * 128, 128)
                    p.op("tensor", lambda e, half=half, dc=dc, lhs=lhs, pv=pv:
                         e.matmul(
                        pv[:], lhs,
                        wva[:, dc * D + half * 512:dc * D + half * 512 + 512],
                        start=(dc == 0), stop=(dc == 7)),
                        reads=[("wva",), rk], writes=[pvk])
                p.op("vector", lambda e, tl=tl, half=half, pv=pv: e.tensor_add(
                    v1_r[:, tl, half * 8:(half + 1) * 8, 0:64],
                    pv[:].rearrange("p (hh c) -> p hh c", hh=8),
                    bvb[:, half * 512:(half + 1) * 512].rearrange(
                        "p (hh c) -> p hh c", hh=8)),
                    reads=[pvk, ("bvb",)], writes=[("v1", tl, half)])
        qk1 = [("q1T", oc) for oc in range(8)]
        kk1 = [("k1T", oc) for oc in range(8)]
        for w_, wk_, col in ((q1T, qk1, 0), (k1T, kk1, 1)):
            for oc in range(8):
                p.op("vector", lambda e, oc=oc, w_=w_, wk_=wk_: e.tensor_mul(
                    sq1[:], w_[:, oc * 256:(oc + 1) * 256],
                    w_[:, oc * 256:(oc + 1) * 256]),
                    reads=wk_, writes=[("sq1",)])
                p.op("tensor", lambda e, oc=oc: e.matmul(
                    pN1[:], cm.onesb[:, 0:1], sq1[:],
                    start=(oc == 0), stop=(oc == 7)),
                    reads=[("sq1",), ("c_onesb",)], writes=[("pN1",)])
            p.op("vector", lambda e, col=col: e.reduce_max(
                nrm1[:, col:col + 1], pN1[:], axis=X),
                reads=[("pN1",)], writes=[("nrm1",)], hazard=True)
        p.op("vector", lambda e: e.tensor_mul(nrm1[:, 2:3], nrm1[:, 0:1],
                                              nrm1[:, 1:2]),
             reads=[("nrm1",)], writes=[("nrm1",)], hazard=True)
        p.op("scalar", lambda e: e.activation(nrm1[:, 3:4], nrm1[:, 2:3],
                                              AF.Sqrt),
             reads=[("nrm1",)], writes=[("nrm1",)], hazard=True)
        p.op("vector", lambda e: e.tensor_scalar_mul(nrm1[:, 3:4],
                                                     nrm1[:, 3:4], -SCALE),
             reads=[("nrm1",)], writes=[("nrm1",)], hazard=True)
        p.op("tensor", lambda e: e.matmul(
            pM1[:], cm.onesf[0:1, 0:128], nrm1[:, 3:4], start=True, stop=True),
            reads=[("nrm1",), ("c_onesf",)], writes=[("pM1",)])
        p.op("vector", lambda e: e.tensor_copy(out=negM1[:], in_=pM1[:]),
             reads=[("pM1",)], writes=[("negM1",)])
        for tl in range(2):
            ti = g4 * 2 + tl
            for hh in range(16):
                oc = hh // 2
                rb = 64 * (hh % 2)
                c0 = oc * 256 + tl * 128
                for blk in range(2):
                    p.op("tensor", lambda e, oc=oc, rb=rb, blk=blk, c0=c0:
                         e.matmul(
                        pSL[blk * 64:(blk + 1) * 64, :],
                        k1T[rb:rb + 64, c0 + blk * 64:c0 + blk * 64 + 64],
                        q1T[rb:rb + 64, c0 + blk * 64:c0 + blk * 64 + 64],
                        start=True, stop=True, tile_position=(rb, blk * 64)),
                        reads=[("q1T", oc), ("k1T", oc)], writes=[("pSL",)])
                p.op("scalar", lambda e: e.activation(
                    ptb1[:], pSL[:], AF.Exp, bias=negM1[:, 0:1], scale=SCALE),
                    reads=[("pSL",), ("negM1",)], writes=[("ptb1",)])
                for blk in range(2):
                    p.op("tensor", lambda e, tl=tl, hh=hh, blk=blk: e.matmul(
                        pOL[blk * 64:(blk + 1) * 64, :],
                        ptb1[blk * 64:(blk + 1) * 64, :],
                        v1t[blk * 64:(blk + 1) * 64,
                            tl * 1040 + hh * 65:tl * 1040 + hh * 65 + 65],
                        start=True, stop=True, tile_position=(blk * 64, blk * 64)),
                        reads=[("ptb1",), ("v1", tl, hh // 8), ("v1o",)],
                        writes=[("pOL",)])
                p.op("scalar", lambda e: e.copy(out=ssum1[:, 0:1],
                                                in_=pOL[:, 64:65]),
                     reads=[("pOL",)], writes=[("ssum1",)], hazard=True)
                p.op("vector", lambda e: e.reciprocal(ssum1[:, 1:2],
                                                      ssum1[:, 0:1]),
                     reads=[("ssum1",)], writes=[("ssum1",)], hazard=True)
                p.op("vector", lambda e, ti=ti, hh=hh: e.scalar_tensor_tensor(
                    x2acc[:, ti * D + hh * 64:ti * D + hh * 64 + 64],
                    pOL[:, 0:64], ssum1[:, 1:2],
                    x2acc[:, ti * D + hh * 64:ti * D + hh * 64 + 64],
                    ALU.mult, ALU.add),
                    reads=[("pOL",), ("ssum1",), ("x2", ti)],
                    writes=[("x2", ti)], hazard=True)
    tapmap["x2acc"] = (x2acc, [("x2", i) for i in range(16)])

    for g in (g_pOL, g_pSL, g_pM1, g_pN1, g_pV1b, g_pV1a, g_pK1, g_pQ1):
        g.__exit__(None, None, None)
    for g in reversed(att_guards):
        g.__exit__(None, None, None)
    g_wva.__exit__(None, None, None)
    g_wka.__exit__(None, None, None)
    g_wqa.__exit__(None, None, None)
    if own_only:
        g_xn1T_own.__exit__(None, None, None)
    p.barrier()

    if own_only:
        p.dma(qx, lambda e: e.dma_start(out=ccstg[:, 0:1025], in_=cc["cc_out"]),
              reads=[("cc_out",)], writes=[("ccstg",)])
        p.op("vector", lambda e: e.tensor_copy(out=slotin[:],
                                               in_=ccstg[:, 0:D]),
             reads=[("ccstg",)], writes=[("slotin",)])
        p.op("vector", lambda e: e.tensor_scalar_add(
            denom[:, 0:1], ccstg[:, D:D + 1], 1e-8),
            reads=[("ccstg",)], writes=[("denom",)], hazard=True)
        p.op("vector", lambda e: e.reciprocal(denom[:, 1:2], denom[:, 0:1]),
             reads=[("denom",)], writes=[("denom",)], hazard=True)
        p.op("vector", lambda e: e.tensor_scalar_mul(slotin[:], slotin[:],
                                                     denom[:, 1:2]),
             reads=[("slotin",), ("denom",)], writes=[("slotin",)], hazard=True)
    mid2 = []

    def m2a(nm, shape, dtype):
        g = nc.sbuf_tensor(nm, shape, dtype, side="right")
        t = g.__enter__()
        mid2.append(g)
        return t

    siT = m2a("siT", [128, 8 * 64], BF16)
    s0T = m2a("s0T", [128, 8 * 64], BF16)
    sib = m2a("sib", [64, D], BF16)
    giT = m2a("giT", [128, 24 * 64], F32)
    ghT = m2a("ghT", [128, 24 * 64], F32)
    g_pTR, pTR = palloc("pTR", [128, 128], BF16)
    p.op("vector", lambda e: e.tensor_copy(out=sib[:], in_=slotin[:]),
         reads=[("slotin",)], writes=[("sib",)])
    for c in range(8):
        p.op("tensor", lambda e, c=c: e.transpose(
            pTR[:, 0:64], sib[0:64, c * 128:(c + 1) * 128],
            cm.identb[0:64, 0:64]),
            reads=[("sib",), ("c_identb",)], writes=[("pTR",)])
        p.op("vector", lambda e, c=c: e.tensor_copy(
            out=siT[:, c * 64:(c + 1) * 64], in_=pTR[:, 0:64]),
            reads=[("pTR",)], writes=[("siT",)])
        p.op("tensor", lambda e, c=c: e.transpose(
            pTR[:, 0:64], s0b[0:64, c * 128:(c + 1) * 128],
            cm.identb[0:64, 0:64]),
            reads=[("s0b",), ("c_identb",)], writes=[("pTR",)])
        p.op("vector", lambda e, c=c: e.tensor_copy(
            out=s0T[:, c * 64:(c + 1) * 64], in_=pTR[:, 0:64]),
            reads=[("pTR",)], writes=[("s0T",)])

    wblk = [m2a(f"wblk{j}", [128, D], BF16) for j in range(2)]
    wtT = [m2a(f"wtT{j}", [128, 128], BF16) for j in range(2)]
    g_pWT, pWT = palloc("pWT", [128, 128], BF16)
    g_pGI, pGI = palloc("pGI", [128, 64], F32)
    for wsrc, dstT, bias_s, bk_, dk in ((Wih, giT, bih_s, "bih_s", "giT"),
                                        (Whh, ghT, bhh_s, "bhh_s", "ghT")):
        rhsT = siT if dk == "giT" else s0T
        rhk = ("siT",) if dk == "giT" else ("s0T",)
        for jc in range(24):
            jb = jc % 2
            p.dma(qw, lambda e, jc=jc, jb=jb, wsrc=wsrc: e.dma_start(
                out=wblk[jb][:], in_=wsrc[jc * 128:(jc + 1) * 128, :]),
                writes=[("wblk", jb)], tag=f"{dk}_blk{jc}")
            for dc in range(8):
                db = dc % 2
                p.op("tensor", lambda e, jb=jb, dc=dc: e.transpose(
                    pWT[:], wblk[jb][:, dc * 128:(dc + 1) * 128], cm.identb[:]),
                    reads=[("wblk", jb), ("c_identb",)], writes=[("pWT",)])
                p.op("vector", lambda e, db=db: e.tensor_copy(
                    out=wtT[db][:], in_=pWT[:]),
                    reads=[("pWT",)], writes=[("wtTx", db)])
                p.op("tensor", lambda e, dc=dc, db=db, rhsT=rhsT, rhk=rhk:
                     e.matmul(
                    pGI[:], wtT[db][:], rhsT[:, dc * 64:(dc + 1) * 64],
                    start=(dc == 0), stop=(dc == 7)),
                    reads=[("wtTx", db), rhk], writes=[("pGI",)])
            p.op("vector", lambda e, jc=jc, dstT=dstT, bias_s=bias_s, bk_=bk_,
                 dk=dk: e.tensor_scalar_add(
                dstT[:, jc * 64:(jc + 1) * 64], pGI[:], bias_s[:, jc:jc + 1]),
                reads=[("pGI",), (bk_,)], writes=[(dk, jc)])

    rz = m2a("rz", [128, 16 * 64], F32)
    snT = m2a("snT", [128, 8 * 64], F32)
    snTb = m2a("snTb", [128, 8 * 64], BF16)
    tmp1 = m2a("tmp1", [128, 64], F32)
    for c in range(16):
        p.op("vector", lambda e, c=c: e.tensor_add(
            tmp1[:], giT[:, c * 64:(c + 1) * 64], ghT[:, c * 64:(c + 1) * 64]),
            reads=[("giT", c), ("ghT", c)], writes=[("tmp1",)], hazard=True)
        p.op("scalar", lambda e, c=c: e.activation(
            rz[:, c * 64:(c + 1) * 64], tmp1[:], AF.Sigmoid),
            reads=[("tmp1",)], writes=[("rz", c)])
    for c in range(8):
        p.op("vector", lambda e, c=c: e.tensor_mul(
            tmp1[:], rz[:, c * 64:(c + 1) * 64],
            ghT[:, (16 + c) * 64:(17 + c) * 64]),
            reads=[("rz", c), ("ghT", 16 + c)], writes=[("tmp1",)], hazard=True)
        p.op("vector", lambda e, c=c: e.tensor_add(
            tmp1[:], tmp1[:], giT[:, (16 + c) * 64:(17 + c) * 64]),
            reads=[("tmp1",), ("giT", 16 + c)], writes=[("tmp1",)], hazard=True)
        p.op("scalar", lambda e, c=c: e.activation(
            snT[:, c * 64:(c + 1) * 64], tmp1[:], AF.Tanh),
            reads=[("tmp1",)], writes=[("snT", c)], hazard=True)
        p.op("vector", lambda e, c=c: e.tensor_sub(
            tmp1[:], s0T[:, c * 64:(c + 1) * 64], snT[:, c * 64:(c + 1) * 64]),
            reads=[("s0T",), ("snT", c)], writes=[("tmp1",)], hazard=True)
        p.op("vector", lambda e, c=c: e.tensor_mul(
            tmp1[:], tmp1[:], rz[:, (8 + c) * 64:(9 + c) * 64]),
            reads=[("tmp1",), ("rz", 8 + c)], writes=[("tmp1",)], hazard=True)
        p.op("vector", lambda e, c=c: e.tensor_add(
            snT[:, c * 64:(c + 1) * 64], snT[:, c * 64:(c + 1) * 64], tmp1[:]),
            reads=[("tmp1",), ("snT", c)], writes=[("snT", c)], hazard=True)
        p.op("vector", lambda e, c=c: e.tensor_copy(
            out=snTb[:, c * 64:(c + 1) * 64], in_=snT[:, c * 64:(c + 1) * 64]),
            reads=[("snT", c)], writes=[("snTb", c)])
    tapmap["snT"] = (snT, [("snT", c) for c in range(8)])

    qsT = m2a("qsT", [128, 2 * 64], BF16)
    ksT = m2a("ksT", [128, 2 * 64], BF16)
    vsT = m2a("vsT", [128, 8 * 64], BF16)
    wqk = m2a("wqk", [128, 8 * 256], BF16)
    wvv = m2a("wvv", [128, 8 * 1024], BF16)
    snk = [("snTb", c) for c in range(8)]
    g_pQS, pQS = palloc("pQS", [128, 64], F32)
    for wsrc, dst, dstk, bias_s, bkk, ncol in ((wq1, qsT, "qsT", bq1_s, "bq1_s", 2),
                                               (wk1, ksT, "ksT", bk1_s, "bk1_s", 2)):
        p.dma(qw, lambda e, wsrc=wsrc: e.dma_start(
            out=wqk[:].rearrange("p (dc n) -> p dc n", dc=8),
            in_=wsrc.rearrange("(dc p) n -> p dc n", p=128)),
            reads=[("wqk",)], writes=[("wqk",)])
        for qc in range(ncol):
            for dc in range(8):
                p.op("tensor", lambda e, qc=qc, dc=dc: e.matmul(
                    pQS[:], wqk[:, dc * 256 + qc * 128:dc * 256 + qc * 128 + 128],
                    snTb[:, dc * 64:(dc + 1) * 64],
                    start=(dc == 0), stop=(dc == 7)),
                    reads=[("wqk",)] + snk, writes=[("pQS",)])
            p.op("vector", lambda e, qc=qc, dst=dst, dstk=dstk, bias_s=bias_s,
                 bkk=bkk: e.tensor_scalar_add(
                dst[:, qc * 64:(qc + 1) * 64], pQS[:], bias_s[:, qc:qc + 1]),
                reads=[("pQS",), (bkk,)], writes=[(dstk,)])
    p.dma(qw, lambda e: e.dma_start(
        out=wvv[:].rearrange("p (dc n) -> p dc n", dc=8),
        in_=wv1.rearrange("(dc p) n -> p dc n", p=128)),
        reads=[("wvv",)], writes=[("wvv",)])
    for vc in range(8):
        for dc in range(8):
            p.op("tensor", lambda e, vc=vc, dc=dc: e.matmul(
                pQS[:], wvv[:, dc * 1024 + vc * 128:dc * 1024 + vc * 128 + 128],
                snTb[:, dc * 64:(dc + 1) * 64], start=(dc == 0), stop=(dc == 7)),
                reads=[("wvv",)] + snk, writes=[("pQS",)])
        p.op("vector", lambda e, vc=vc: e.tensor_scalar_add(
            vsT[:, vc * 64:(vc + 1) * 64], pQS[:], bv1_s[:, vc:vc + 1]),
            reads=[("pQS",), ("bv1_s",)], writes=[("vsT",)])

    satp = m2a("satp", [64, 64], BF16)
    satt = m2a("satt", [64, 64], BF16)
    sats = m2a("sats", [64, 4], F32)
    g_pSA, pSA = palloc("pSA", [64, 64], F32)
    for qc in range(2):
        p.op("tensor", lambda e, qc=qc: e.matmul(
            pSA[:], qsT[:, qc * 64:(qc + 1) * 64], ksT[:, qc * 64:(qc + 1) * 64],
            start=(qc == 0), stop=(qc == 1)),
            reads=[("qsT",), ("ksT",)], writes=[("pSA",)])
    p.op("vector", lambda e: e.reduce_max(sats[:, 0:1], pSA[:], axis=X,
                                          negate=True),
         reads=[("pSA",)], writes=[("sats",)], hazard=True)
    p.op("vector", lambda e: e.tensor_scalar_mul(sats[:, 1:2], sats[:, 0:1],
                                                 1.0 / 16.0),
         reads=[("sats",)], writes=[("sats",)], hazard=True)
    p.op("scalar", lambda e: e.activation(satp[:], pSA[:], AF.Exp,
                                          bias=sats[:, 1:2], scale=1.0 / 16.0),
         reads=[("pSA",), ("sats",)], writes=[("satp",)], hazard=True)
    p.op("vector", lambda e: e.reduce_sum(sats[:, 2:3], satp[:], axis=X),
         reads=[("satp",)], writes=[("sats",)], hazard=True)
    p.op("vector", lambda e: e.reciprocal(sats[:, 3:4], sats[:, 2:3]),
         reads=[("sats",)], writes=[("sats",)], hazard=True)
    p.op("vector", lambda e: e.tensor_scalar_mul(satp[:], satp[:],
                                                 sats[:, 3:4]),
         reads=[("satp",), ("sats",)], writes=[("satp",)], hazard=True)
    p.op("tensor", lambda e: e.transpose(pTR[0:64, 0:64], satp[:],
                                         cm.identb[0:64, 0:64]),
         reads=[("satp",), ("c_identb",)], writes=[("pTR",)])
    p.op("vector", lambda e: e.tensor_copy(out=satt[:], in_=pTR[0:64, 0:64]),
         reads=[("pTR",)], writes=[("satt",)])
    vssm = m2a("vssm", [64, 8 * 128], BF16)
    for c in range(8):
        p.op("tensor", lambda e, c=c: e.transpose(
            pTR[0:64, :], vsT[:, c * 64:(c + 1) * 64], cm.identb[:]),
            reads=[("vsT",), ("c_identb",)], writes=[("pTR",)])
        p.op("vector", lambda e, c=c: e.tensor_copy(
            out=vssm[:, c * 128:(c + 1) * 128], in_=pTR[0:64, 0:128]),
            reads=[("pTR",)], writes=[("vssm",)])

    yT = m2a("yT", [128, 8 * 64], F32)
    yTb = m2a("yTb", [128, 8 * 64], BF16)
    ysq = m2a("ysq", [128, 8 * 64], BF16)
    g_pSC, pSC = palloc("pSC", [128, 64], F32)
    for c in range(8):
        p.op("tensor", lambda e, c=c: e.matmul(
            pSC[:], vssm[:, c * 128:(c + 1) * 128], satt[:],
            start=True, stop=True),
            reads=[("vssm",), ("satt",)], writes=[("pSC",)])
        p.op("vector", lambda e, c=c: e.tensor_add(
            yT[:, c * 64:(c + 1) * 64], snT[:, c * 64:(c + 1) * 64], pSC[:]),
            reads=[("pSC",), ("snT", c)], writes=[("yT", c)])
        p.op("vector", lambda e, c=c: e.tensor_copy(
            out=yTb[:, c * 64:(c + 1) * 64], in_=yT[:, c * 64:(c + 1) * 64]),
            reads=[("yT", c)], writes=[("yTb", c)])
        p.op("scalar", lambda e, c=c: e.activation(
            ysq[:, c * 64:(c + 1) * 64], yT[:, c * 64:(c + 1) * 64], AF.Square),
            reads=[("yT", c)], writes=[("ysq", c)])
    lnstat = m2a("lnstat", [1, 8 * 64], F32)
    g_pLN, pLN = palloc("pLN", [1, 64], F32)
    ykeys = [("yTb", c) for c in range(8)]
    sqkeys = [("ysq", c) for c in range(8)]
    for which, srck, col0 in ((yTb, ykeys, 0), (ysq, sqkeys, 64)):
        for c in range(8):
            p.op("tensor", lambda e, c=c, which=which: e.matmul(
                pLN[:], cm.onesb[:, 0:1], which[:, c * 64:(c + 1) * 64],
                start=(c == 0), stop=(c == 7)),
                reads=srck + [("c_onesb",)], writes=[("pLN",)])
        p.op("vector", lambda e, col0=col0: e.tensor_copy(
            out=lnstat[:, col0:col0 + 64], in_=pLN[:]),
            reads=[("pLN",)], writes=[("lnstat",)], hazard=True)
    p.op("vector", lambda e: e.tensor_scalar_mul(
        lnstat[:, 128:192], lnstat[:, 0:64], 1.0 / D),
        reads=[("lnstat",)], writes=[("lnstat",)], hazard=True)
    p.op("vector", lambda e: e.scalar_tensor_tensor(
        lnstat[:, 192:256], lnstat[:, 128:192], 1.0, lnstat[:, 128:192],
        ALU.mult, ALU.mult),
        reads=[("lnstat",)], writes=[("lnstat",)], hazard=True)
    p.op("vector", lambda e: e.scalar_tensor_tensor(
        lnstat[:, 256:320], lnstat[:, 64:128], 1.0 / D, lnstat[:, 192:256],
        ALU.mult, ALU.subtract),
        reads=[("lnstat",)], writes=[("lnstat",)], hazard=True)
    p.op("vector", lambda e: e.tensor_scalar_add(
        lnstat[:, 256:320], lnstat[:, 256:320], 1e-5),
        reads=[("lnstat",)], writes=[("lnstat",)], hazard=True)
    p.op("scalar", lambda e: e.activation(
        lnstat[:, 320:384], lnstat[:, 256:320], AF.Sqrt),
        reads=[("lnstat",)], writes=[("lnstat",)], hazard=True)
    p.op("vector", lambda e: e.reciprocal(
        lnstat[:, 320:384], lnstat[:, 320:384]),
        reads=[("lnstat",)], writes=[("lnstat",)], hazard=True)
    p.op("vector", lambda e: e.scalar_tensor_tensor(
        lnstat[:, 384:448], lnstat[:, 128:192], -1.0, lnstat[:, 320:384],
        ALU.mult, ALU.mult),
        reads=[("lnstat",)], writes=[("lnstat",)], hazard=True)
    abb = m2a("abb", [128, 128], F32)
    g_pAB, pAB = palloc("pAB", [128, 128], F32)
    p.op("tensor", lambda e: e.matmul(
        pAB[:, 0:64], cm.onesf[0:1, 0:128], lnstat[:, 320:384],
        start=True, stop=True),
        reads=[("lnstat",), ("c_onesf",)], writes=[("pAB",)])
    p.op("tensor", lambda e: e.matmul(
        pAB[:, 64:128], cm.onesf[0:1, 0:128], lnstat[:, 384:448],
        start=True, stop=True),
        reads=[("lnstat",), ("c_onesf",)], writes=[("pAB",)])
    p.op("vector", lambda e: e.tensor_copy(out=abb[:], in_=pAB[:]),
         reads=[("pAB",)], writes=[("abb",)])
    ynTb = m2a("ynTb", [128, 8 * 64], BF16)
    for c in range(8):
        p.op("vector", lambda e, c=c: e.tensor_mul(
            tmp1[:], yT[:, c * 64:(c + 1) * 64], abb[:, 0:64]),
            reads=[("yT", c), ("abb",)], writes=[("tmp1",)], hazard=True)
        p.op("vector", lambda e, c=c: e.tensor_add(tmp1[:], tmp1[:],
                                                   abb[:, 64:128]),
             reads=[("tmp1",), ("abb",)], writes=[("tmp1",)], hazard=True)
        p.op("scalar", lambda e, c=c: e.activation(
            ynTb[:, c * 64:(c + 1) * 64], tmp1[:], AF.Identity,
            bias=lnb_s[:, c:c + 1], scale=lng_s[:, c:c + 1]),
            reads=[("tmp1",), ("lnb_s",), ("lng_s",)], writes=[("ynTb", c)])
    svT = m2a("svT", [128, 8 * 64], BF16)
    ynk = [("ynTb", c) for c in range(8)]
    p.dma(qw, lambda e: e.dma_start(
        out=wvv[:].rearrange("p (dc n) -> p dc n", dc=8),
        in_=wvp.rearrange("(dc p) n -> p dc n", p=128)),
        reads=[("wvv",)], writes=[("wvv",)])
    for vc in range(8):
        for dc in range(8):
            p.op("tensor", lambda e, vc=vc, dc=dc: e.matmul(
                pQS[:], wvv[:, dc * 1024 + vc * 128:dc * 1024 + vc * 128 + 128],
                ynTb[:, dc * 64:(dc + 1) * 64], start=(dc == 0), stop=(dc == 7)),
                reads=[("wvv",)] + ynk, writes=[("pQS",)])
        p.op("vector", lambda e, vc=vc: e.tensor_scalar_add(
            svT[:, vc * 64:(vc + 1) * 64], pQS[:], bvp_s[:, vc:vc + 1]),
            reads=[("pQS",), ("bvp_s",)], writes=[("svT",)])
    svsm = nc.alloc_sbuf_tensor("svsm", [64, 8 * 128], BF16)
    for c in range(8):
        p.op("tensor", lambda e, c=c: e.transpose(
            pTR[0:64, :], svT[:, c * 64:(c + 1) * 64], cm.identb[:]),
            reads=[("svT",), ("c_identb",)], writes=[("pTR",)])
        p.op("vector", lambda e, c=c: e.tensor_copy(
            out=svsm[:, c * 128:(c + 1) * 128], in_=pTR[0:64, 0:128]),
            reads=[("pTR",)], writes=[("svsm",)])
    tapmap["svsm"] = (svsm, [("svsm",)])

    for g in (g_pAB, g_pLN, g_pSC, g_pSA, g_pQS, g_pGI, g_pWT, g_pTR):
        g.__exit__(None, None, None)
    for g in reversed(mid2):
        g.__exit__(None, None, None)
    p.barrier()

    g_outtokT = nc.sbuf_tensor("outtokT", [128, 8 * HALF], BF16, side="right")
    outtokT = g_outtokT.__enter__()
    g_woa = nc.sbuf_tensor("woa", [128, 8 * D], BF16, side="right")
    woa = g_woa.__enter__()
    p.dma(qw, lambda e: e.dma_start(
        out=woa[:].rearrange("p (dc n) -> p dc n", dc=8),
        in_=wo.rearrange("(dc p) n -> p dc n", p=128)),
        writes=[("woa",)])
    g_pOT0, pOT0 = palloc("pOT0", [128, 512], F32)
    g_pOT1, pOT1 = palloc("pOT1", [128, 512], F32)
    for wnd in range(4):
        for dc in range(8):
            ps, psk = (pOT0, ("pOT0",)) if dc % 2 == 0 else (pOT1, ("pOT1",))
            p.op("tensor", lambda e, wnd=wnd, dc=dc, ps=ps: e.matmul(
                ps[:], svsm[:, dc * 128:(dc + 1) * 128],
                alphaT[0:64, wnd * 512:(wnd + 1) * 512],
                start=True, stop=True),
                reads=[("svsm",)] + [("aT1", t) for t in range(wnd * 4, wnd * 4 + 4)],
                writes=[psk])
            p.op("vector", lambda e, wnd=wnd, dc=dc, ps=ps: e.tensor_copy(
                out=outtokT[:, dc * HALF + wnd * 512:dc * HALF + wnd * 512 + 512],
                in_=ps[:]),
                reads=[psk], writes=[("otT", dc, wnd)])
    for ti in range(16):
        for half in range(2):
            ps, psk = (pOT0, ("pOT0",)) if half == 0 else (pOT1, ("pOT1",))
            for dc in range(8):
                p.op("tensor", lambda e, ti=ti, half=half, dc=dc, ps=ps:
                     e.matmul(
                    ps[:, 0:128] if False else ps[:],
                    outtokT[:, dc * HALF + ti * 128:dc * HALF + ti * 128 + 128],
                    woa[:, dc * D + half * 512:dc * D + half * 512 + 512],
                    start=(dc == 0), stop=(dc == 7)),
                    reads=[("otT", dc, ti // 4), ("woa",)], writes=[psk])
            p.op("vector", lambda e, ti=ti, half=half, ps=ps: e.tensor_add(
                x2acc[:, ti * D + half * 512:ti * D + half * 512 + 512],
                x2acc[:, ti * D + half * 512:ti * D + half * 512 + 512],
                ps[:]),
                reads=[psk, ("x2", ti)], writes=[("x2", ti)], hazard=True)
        p.dma(qx, lambda e, ti=ti: e.dma_start(
            out=y[ti * 128:(ti + 1) * 128, :], in_=x2acc[:, ti * D:(ti + 1) * D]),
            reads=[("x2", ti)], writes=[("yout", ti)])

    g_pOT1.__exit__(None, None, None)
    g_pOT0.__exit__(None, None, None)
    g_woa.__exit__(None, None, None)
    g_outtokT.__exit__(None, None, None)
    g_x2acc.__exit__(None, None, None)
    return


def build_l1(taps=()):
    nc = bass.Bass("TRN2", target_bir_lowering=False)
    x1 = nc.dram_tensor("x1", [SEQ, D], F32, kind="ExternalInput").ap()
    ins = l1_dram_ins(nc)
    y = nc.dram_tensor("y", [HALF, D], F32, kind="ExternalOutput").ap()
    p = _mk_prog(nc)
    tapmap = {}
    cm = Common(nc, p)
    emit_l1(nc, p, cm, tapmap, ins, x1, y, own_only=False)
    return _do_finish(nc, p, tapmap, taps)


def build_fused(taps=()):
    nc = bass.Bass("TRN2", target_bir_lowering=False)
    ins0 = l0_dram_ins(nc)
    ins1 = l1_dram_ins(nc)
    x1s = nc.dram_tensor("x1s", [HALF, D], F32).ap()
    cc_in = nc.dram_tensor("cc_in", [64, 1025], F32).ap()
    cc_out = nc.dram_tensor("cc_out", [64, 1025], F32).ap()
    y = nc.dram_tensor("y", [HALF, D], F32, kind="ExternalOutput").ap()
    p = _mk_prog(nc)
    p.dma_queue("qc", "gpsimd")
    tapmap = {}
    cm = Common(nc, p)
    emit_l0(nc, p, cm, tapmap, ins0, x1s)
    nc1 = NPfx(nc, "z_")
    p1 = KeyPrefix(p, "z_", exempt=("y1out", "c_identf", "c_identb",
                                    "c_onesf", "c_onesb", "c_hmask"))
    cc = dict(cc_in=cc_in, cc_out=cc_out, queue="qc",
              groups=[[0, 1], [2, 3], [4, 5], [6, 7]])
    emit_l1(nc1, p1, cm, tapmap, ins1, x1s, y, own_only=True,
            x1_rk=lambda i: [("y1out", i)], cc=cc)
    return _do_finish(nc, p, tapmap, taps)



_CACHE = {}


def _rot(xseq, h):
    if h == 0:
        return np.ascontiguousarray(xseq)
    return np.ascontiguousarray(
        np.concatenate([xseq[HALF:], xseq[:HALF]], axis=0))


def kernel(**inputs):
    if "fused" not in _CACHE:
        _CACHE["fused"] = build_fused()[0]
    nc = _CACHE["fused"]
    x = np.asarray(inputs["L0_attn_x"], dtype=np.float32)
    w = dict(
        w_qkv=inputs["L0_attn_w_qkv"], w_proj=inputs["L0_attn_w_proj"],
        b_proj=inputs["L0_attn_b_proj"], gate_w=inputs["L0_ffn_gate_w"],
        w_gate=inputs["L0_ffn_w_gate"], w_up=inputs["L0_ffn_w_up"],
        w_down=inputs["L0_ffn_w_down"], sw_gate=inputs["L0_ffn_sw_gate"],
        sw_up=inputs["L0_ffn_sw_up"], sw_down=inputs["L0_ffn_sw_down"],
        Wq=inputs["L1_attn_Wq"], bq=inputs["L1_attn_bq"],
        Wk=inputs["L1_attn_Wk"], bk=inputs["L1_attn_bk"],
        Wv=inputs["L1_attn_Wv"], bv=inputs["L1_attn_bv"],
        slot_init=inputs["L1_ffn_slot_init"], w1=inputs["L1_ffn_w1"],
        b1=inputs["L1_ffn_b1"], w2=inputs["L1_ffn_w2"], b2=inputs["L1_ffn_b2"],
        tau=inputs["L1_ffn_tau"], Wih=inputs["L1_ffn_Wih"],
        Whh=inputs["L1_ffn_Whh"], bih=inputs["L1_ffn_bih"],
        bhh=inputs["L1_ffn_bhh"], wq1=inputs["L1_ffn_wq"],
        bq1=inputs["L1_ffn_bq"], wk1=inputs["L1_ffn_wk"],
        bk1=inputs["L1_ffn_bk"], wv1=inputs["L1_ffn_wv"],
        bv1=inputs["L1_ffn_bv"], ln_g=inputs["L1_ffn_ln_g"],
        ln_b=inputs["L1_ffn_ln_b"], wvp=inputs["L1_ffn_wvp"],
        bvp=inputs["L1_ffn_bvp"], wo=inputs["L1_ffn_wo"],
        bo=inputs["L1_ffn_bo"])
    w = {k: np.ascontiguousarray(np.asarray(v, dtype=np.float32))
         for k, v in w.items()}
    in_maps = []
    for core in range(8):
        b, h = core // 2, core % 2
        im = {"x": _rot(x[b], h)}
        im.update(w)
        in_maps.append(im)
    res = run_bass_kernel_spmd(nc, in_maps, core_ids=list(range(8)))
    y = np.empty((4, SEQ, D), dtype=np.float32)
    for core in range(8):
        b, h = core // 2, core % 2
        y[b, h * HALF:(h + 1) * HALF] = res.results[core]["y"]
    return y


# version 11
# speedup vs baseline: 1.0055x; 1.0055x over previous
import numpy as np
import concourse.bass as bass
import concourse.mybir as mybir
from concourse.bass_utils import run_bass_kernel_spmd

F32 = mybir.dt.float32
BF16 = mybir.dt.bfloat16
AF = mybir.ActivationFunctionType
ALU = mybir.AluOpType
X = mybir.AxisListType.X

D = 1024
SEQ = 4096
HALF = 2048
NH = 16
HD = 64
SCALE = HD ** -0.5


ENGINE_NAMES = ("tensor", "vector", "scalar", "gpsimd", "sync")
CHAIN_ALL_QUEUES = True


class _Op:
    __slots__ = ("idx", "engine", "emit", "reads", "writes", "queue", "tag",
                 "deps", "need_inc", "inc_val", "waits", "hazard", "inc_amt")

    def __init__(self, idx, engine, emit, reads, writes, queue, tag, hazard=False):
        self.idx = idx
        self.engine = engine
        self.emit = emit
        self.reads = reads
        self.writes = writes
        self.queue = queue
        self.tag = tag
        self.hazard = hazard
        self.deps = []
        self.need_inc = False
        self.inc_val = None
        self.waits = []


class _Queue:
    __slots__ = ("name", "engine", "count")

    def __init__(self, name, engine):
        self.name = name
        self.engine = engine
        self.count = 0


class Prog:
    def __init__(self, nc):
        self.nc = nc
        self.ops = []
        self.queues = {}
        self._last_write = {}
        self._readers = {}
        self._barrier = []
        self._last_dma = {}
        self.psum_prev = {}
        self.pg_scr = None
        self.qsplit = {}
        self._rr = {}
        self.chain = set()

    def palloc(self, nc, nm, shape, dtype, key=None):
        b0 = nc.psum_base
        g = nc.psum_tensor(nm, shape, dtype)
        t = g.__enter__()
        b1 = nc.psum_base
        key = key if key is not None else (nm,)
        prev = set()
        for b in range(b0, b1):
            pk = self.psum_prev.get(b)
            if pk is not None and pk != key:
                prev.add(pk)
            self.psum_prev[b] = key
        if prev and self.pg_scr is not None:
            scr = self.pg_scr
            self.op("vector", lambda e: e.memset(scr[0:1, 0:2], 0.0),
                    writes=sorted(prev) + [key, ("c_pgscr",)], tag=f"pg_{nm}")
        return g, t

    def barrier(self):
        last = {}
        for o in self.ops:
            key = ("q", o.queue) if o.queue is not None else ("e", o.engine)
            last[key] = o.idx
        self._barrier = sorted(last.values())

    def dma_queue(self, name, engine):
        q = _Queue(name, engine)
        self.queues[name] = q
        return name

    def op(self, engine, emit, reads=(), writes=(), tag="", hazard=False):
        o = _Op(len(self.ops), engine, emit, tuple(reads), tuple(writes), None,
                tag, hazard=hazard)
        self.ops.append(o)
        self._track(o)
        return o

    def dma(self, queue, emit, reads=(), writes=(), tag="", inc=16):
        if queue in self.qsplit:
            lst = self.qsplit[queue]
            i = self._rr.get(queue, 0)
            self._rr[queue] = (i + 1) % len(lst)
            queue = lst[i]
        q = self.queues[queue]
        o = _Op(len(self.ops), q.engine, emit, tuple(reads), tuple(writes),
                queue, tag)
        o.inc_amt = inc
        self.ops.append(o)
        self._track(o)
        if q.engine == "gpsimd" or queue in self.chain or CHAIN_ALL_QUEUES:
            prev = self._last_dma.get(queue)
            if prev is not None and prev not in o.deps:
                o.deps.append(prev)
                self.ops[prev].need_inc = True
        self._last_dma[queue] = o.idx
        return o

    def _track(self, o):
        deps = set()
        for r in o.reads:
            w = self._last_write.get(r)
            if w is not None:
                deps.add(w)
            self._readers.setdefault(r, []).append(o.idx)
        for w in o.writes:
            pw = self._last_write.get(w)
            if pw is not None:
                deps.add(pw)
            for rd in self._readers.get(w, ()):
                deps.add(rd)
            self._last_write[w] = o.idx
            self._readers[w] = []
        deps.discard(o.idx)
        deps.update(self._barrier)
        deps.discard(o.idx)
        real = []
        for d in sorted(deps):
            pr = self.ops[d]
            if (pr.queue is None and pr.engine == o.engine
                    and pr.engine == "tensor" and not pr.hazard):
                continue
            real.append(d)
        o.deps = real
        for d in real:
            self.ops[d].need_inc = True

    def emit(self, final_engine="sync"):
        nc = self.nc
        eng_count = {e: 0 for e in ENGINE_NAMES}
        for o in self.ops:
            if o.queue is not None:
                q = self.queues[o.queue]
                q.count += getattr(o, "inc_amt", 16)
                o.inc_val = q.count
                o.need_inc = True
            elif o.need_inc:
                eng_count[o.engine] += 1
                o.inc_val = eng_count[o.engine]
        hw = {}
        for o in self.ops:
            for d in o.deps:
                pr = self.ops[d]
                if pr.queue is not None:
                    sem_key = ("q", pr.queue)
                else:
                    sem_key = ("e", pr.engine)
                    if pr.inc_val is None:
                        continue
                k = (o.engine, sem_key)
                if hw.get(k, 0) < pr.inc_val:
                    hw[k] = pr.inc_val
                    o.waits.append((sem_key, pr.inc_val))
        sems = {}
        for e in ENGINE_NAMES:
            if eng_count[e] > 0:
                sems[("e", e)] = nc.alloc_semaphore(f"sem_{e}")
        for qn, q in self.queues.items():
            if q.count > 0:
                sems[("q", qn)] = nc.alloc_semaphore(f"sem_q_{qn}")

        per_engine = {e: [] for e in ENGINE_NAMES}
        for o in self.ops:
            per_engine[o.engine].append(o)

        final_waits = []
        for qn, q in self.queues.items():
            if q.count > 0:
                final_waits.append((("q", qn), q.count))
        for e in ENGINE_NAMES:
            if eng_count[e] > 0 and e != final_engine:
                final_waits.append((("e", e), eng_count[e]))

        with nc.Block() as block:
            for e in ENGINE_NAMES:
                ops = per_engine[e]
                if not ops and e != final_engine:
                    continue

                def make_body(eng_name, eng_ops):
                    def body(eng):
                        for o in eng_ops:
                            for sem_key, val in o.waits:
                                eng.wait_ge(sems[sem_key], val)
                            inst = o.emit(eng)
                            if o.queue is not None:
                                inst.then_inc(sems[("q", o.queue)],
                                              getattr(o, "inc_amt", 16))
                            elif o.inc_val is not None:
                                inst.then_inc(sems[("e", eng_name)], 1)
                        if eng_name == final_engine:
                            for sem_key, val in final_waits:
                                eng.wait_ge(sems[sem_key], val)
                    return body

                getattr(block, e)(make_body(e, ops))

    def stats(self):
        n_inc = sum(1 for o in self.ops if o.inc_val is not None)
        n_wait = sum(len(o.waits) for o in self.ops)
        per_eng = {}
        for o in self.ops:
            per_eng[o.engine] = per_eng.get(o.engine, 0) + 1
        return dict(n_ops=len(self.ops), n_inc=n_inc, n_wait=n_wait,
                    per_engine=per_eng)


class KeyPrefix:

    def __init__(self, p, pfx, exempt=()):
        self.p = p
        self.pfx = pfx
        self.exempt = set(exempt)

    def _k(self, keys):
        out = []
        for k in keys:
            if k[0] in self.exempt:
                out.append(k)
            else:
                out.append((self.pfx + k[0],) + tuple(k[1:]))
        return out

    def op(self, engine, emit, reads=(), writes=(), **kw):
        return self.p.op(engine, emit, self._k(reads), self._k(writes), **kw)

    def dma(self, queue, emit, reads=(), writes=(), **kw):
        return self.p.dma(queue, emit, self._k(reads), self._k(writes), **kw)

    def palloc(self, nc, nm, shape, dtype, key=None):
        key = self._k([key if key is not None else (nm,)])[0]
        return self.p.palloc(nc, nm, shape, dtype, key=key)

    def barrier(self):
        self.p.barrier()

    @property
    def queues(self):
        return self.p.queues


class NPfx:

    def __init__(self, nc, pfx):
        self._nc = nc
        self._pfx = pfx

    def alloc_sbuf_tensor(self, name, shape, dtype, **kw):
        return self._nc.alloc_sbuf_tensor(self._pfx + name, shape, dtype, **kw)

    def sbuf_tensor(self, name, shape, dtype, **kw):
        return self._nc.sbuf_tensor(self._pfx + name, shape, dtype, **kw)

    def psum_tensor(self, name, shape, dtype, **kw):
        return self._nc.psum_tensor(self._pfx + name, shape, dtype, **kw)

    def dram_tensor(self, name, *a, **kw):
        return self._nc.dram_tensor(self._pfx + name, *a, **kw)

    def __getattr__(self, a):
        return getattr(self._nc, a)




def make_identity(nc, ident):
    nc.gpsimd.memset(ident, 0.0)
    return nc.gpsimd.affine_select(
        out=ident, in_=ident, compare_op=ALU.not_equal,
        fill=1.0, base=0, pattern=[[-1, ident.shape[0]]], channel_multiplier=1)


class Common:
    def __init__(self, nc, p):
        self.identf = nc.alloc_sbuf_tensor("c_identf", [128, 128], F32)
        self.identb = nc.alloc_sbuf_tensor("c_identb", [128, 128], BF16)
        self.onesf = nc.alloc_sbuf_tensor("c_onesf", [128, 128], F32)
        self.onesb = nc.alloc_sbuf_tensor("c_onesb", [128, 128], BF16)
        self.hmask = nc.alloc_sbuf_tensor("c_hmask", [128, 2], BF16)
        self.pgscr = nc.alloc_sbuf_tensor("c_pgscr", [1, 2], F32)
        p.pg_scr = self.pgscr
        for t, k in ((self.identf, "c_identf"), (self.identb, "c_identb")):
            p.op("gpsimd", lambda e, t=t: e.memset(t[:], 0.0),
                 writes=[(k,)], hazard=True)
            p.op("gpsimd", lambda e, t=t: e.affine_select(
                out=t[:], in_=t[:], compare_op=ALU.not_equal,
                fill=1.0, base=0, pattern=[[-1, 128]], channel_multiplier=1),
                reads=[(k,)], writes=[(k,)], hazard=True)
        p.op("vector", lambda e: e.memset(self.onesf[:], 1.0), writes=[("c_onesf",)])
        p.op("vector", lambda e: e.memset(self.onesb[:], 1.0), writes=[("c_onesb",)])
        p.op("vector", lambda e: e.memset(self.hmask[:], 0.0), writes=[("c_hmask",)])
        p.op("vector", lambda e: e.memset(self.hmask[0:64, 0:1], 1.0),
             writes=[("c_hmask",)])
        p.op("vector", lambda e: e.memset(self.hmask[64:128, 1:2], 1.0),
             writes=[("c_hmask",)])


def ln_tiles(nc, p, cm, name, x_ap, n_tiles, own_tiles, xnT_own, xnT_par,
             pT, qdma, invn=None, eps=1e-5, out_tm=None, src_rk=None):
    guards = []

    def ra(nm, shape, dtype):
        g = nc.sbuf_tensor(nm, shape, dtype)
        t = g.__enter__()
        guards.append(g)
        return t

    xt = [ra(f"{name}_xt{j}", [128, D], F32) for j in range(2)]
    xm = [ra(f"{name}_xm{j}", [128, D], BF16) for j in range(2)]
    dmy = ra(f"{name}_dmy", [128, D], F32)
    st = ra(f"{name}_st", [128, 8], F32)
    SK = (f"{name}_st",)
    for i in range(n_tiles):
        j = i % 2
        xk = (f"{name}_xt", j)
        p.dma(qdma, lambda e, i=i, j=j: e.dma_start(
            out=xt[j][:], in_=x_ap[i * 128:(i + 1) * 128, :]),
            reads=(src_rk(i) if src_rk else []), writes=[xk])
        p.op("vector", lambda e, j=j: e.reduce_sum(st[:, 0:1], xt[j][:], axis=X),
             reads=[xk], writes=[SK], hazard=True)
        p.op("scalar", lambda e, j=j: e.activation(dmy[:], xt[j][:], AF.Square,
                                                   accum_out=st[:, 1:2]),
             reads=[xk], writes=[SK, (f"{name}_dmy",)], hazard=True)
        p.op("vector", lambda e: e.tensor_scalar_mul(st[:, 2:3], st[:, 0:1],
                                                     1.0 / D),
             reads=[SK], writes=[SK])
        p.op("vector", lambda e: e.scalar_tensor_tensor(
            st[:, 6:7], st[:, 2:3], 1.0, st[:, 2:3], ALU.mult, ALU.mult),
            reads=[SK], writes=[SK])
        p.op("vector", lambda e: e.scalar_tensor_tensor(
            st[:, 3:4], st[:, 1:2], 1.0 / D, st[:, 6:7], ALU.mult, ALU.subtract),
            reads=[SK], writes=[SK])
        p.op("vector", lambda e: e.tensor_scalar_add(st[:, 6:7], st[:, 3:4], eps),
             reads=[SK], writes=[SK])
        p.op("scalar", lambda e: e.activation(st[:, 7:8], st[:, 6:7], AF.Sqrt),
             reads=[SK], writes=[SK], hazard=True)
        p.op("vector", lambda e: e.reciprocal(st[:, 4:5], st[:, 7:8]),
             reads=[SK], writes=[SK], hazard=True)
        p.op("vector", lambda e: e.scalar_tensor_tensor(
            st[:, 5:6], st[:, 2:3], -1.0, st[:, 4:5], ALU.mult, ALU.mult),
            reads=[SK], writes=[SK])
        p.op("scalar", lambda e, j=j: e.activation(
            xm[j][:], xt[j][:], AF.Identity, bias=st[:, 5:6], scale=st[:, 4:5]),
            reads=[xk, SK], writes=[(f"{name}_xm", j)])
        if invn is not None and i < own_tiles:
            p.op("vector", lambda e: e.tensor_scalar_mul(st[:, 7:8], st[:, 3:4],
                                                         float(D)),
                 reads=[SK], writes=[SK])
            p.op("vector", lambda e: e.reciprocal(st[:, 7:8], st[:, 7:8]),
                 reads=[SK], writes=[SK], hazard=True)
            p.op("vector", lambda e: e.tensor_mul(st[:, 7:8], st[:, 7:8],
                                                  st[:, 6:7]),
                 reads=[SK], writes=[SK])
            p.op("scalar", lambda e, i=i: e.activation(invn[:, i:i + 1],
                                                       st[:, 7:8], AF.Sqrt),
                 reads=[SK], writes=[("invn",)], hazard=True)
        if out_tm is not None:
            p.op("vector", lambda e, j=j, i=i: e.tensor_copy(
                out=out_tm[:, i * D:(i + 1) * D], in_=xm[j][:]),
                reads=[(f"{name}_xm", j)], writes=[("xtm", i // 4)])
        dst = xnT_own if i < own_tiles else xnT_par
        dk = "xno" if i < own_tiles else "xnp"
        loc = i if i < own_tiles else i - own_tiles
        for c in range(8):
            s = c % 2
            p.op("tensor", lambda e, j=j, c=c, s=s: e.transpose(
                pT[s][:], xm[j][:, c * 128:(c + 1) * 128], cm.identb[:]),
                reads=[(f"{name}_xm", j), ("c_identb",)], writes=[("pT", s)])
            p.op("vector", lambda e, c=c, s=s, loc=loc, dst=dst: e.tensor_copy(
                out=dst[:, c * HALF + loc * 128:c * HALF + loc * 128 + 128],
                in_=pT[s][:]),
                reads=[("pT", s)], writes=[(dk, loc // 4)])
    return guards


def free_guards(guards):
    for g in reversed(guards):
        g.__exit__(None, None, None)




def emit_l0(nc, p, cm, tapmap, ins, y1, stop_after="full"):
    _stops = ["consts", "ln1t", "ln", "pair_k", "pair_q", "pair_v", "pair_a", "pairs", "proj", "gating", "ffn", "full"]
    def _past(ph):
        return _stops.index(stop_after) <= _stops.index(ph)
    x = ins["x"]
    w_qkv = ins["w_qkv"]
    w_proj = ins["w_proj"]
    b_proj = ins["b_proj"]
    gate_w = ins["gate_w"]
    w_gate = ins["w_gate"]
    w_up = ins["w_up"]
    w_down = ins["w_down"]
    sw_gate = ins["sw_gate"]
    sw_up = ins["sw_up"]
    sw_down = ins["sw_down"]
    qx, qw, qm = "qx", "qw", "qm"

    g_dacc = nc.sbuf_tensor("dacc", [128, 8 * HALF], F32, side="right")
    dacc = g_dacc.__enter__()
    g_xnT_own = nc.sbuf_tensor("xnT_own", [128, 8 * HALF], BF16, side="right")
    xnT_own = g_xnT_own.__enter__()
    g_attnoT = nc.sbuf_tensor("attnoT", [128, 8 * HALF], BF16, side="right")
    attnoT = g_attnoT.__enter__()
    g_xnT_par = nc.sbuf_tensor("xnT_par", [128, 8 * HALF], BF16, side="right")
    xnT_par = g_xnT_par.__enter__()
    invn = nc.alloc_sbuf_tensor("invn", [128, 16], F32)
    negM = nc.alloc_sbuf_tensor("negM", [128, 16], F32)
    bpr = nc.alloc_sbuf_tensor("bpr", [128, 8], F32)

    def palloc(nm, shape, dtype, key=None):
        return p.palloc(nc, nm, shape, dtype, key=key)

    g_pT0, pT0 = palloc("pT0", [128, 128], BF16, key=("pT", 0))
    g_pT1, pT1 = palloc("pT1", [128, 128], BF16, key=("pT", 1))

    for c in range(8):
        p.dma(qx, lambda e, c=c: e.dma_start(
            out=bpr[:, c:c + 1],
            in_=b_proj[c * 128:(c + 1) * 128].rearrange("(a b) -> a b", b=1)),
            writes=[("bpr",)])

    if _past("consts"):
        p.dma(qx, lambda e: e.dma_start(out=y1[0:128, 0:8], in_=bpr[:]),
              reads=[("bpr",)])
        return True

    tapmap["xnT_own"] = (xnT_own, [("xno", i) for i in range(4)])
    tapmap["xnT_par"] = (xnT_par, [("xnp", i) for i in range(4)])
    tapmap["invn"] = (invn, [("invn",)])
    tapmap["negM"] = (negM, [("negM",)])

    n_ln_tiles = 1 if stop_after == "ln1t" else 32
    lng = ln_tiles(nc, p, cm, "ln0", x, n_ln_tiles, 16, xnT_own, xnT_par,
                   [pT0, pT1], qx, invn=invn)
    free_guards(lng)
    p.barrier()
    if _past("ln1t"):
        p.dma(qx, lambda e: e.dma_start(out=y1[0:128, 0:8], in_=bpr[:]),
              reads=[("bpr",)])
        return True

    if _past("ln"):
        p.dma(qx, lambda e: e.dma_start(out=y1[0:128, 0:8], in_=bpr[:]),
              reads=[("bpr",)])
        return True

    g_pT1.__exit__(None, None, None)
    g_pT0.__exit__(None, None, None)
    g_pKV0, pKV0 = palloc("pKV0", [128, 512], F32)
    g_pKV1, pKV1 = palloc("pKV1", [128, 512], F32)
    g_pS0, pS0 = palloc("pS0", [128, 512], F32)
    g_pS1, pS1 = palloc("pS1", [128, 512], F32)
    g_pO, pO = palloc("pO", [128, 512], F32)
    g_pB, pB = palloc("pB", [64, 512], F32)


    def xn_rhs(dc, t0, w):
        if t0 < HALF:
            return (xnT_own[:, dc * HALF + t0:dc * HALF + t0 + w],
                    ("xno", t0 // 512))
        t = t0 - HALF
        return (xnT_par[:, dc * HALF + t:dc * HALF + t + w], ("xnp", t // 512))

    pair_guards = []

    def rpa(nm, shape, dtype):
        g = nc.sbuf_tensor(nm, shape, dtype, side="right")
        t = g.__enter__()
        pair_guards.append(g)
        return t

    sqs = rpa("sqs", [128, 512], BF16)
    nsc = rpa("nsc", [2, 16], F32)
    diag = rpa("diag", [2, 2], F32)
    pts = [rpa(f"pts{j}", [128, 512], BF16) for j in range(4)]
    rown = rpa("rown", [128, 512], F32)
    bcs = rpa("bcs", [64, 512], F32)
    ostg = rpa("ostg", [64, 512], BF16)
    wq_g = rpa("wq_g", [128, 8 * 128], BF16)
    wk_g = rpa("wk_g", [128, 8 * 128], BF16)
    wv_g = rpa("wv_g", [128, 8 * 128], BF16)
    kT = rpa("kT", [128, SEQ], BF16)
    qT = rpa("qT", [128, HALF], BF16)
    vt = rpa("vt", [128, 32 * 256], BF16)

    tapmap["kT"] = (kT, [("kT", i) for i in range(4)])
    tapmap["qT"] = (qT, [("qT", i) for i in range(4)])
    tapmap["vt"] = (vt, [("vt", i) for i in range(32)] + [("vt_ones",)])
    vt_r = vt[:].rearrange("p (i hh c) -> p i hh c", i=32, hh=2)
    p.op("vector", lambda e: e.memset(vt[:], 0.0), writes=[("vt_ones",)])
    p.op("vector", lambda e: e.memset(vt_r[:, :, :, 64:65], 1.0),
         reads=[("vt_ones",)], writes=[("vt_ones",)], hazard=True)

    import os as _os
    n_pairs = 1 if stop_after.startswith("pair_") else 8
    for kc in range(n_pairs):
        g_pN, pN = palloc(f"pN{kc}", [2, 512], F32)
        g_pM, pM = palloc(f"pM{kc}", [128, 2], F32)
        for wt, wn, col0 in ((wq_g, "wq_g", kc * 128),
                             (wk_g, "wk_g", D + kc * 128),
                             (wv_g, "wv_g", 2 * D + kc * 128)):
            p.dma(qw, lambda e, wt=wt, col0=col0: e.dma_start(
                out=wt[:].rearrange("p (dc n) -> p dc n", dc=8),
                in_=w_qkv[:, col0:col0 + 128].rearrange("(dc p) n -> p dc n",
                                                        p=128)),
                writes=[(wn,)], tag=f"load_{wn}{kc}")

        p.op("vector", lambda e: e.memset(nsc[:, 0:2], 0.0), writes=[("nsc",)])
        for t in range(int(_os.environ.get("KWIN", "8"))):
            ps, pk = (pKV0, ("pKV0",)) if t % 2 == 0 else (pKV1, ("pKV1",))
            _kdc = int(_os.environ.get("KDC", "8"))
            for dc in range(_kdc):
                rhs, rk = xn_rhs(dc, t * 512, 512)
                p.op("tensor", lambda e, dc=dc, rhs=rhs, ps=ps, _kdc=_kdc: e.matmul(
                    ps[:], wk_g[:, dc * 128:(dc + 1) * 128],
                    rhs, start=(dc == 0), stop=(dc == _kdc - 1)),
                    reads=[("wk_g",), rk], writes=[pk])
            _kp = int(_os.environ.get("KPART", "5"))
            if _kp >= 2:
                p.op("scalar", lambda e, t=t, ps=ps: e.copy(
                    out=kT[:, t * 512:t * 512 + 512], in_=ps[:]),
                    reads=[pk], writes=[("kT", t // 2)])
            if _kp >= 3:
                p.op("vector", lambda e, t=t: e.tensor_mul(
                    sqs[:], kT[:, t * 512:t * 512 + 512],
                    kT[:, t * 512:t * 512 + 512]),
                    reads=[("kT", t // 2)], writes=[("sqs",)])
            if _kp >= 4:
                p.op("tensor", lambda e, pN=pN: e.matmul(
                    pN[:], cm.hmask[:], sqs[:], start=True, stop=True),
                    reads=[("sqs",), ("c_hmask",)], writes=[(f"pN{kc}",)])
            if _kp >= 5:
                p.op("vector", lambda e, pN=pN: e.reduce_max(
                    nsc[:, 2:3], pN[:], axis=X),
                    reads=[(f"pN{kc}",)], writes=[("nsc",)], hazard=True)
                p.op("vector", lambda e: e.tensor_max(nsc[:, 1:2], nsc[:, 1:2],
                                                      nsc[:, 2:3]),
                     reads=[("nsc",)], writes=[("nsc",)])
        for t in range(4 if not _past("pair_k") else 0):
            ps, pk = (pKV0, ("pKV0",)) if t % 2 == 0 else (pKV1, ("pKV1",))
            for dc in range(8):
                rhs, rk = xn_rhs(dc, t * 512, 512)
                p.op("tensor", lambda e, dc=dc, rhs=rhs, ps=ps: e.matmul(
                    ps[:], wq_g[:, dc * 128:(dc + 1) * 128],
                    rhs, start=(dc == 0), stop=(dc == 7)),
                    reads=[("wq_g",), rk], writes=[pk])
            p.op("scalar", lambda e, t=t, ps=ps: e.copy(
                out=qT[:, t * 512:t * 512 + 512], in_=ps[:]),
                reads=[pk], writes=[("qT", t)])
            p.op("vector", lambda e, t=t: e.tensor_mul(
                sqs[:], qT[:, t * 512:t * 512 + 512],
                qT[:, t * 512:t * 512 + 512]),
                reads=[("qT", t)], writes=[("sqs",)])
            p.op("tensor", lambda e, pN=pN: e.matmul(pN[:], cm.hmask[:], sqs[:],
                                              start=True, stop=True),
                 reads=[("sqs",), ("c_hmask",)], writes=[(f"pN{kc}",)])
            p.op("vector", lambda e, pN=pN: e.reduce_max(nsc[:, 3:4], pN[:], axis=X),
                 reads=[(f"pN{kc}",)], writes=[("nsc",)], hazard=True)
            p.op("vector", lambda e: e.tensor_max(nsc[:, 0:1], nsc[:, 0:1],
                                                  nsc[:, 3:4]),
                 reads=[("nsc",)], writes=[("nsc",)])
        if _past("pair_q"):
            continue
        p.op("vector", lambda e: e.tensor_mul(nsc[:, 4:5], nsc[:, 0:1],
                                              nsc[:, 1:2]),
             reads=[("nsc",)], writes=[("nsc",)])
        p.op("scalar", lambda e: e.activation(nsc[:, 5:6], nsc[:, 4:5], AF.Sqrt),
             reads=[("nsc",)], writes=[("nsc",)], hazard=True)
        p.op("vector", lambda e: e.tensor_scalar_mul(nsc[:, 5:6], nsc[:, 5:6],
                                                     -SCALE),
             reads=[("nsc",)], writes=[("nsc",)], hazard=True)
        p.op("vector", lambda e: e.tensor_mul(
            diag[:], nsc[:, 5:6].to_broadcast([2, 2]), cm.identf[0:2, 0:2]),
            reads=[("nsc",), ("c_identf",)], writes=[("diag",)])
        h0 = 2 * kc
        p.op("tensor", lambda e: e.matmul(
            pM[:], cm.onesf[0:2, 0:128], diag[:], start=True, stop=True),
            reads=[("diag",), ("c_onesf",)], writes=[(f"pM{kc}",)])
        p.op("vector", lambda e, h0=h0, pM=pM, kc=kc: e.tensor_copy(
            out=negM[:, h0:h0 + 2], in_=pM[:]),
            reads=[(f"pM{kc}",)], writes=[("negM",)])
        g_pM.__exit__(None, None, None)
        g_pN.__exit__(None, None, None)
        g_pOB, pOB = palloc(f"pOB{kc}", [128, 512], F32)

        for i in range(32 if not _past("pair_v") else 0):
            ps, pk = (pKV0, ("pKV0",)) if i % 2 == 0 else (pKV1, ("pKV1",))
            for dc in range(8):
                lhs, rk = xn_rhs(dc, i * 128, 128)
                p.op("tensor", lambda e, lhs=lhs, dc=dc, ps=ps: e.matmul(
                    ps[:, 0:128], lhs, wv_g[:, dc * 128:(dc + 1) * 128],
                    start=(dc == 0), stop=(dc == 7)),
                    reads=[("wv_g",), rk], writes=[pk])
            p.op("vector", lambda e, i=i, ps=ps: e.tensor_copy(
                out=vt_r[:, i, :, 0:64],
                in_=ps[:, 0:128].rearrange("p (hh c) -> p hh c", hh=2)),
                reads=[pk], writes=[("vt", i)])

        if _past("pair_a"):
            g_pOB.__exit__(None, None, None)
            continue
        pSa = (pS0, pS1)
        pSak = (("pS0",), ("pS1",))
        pSb = (pKV0, pKV1)
        pSbk = (("pKV0",), ("pKV1",))
        pOk = ("pO",)
        pOBk = (f"pOB{kc}",)

        def emit_S2(tc, q0, qb):
            sa, sak = pSa[tc % 2], pSak[tc % 2]
            sb, sbk = pSb[tc % 2], pSbk[tc % 2]
            p.op("tensor", lambda e, tc=tc, q0=q0, sa=sa: e.matmul(
                sa[:], kT[0:64, tc * 128:tc * 128 + 128],
                qT[0:64, q0:q0 + 512],
                start=True, stop=True, tile_position=(0, 0)),
                reads=[("kT", tc // 8), ("qT", qb)], writes=[sak])
            p.op("tensor", lambda e, tc=tc, q0=q0, sb=sb: e.matmul(
                sb[:], kT[64:128, tc * 128:tc * 128 + 128],
                qT[64:128, q0:q0 + 512],
                start=True, stop=True, tile_position=(64, 0)),
                reads=[("kT", tc // 8), ("qT", qb)], writes=[sbk])

        hA, hB = 2 * kc, 2 * kc + 1
        for qb in range(4):
            q0 = qb * 512
            emit_S2(0, q0, qb)
            for tc in range(32):
                sa, sak = pSa[tc % 2], pSak[tc % 2]
                sb, sbk = pSb[tc % 2], pSbk[tc % 2]
                pa, pak = pts[tc % 2], ("pts", tc % 2)
                pb_, pbk = pts[2 + tc % 2], ("pts", 2 + tc % 2)
                p.op("scalar", lambda e, sa=sa, pa=pa, hA=hA: e.activation(
                    pa[:], sa[:], AF.Exp, bias=negM[:, hA:hA + 1], scale=SCALE),
                    reads=[sak, ("negM",)], writes=[pak])
                p.op("scalar", lambda e, sb=sb, pb_=pb_, hB=hB: e.activation(
                    pb_[:], sb[:], AF.Exp, bias=negM[:, hB:hB + 1], scale=SCALE),
                    reads=[sbk, ("negM",)], writes=[pbk])
                if tc + 1 < 32:
                    emit_S2(tc + 1, q0, qb)
                p.op("tensor", lambda e, tc=tc, pa=pa: e.matmul(
                    pO[:], vt[:, tc * 256:tc * 256 + 128],
                    pa[:], start=(tc == 0), stop=(tc == 31)),
                    reads=[pak, ("vt", tc), ("vt_ones",)], writes=[pOk])
                p.op("tensor", lambda e, tc=tc, pb_=pb_, pOB=pOB: e.matmul(
                    pOB[:], vt[:, tc * 256 + 128:tc * 256 + 256],
                    pb_[:], start=(tc == 0), stop=(tc == 31)),
                    reads=[pbk, ("vt", tc), ("vt_ones",)], writes=[pOBk])
            p.op("scalar", lambda e: e.copy(out=rown[64:65, :],
                                            in_=pO[64:65, :]),
                 reads=[pOk], writes=[("rown",)])
            p.op("vector", lambda e: e.reciprocal(rown[64:65, :],
                                                  rown[64:65, :]),
                 reads=[("rown",)], writes=[("rown",)], hazard=True)
            p.op("tensor", lambda e: e.matmul(
                pB[:], cm.onesf[64:65, 0:64], rown[64:65, :],
                start=True, stop=True, tile_position=(64, 0)),
                reads=[("rown",), ("c_onesf",)], writes=[("pB",)])
            p.op("scalar", lambda e: e.copy(out=bcs[:], in_=pB[:]),
                 reads=[("pB",)], writes=[("bcs",)])
            p.op("vector", lambda e, q0=q0, kc=kc: e.tensor_mul(
                attnoT[0:64, kc * HALF + q0:kc * HALF + q0 + 512],
                pO[0:64, :], bcs[:]),
                reads=[pOk, ("bcs",)], writes=[("aT", kc, 0, q0)])
            p.op("scalar", lambda e, pOB=pOB: e.copy(out=rown[64:65, :],
                                                     in_=pOB[64:65, :]),
                 reads=[pOBk, ("aT", kc, 0, q0)], writes=[("rown",)])
            p.op("vector", lambda e: e.reciprocal(rown[64:65, :],
                                                  rown[64:65, :]),
                 reads=[("rown",)], writes=[("rown",)], hazard=True)
            p.op("tensor", lambda e: e.matmul(
                pB[:], cm.onesf[64:65, 0:64], rown[64:65, :],
                start=True, stop=True, tile_position=(64, 0)),
                reads=[("rown",), ("c_onesf",)], writes=[("pB",)])
            p.op("scalar", lambda e: e.copy(out=bcs[:], in_=pB[:]),
                 reads=[("pB",)], writes=[("bcs",)])
            p.op("vector", lambda e, pOB=pOB: e.tensor_mul(
                ostg[:], pOB[0:64, :], bcs[:]),
                reads=[pOBk, ("bcs",)], writes=[("ostg",)])
            p.dma(qm, lambda e, q0=q0, kc=kc: e.dma_start(
                out=attnoT[64:128, kc * HALF + q0:kc * HALF + q0 + 512],
                in_=ostg[:]),
                reads=[("ostg",)], writes=[("aT", kc, 1, q0)])
        g_pOB.__exit__(None, None, None)

    if _past("pairs"):
        p.dma(qx, lambda e: e.dma_start(out=y1[0:128, 0:8], in_=bpr[:]),
              reads=[("bpr",)])
        return True

    if _past("pair_a") and not _past("pairs"):
        pass
    if stop_after.startswith("pair_"):
        p.dma(qx, lambda e: e.dma_start(out=y1[0:128, 0:8], in_=bpr[:]),
              reads=[("bpr",)])
        return True

    for g in (g_pB, g_pO, g_pS1, g_pS0, g_pKV1, g_pKV0):
        g.__exit__(None, None, None)
    free_guards(pair_guards)
    if not _os.environ.get("KEEP_PAR"):
        g_xnT_par.__exit__(None, None, None)
    p.barrier()
    g_pP0, pP0 = palloc("pP0", [128, 512], F32)
    g_pP1, pP1 = palloc("pP1", [128, 512], F32)
    tapmap["dacc"] = (dacc, [("dacc", c, w) for c in range(8) for w in range(4)])
    g_wpj = nc.sbuf_tensor("wpj", [128, 8 * 1024], BF16, side="right")
    wpj = g_wpj.__enter__()
    p.dma(qw, lambda e: e.dma_start(
        out=wpj[:].rearrange("p (c n) -> p c n", c=8),
        in_=w_proj.rearrange("(c p) n -> p c n", p=128)),
        writes=[("wpj",)])
    aT_keys = [("aT", c, par, q0) for c in range(8) for par in range(2)
               for q0 in (0, 512, 1024, 1536)]
    tapmap["attnoT"] = (attnoT, aT_keys)
    for qb in range(4):
        q0 = qb * 512
        for dc8 in range(8):
            ps, pk = (pP0, ("pP0",)) if dc8 % 2 == 0 else (pP1, ("pP1",))
            rks = [("aT", c, par, q0) for c in range(8) for par in range(2)]
            for c in range(8):
                p.op("tensor", lambda e, c=c, dc8=dc8, q0=q0, ps=ps: e.matmul(
                    ps[:], wpj[:, c * 1024 + dc8 * 128:c * 1024 + dc8 * 128 + 128],
                    attnoT[:, c * HALF + q0:c * HALF + q0 + 512],
                    start=(c == 0), stop=(c == 7)),
                    reads=[("wpj",)] + rks, writes=[pk])
            p.op("scalar", lambda e, dc8=dc8, q0=q0, ps=ps: e.activation(
                dacc[:, dc8 * HALF + q0:dc8 * HALF + q0 + 512], ps[:], AF.Identity,
                bias=bpr[:, dc8:dc8 + 1]),
                reads=[pk, ("bpr",)], writes=[("dacc", dc8, qb)])

    g_wpj.__exit__(None, None, None)
    if not _os.environ.get("KEEP_PAR"):
        g_attnoT.__exit__(None, None, None)
    p.barrier()

    if _past("proj"):
        p.dma(qx, lambda e: e.dma_start(out=y1[0:128, 0:8], in_=bpr[:]),
              reads=[("bpr",)])
        return True

    g_pP1.__exit__(None, None, None)
    g_pP0.__exit__(None, None, None)
    g_pG, pG = palloc("pG", [128, 16], F32)
    g_pGW, pGW = palloc("pGW", [128, 16], BF16)
    g_pCT, pCT = palloc("pCT", [16, 128], BF16)
    gat_guards = []

    def gga(nm, shape, dtype):
        g = nc.sbuf_tensor(nm, shape, dtype, side="right")
        t = g.__enter__()
        gat_guards.append(g)
        return t

    gst = gga("gst", [16, 4], F32)
    gwn = gga("gwn", [16, D], BF16)
    gwnT = gga("gwnT", [128, 128], BF16)
    scor = gga("scor", [128, 16], F32)
    m8 = gga("m8", [128, 8], F32)
    s8 = gga("s8", [128, 1], F32)
    r8 = gga("r8", [128, 1], F32)
    zap = gga("zap", [128, 16], F32)
    cmbf = gga("cmbf", [128, 16], BF16)
    combT = gga("combT", [16, HALF], BF16)
    tapmap["combT"] = (combT, [("combT", i) for i in range(4)])
    sel = gga("sel", [16, 16 * 128], BF16)
    g_gwf = nc.sbuf_tensor("gwf", [16, D], F32, side="right")
    gwf = g_gwf.__enter__()
    g_gsq = nc.sbuf_tensor("gsq", [16, D], F32, side="right")
    gsq = g_gsq.__enter__()

    p.dma(qx, lambda e: e.dma_start(out=gwf[:], in_=gate_w), writes=[("gwf",)])
    p.op("scalar", lambda e: e.activation(gsq[:], gwf[:], AF.Square,
                                          accum_out=gst[:, 0:1]),
         reads=[("gwf",)], writes=[("gsq",), ("gst",)], hazard=True)
    p.op("vector", lambda e: e.reciprocal(gst[:, 1:2], gst[:, 0:1]),
         reads=[("gst",)], writes=[("gst",)], hazard=True)
    p.op("scalar", lambda e: e.activation(gst[:, 2:3], gst[:, 1:2], AF.Sqrt),
         reads=[("gst",)], writes=[("gst",)], hazard=True)
    p.op("vector", lambda e: e.tensor_scalar_mul(gwn[:], gwf[:], gst[:, 2:3]),
         reads=[("gwf",), ("gst",)], writes=[("gwn",)])
    for c in range(8):
        p.op("tensor", lambda e, c=c: e.transpose(
            pGW[:], gwn[0:16, c * 128:(c + 1) * 128], cm.identb[0:16, 0:16]),
            reads=[("gwn",), ("c_identb",)], writes=[("pGW",)])
        p.op("vector", lambda e, c=c: e.tensor_copy(
            out=gwnT[:, c * 16:(c + 1) * 16], in_=pGW[:]),
            reads=[("pGW",)], writes=[("gwnT",)])
    g_gsq.__exit__(None, None, None)
    g_gwf.__exit__(None, None, None)
    p.barrier()

    def make_sel(e):
        nc.gpsimd.memset(sel[:], 0.0)
        return nc.gpsimd.affine_select(
            out=sel[:].rearrange("p (eb c) -> p eb c", eb=16),
            in_=sel[:].rearrange("p (eb c) -> p eb c", eb=16),
            compare_op=ALU.not_equal, fill=1.0, base=0,
            pattern=[[1, 16], [0, 128]], channel_multiplier=-1)
    p.op("gpsimd", lambda e: make_sel(e), writes=[("sel",)])

    for i in range(16):
        for dc in range(8):
            p.op("tensor", lambda e, i=i, dc=dc: e.matmul(
                pG[:], xnT_own[:, dc * HALF + i * 128:dc * HALF + i * 128 + 128],
                gwnT[:, dc * 16:(dc + 1) * 16],
                start=(dc == 0), stop=(dc == 7)),
                reads=[("gwnT",), ("xno", i // 4)], writes=[("pG",)])
        p.op("scalar", lambda e, i=i: e.activation(
            scor[:], pG[:], AF.Sigmoid, scale=invn[:, i:i + 1]),
            reads=[("pG",), ("invn",)], writes=[("scor",)])
        p.op("vector", lambda e: e.max(out=m8[:], in_=scor[:]),
             reads=[("scor",)], writes=[("m8",)], hazard=True)
        p.op("vector", lambda e: e.reduce_sum(s8[:], m8[:], axis=X),
             reads=[("m8",)], writes=[("s8",)], hazard=True)
        p.op("vector", lambda e: e.reciprocal(r8[:], s8[:]),
             reads=[("s8",)], writes=[("r8",)], hazard=True)
        p.op("vector", lambda e: e.match_replace(out=zap[:], in_to_replace=m8[:],
                                                 in_values=scor[:], imm_value=0.0),
             reads=[("scor",), ("m8",)], writes=[("zap",), ("m8",)],
             hazard=True)
        p.op("vector", lambda e: e.tensor_sub(zap[:], scor[:], zap[:]),
             reads=[("scor",), ("zap",)], writes=[("zap",)])
        p.op("vector", lambda e: e.tensor_scalar_mul(cmbf[:], zap[:], r8[:]),
             reads=[("zap",), ("r8",)], writes=[("cmbf",)])
        p.op("tensor", lambda e: e.transpose(
            pCT[:], cmbf[:], cm.identb[:]),
            reads=[("cmbf",), ("c_identb",)], writes=[("pCT",)])
        p.op("vector", lambda e, i=i: e.tensor_copy(
            out=combT[0:16, i * 128:(i + 1) * 128], in_=pCT[:]),
            reads=[("pCT",)], writes=[("combT", i // 4)])

    if _past("gating"):
        p.dma(qx, lambda e: e.dma_start(out=y1[0:128, 0:8], in_=bpr[:]),
              reads=[("bpr",)])
        return True

    p.barrier()
    g_pCT.__exit__(None, None, None)
    g_pGW.__exit__(None, None, None)
    g_pG.__exit__(None, None, None)
    g_pGU0, pGU0 = palloc("pGU0", [128, 512], F32)
    g_pGU1, pGU1 = palloc("pGU1", [128, 512], F32)
    g_pBC, pBC = palloc("pBC", [128, 512], F32)
    _psRg = [palloc(f"pR{j}", [128, 512], F32, key=("psR", j))
             for j in range(4)]
    psR = [t for _, t in _psRg]
    ffn_guards = []

    def ffa(nm, shape, dtype):
        g = nc.sbuf_tensor(nm, shape, dtype, side="right")
        t = g.__enter__()
        ffn_guards.append(g)
        return t

    wg = [ffa(f"wg{j}", [128, 8 * 512], BF16) for j in range(2)]
    wu = [ffa(f"wu{j}", [128, 8 * 512], BF16) for j in range(2)]
    wd = [ffa(f"wd{j}", [128, 4 * 1024], BF16) for j in range(2)]
    fbc = ffa("fbc", [128, 512], BF16)
    gsil = ffa("gsil", [128, 4 * 512], BF16)
    ub = ffa("ub", [128, 4 * 512], BF16)
    hb = ffa("hb", [128, 4 * 512], BF16)

    def load_expert(e_, j):
        if e_ < 16:
            srcs = (w_gate[e_], w_up[e_], w_down[e_])
        else:
            srcs = (sw_gate, sw_up, sw_down)
        p.dma(qw, lambda e, s=srcs[0], j=j: e.dma_start(
            out=wg[j][:].rearrange("p (dc n) -> p dc n", dc=8),
            in_=s.rearrange("(dc p) n -> p dc n", p=128)),
            writes=[("wg", j)])
        p.dma(qw, lambda e, s=srcs[1], j=j: e.dma_start(
            out=wu[j][:].rearrange("p (dc n) -> p dc n", dc=8),
            in_=s.rearrange("(dc p) n -> p dc n", p=128)),
            writes=[("wu", j)])
        p.dma(qw, lambda e, s=srcs[2], j=j: e.dma_start(
            out=wd[j][:].rearrange("p (hc n) -> p hc n", hc=4),
            in_=s.rearrange("(hc p) n -> p hc n", p=128)),
            writes=[("wd", j)])

    load_expert(0, 0)
    for ex in range(17):
        j = ex % 2
        if ex + 1 < 17:
            load_expert(ex + 1, (ex + 1) % 2)
        for w4 in range(4):
            t0 = w4 * 512
            if ex < 16:
                p.op("tensor", lambda e, ex=ex, t0=t0: e.matmul(
                    pBC[:], sel[:, ex * 128:(ex + 1) * 128],
                    combT[0:16, t0:t0 + 512], start=True, stop=True),
                    reads=[("sel",), ("combT", w4)], writes=[("pBC",)])
                p.op("scalar", lambda e: e.copy(out=fbc[:], in_=pBC[:]),
                     reads=[("pBC",)], writes=[("fbc",)])
            for hc in range(4):
                for dc in range(8):
                    p.op("tensor", lambda e, j=j, hc=hc, dc=dc, t0=t0: e.matmul(
                        pGU0[:], wg[j][:, dc * 512 + hc * 128:dc * 512 + hc * 128 + 128],
                        xnT_own[:, dc * HALF + t0:dc * HALF + t0 + 512],
                        start=(dc == 0), stop=(dc == 7)),
                        reads=[("wg", j), ("xno", w4)], writes=[("pGU0",)])
                p.op("scalar", lambda e, hc=hc: e.activation(
                    gsil[:, hc * 512:(hc + 1) * 512], pGU0[:], AF.Silu),
                    reads=[("pGU0",)], writes=[("gsil", hc)])
                for dc in range(8):
                    p.op("tensor", lambda e, j=j, hc=hc, dc=dc, t0=t0: e.matmul(
                        pGU1[:], wu[j][:, dc * 512 + hc * 128:dc * 512 + hc * 128 + 128],
                        xnT_own[:, dc * HALF + t0:dc * HALF + t0 + 512],
                        start=(dc == 0), stop=(dc == 7)),
                        reads=[("wu", j), ("xno", w4)], writes=[("pGU1",)])
                if ex < 16:
                    p.op("vector", lambda e, hc=hc: e.tensor_mul(
                        ub[:, hc * 512:(hc + 1) * 512], pGU1[:], fbc[:]),
                        reads=[("pGU1",), ("fbc",)], writes=[("ub", hc)])
                else:
                    p.op("scalar", lambda e, hc=hc: e.copy(
                        out=ub[:, hc * 512:(hc + 1) * 512], in_=pGU1[:]),
                        reads=[("pGU1",)], writes=[("ub", hc)])
                p.op("vector", lambda e, hc=hc: e.tensor_mul(
                    hb[:, hc * 512:(hc + 1) * 512],
                    gsil[:, hc * 512:(hc + 1) * 512],
                    ub[:, hc * 512:(hc + 1) * 512]),
                    reads=[("gsil", hc), ("ub", hc)], writes=[("hb", hc)])
            for half in range(2):
                for d4 in range(4):
                    dc8 = half * 4 + d4
                    for hc in range(4):
                        p.op("tensor", lambda e, j=j, hc=hc, dc8=dc8, d4=d4:
                             e.matmul(
                            psR[d4][:],
                            wd[j][:, hc * 1024 + dc8 * 128:hc * 1024 + dc8 * 128 + 128],
                            hb[:, hc * 512:(hc + 1) * 512],
                            start=(hc == 0), stop=(hc == 3)),
                            reads=[("wd", j), ("hb", hc)], writes=[("psR", d4)])
                for d4 in range(4):
                    dc8 = half * 4 + d4
                    p.op("vector", lambda e, dc8=dc8, t0=t0, d4=d4: e.tensor_add(
                        dacc[:, dc8 * HALF + t0:dc8 * HALF + t0 + 512],
                        dacc[:, dc8 * HALF + t0:dc8 * HALF + t0 + 512],
                        psR[d4][:]),
                        reads=[("psR", d4), ("dacc", dc8, w4)],
                        writes=[("dacc", dc8, w4)])

    if _past("ffn"):
        p.dma(qx, lambda e: e.dma_start(out=y1[0:128, 0:8], in_=bpr[:]),
              reads=[("bpr",)])
        return True

    for _g, _t in reversed(_psRg):
        _g.__exit__(None, None, None)
    g_pBC.__exit__(None, None, None)
    g_pGU1.__exit__(None, None, None)
    g_pGU0.__exit__(None, None, None)
    for g in reversed(ffn_guards):
        g.__exit__(None, None, None)
    for g in reversed(gat_guards):
        g.__exit__(None, None, None)
    g_xnT_own.__exit__(None, None, None)
    p.barrier()
    g_pX0, pX0 = palloc("pX0", [128, 128], F32)
    g_pX1, pX1 = palloc("pX1", [128, 128], F32)
    x1g = [nc.sbuf_tensor(f"x1t{j}", [128, D], F32, side="right") for j in range(2)]
    x1t = [g.__enter__() for g in x1g]
    y1g = [nc.sbuf_tensor(f"y1t{j}", [128, D], F32, side="right") for j in range(2)]
    y1t = [g.__enter__() for g in y1g]
    for i in range(16):
        j = i % 2
        p.dma(qx, lambda e, i=i, j=j: e.dma_start(
            out=x1t[j][:], in_=x[i * 128:(i + 1) * 128, :]),
            writes=[("x1t", j)])
        for c in range(8):
            s = c % 2
            px = pX0 if s == 0 else pX1
            pxk = ("pX0",) if s == 0 else ("pX1",)
            p.op("tensor", lambda e, c=c, i=i, px=px: e.transpose(
                px[:], dacc[:, c * HALF + i * 128:c * HALF + i * 128 + 128],
                cm.identf[:]),
                reads=[("dacc", c, i // 4), ("c_identf",)], writes=[pxk])
            p.op("vector", lambda e, c=c, j=j, px=px, pxk=pxk: e.tensor_add(
                y1t[j][:, c * 128:(c + 1) * 128], x1t[j][:, c * 128:(c + 1) * 128],
                px[:]),
                reads=[pxk, ("x1t", j)], writes=[("y1t", j)])
        p.dma(qx, lambda e, i=i, j=j: e.dma_start(
            out=y1[i * 128:(i + 1) * 128, :], in_=y1t[j][:]),
            reads=[("y1t", j)], writes=[("y1out", i)])

    g_pX1.__exit__(None, None, None)
    g_pX0.__exit__(None, None, None)
    for g in reversed(y1g):
        g.__exit__(None, None, None)
    for g in reversed(x1g):
        g.__exit__(None, None, None)
    g_dacc.__exit__(None, None, None)
    p.barrier()
    return False



def _mk_prog(nc):
    p = Prog(nc)
    p.dma_queue("qx", "sync")
    p.dma_queue("qw", "gpsimd")
    p.dma_queue("qwB", "gpsimd")
    p.dma_queue("qm", "scalar")
    p.dma_queue("qb", "sync")
    p.chain = {"qb"}
    p.qsplit = {"qw": ["qw", "qwB"]}
    return p


def _do_finish(nc, p, tapmap, taps):
    for tname in taps:
        if tname not in tapmap:
            continue
        t, keys = tapmap[tname]
        o = nc.dram_tensor("tap_" + tname, list(t.shape), t.dtype,
                           kind="ExternalOutput").ap()
        p.dma("qx", lambda e, t=t, o=o: e.dma_start(out=o, in_=t[:]),
              reads=keys)
    p.emit()
    return nc, p.stats()


def l0_dram_ins(nc):
    return dict(
        x=nc.dram_tensor("x", [SEQ, D], F32, kind="ExternalInput").ap(),
        w_qkv=nc.dram_tensor("w_qkv", [D, 3 * D], F32, kind="ExternalInput").ap(),
        w_proj=nc.dram_tensor("w_proj", [D, D], F32, kind="ExternalInput").ap(),
        b_proj=nc.dram_tensor("b_proj", [D], F32, kind="ExternalInput").ap(),
        gate_w=nc.dram_tensor("gate_w", [16, D], F32, kind="ExternalInput").ap(),
        w_gate=nc.dram_tensor("w_gate", [16, D, 512], F32,
                              kind="ExternalInput").ap(),
        w_up=nc.dram_tensor("w_up", [16, D, 512], F32, kind="ExternalInput").ap(),
        w_down=nc.dram_tensor("w_down", [16, 512, D], F32,
                              kind="ExternalInput").ap(),
        sw_gate=nc.dram_tensor("sw_gate", [D, 512], F32,
                               kind="ExternalInput").ap(),
        sw_up=nc.dram_tensor("sw_up", [D, 512], F32, kind="ExternalInput").ap(),
        sw_down=nc.dram_tensor("sw_down", [512, D], F32,
                               kind="ExternalInput").ap())


def build_l0(taps=(), stop_after="full"):
    nc = bass.Bass("TRN2", target_bir_lowering=False)
    ins = l0_dram_ins(nc)
    y1 = nc.dram_tensor("y1", [HALF, D], F32, kind="ExternalOutput").ap()
    p = _mk_prog(nc)
    tapmap = {}
    cm = Common(nc, p)
    emit_l0(nc, p, cm, tapmap, ins, y1, stop_after=stop_after)
    return _do_finish(nc, p, tapmap, taps)




def l1_dram_ins(nc, own_only=False):
    d = dict(
        Wq=nc.dram_tensor("Wq", [D, D], F32, kind="ExternalInput").ap())
    for nm, shape in (("bq", [D]), ("bk", [D]), ("bv", [D]),
                      ("b1", [512]), ("b2", [64]), ("tau", [1]),
                      ("bih", [3 * D]), ("bhh", [3 * D]), ("bq1", [256]),
                      ("bk1", [256]), ("bv1", [D]), ("ln_g", [D]),
                      ("ln_b", [D]), ("bvp", [D]), ("bo", [D])):
        d[nm] = nc.dram_tensor(nm, shape, F32, kind="ExternalInput").ap()
    for nm, shape in (("Wk", [D, D]), ("Wv", [D, D]), ("slot_init", [64, D]),
                      ("w1", [2 * D, 512]), ("w2", [512, 64]),
                      ("Wih", [3 * D, D]), ("Whh", [3 * D, D]),
                      ("wq1", [D, 256]), ("wk1", [D, 256]), ("wv1", [D, D]),
                      ("wvp", [D, D]), ("wo", [D, D])):
        d[nm] = nc.dram_tensor(nm, shape, F32, kind="ExternalInput").ap()
    return d


def emit_l1(nc, p, cm, tapmap, ins, x1, y, own_only=False, x1_rk=None,
            cc=None):
    (Wq, bq, Wk, bk, Wv, bv, slot_init, w1, b1, w2, b2, tau, Wih, Whh, bih,
     bhh, wq1, bq1, wk1, bk1, wv1, bv1, ln_g, ln_b, wvp, bvp, wo, bo) = (
        ins["Wq"], ins["bq"], ins["Wk"], ins["bk"], ins["Wv"], ins["bv"],
        ins["slot_init"], ins["w1"], ins["b1"], ins["w2"], ins["b2"],
        ins["tau"], ins["Wih"], ins["Whh"], ins["bih"], ins["bhh"],
        ins["wq1"], ins["bq1"], ins["wk1"], ins["bk1"], ins["wv1"],
        ins["bv1"], ins["ln_g"], ins["ln_b"], ins["wvp"], ins["bvp"],
        ins["wo"], ins["bo"])
    qx, qw = "qx", "qw"
    NLT = 16 if own_only else 32
    NWND = 4 if own_only else 8

    def palloc(nm, shape, dtype, key=None):
        return p.palloc(nc, nm, shape, dtype, key=key)

    bstg = [nc.alloc_sbuf_tensor(f"bstg{j}", [24, 128], F32) for j in range(2)]
    g_pBC1, pBC1 = palloc("pBC1", [128, 512], F32)
    _lc_n = [0]

    def load_cols(dst, src_1d, n_chunks, key):
        j = _lc_n[0] & 1
        _lc_n[0] += 1
        sk = (f"bstg{j}",)
        p.dma("qb", lambda e, src_1d=src_1d, n_chunks=n_chunks, j=j: e.dma_start(
            out=bstg[j][0:n_chunks, :],
            in_=src_1d.rearrange("(c p) -> c p", p=128)),
            writes=[sk])
        p.op("tensor", lambda e, n_chunks=n_chunks, j=j: e.matmul(
            pBC1[:, 0:n_chunks], bstg[j][0:n_chunks, :],
            cm.identf[0:n_chunks, 0:n_chunks], start=True, stop=True),
            reads=[sk, ("c_identf",)], writes=[("pBC1",)])
        p.op("vector", lambda e, dst=dst, n_chunks=n_chunks: e.tensor_copy(
            out=dst[:, 0:n_chunks], in_=pBC1[:, 0:n_chunks]),
            reads=[("pBC1",)], writes=[key])

    bq_s = nc.alloc_sbuf_tensor("bq_s", [128, 8], F32)
    bk_s = nc.alloc_sbuf_tensor("bk_s", [128, 8], F32)
    b1_s = nc.alloc_sbuf_tensor("b1_s", [128, 4], F32)
    bih_s = nc.alloc_sbuf_tensor("bih_s", [128, 24], F32)
    bhh_s = nc.alloc_sbuf_tensor("bhh_s", [128, 24], F32)
    bq1_s = nc.alloc_sbuf_tensor("bq1_s", [128, 2], F32)
    bk1_s = nc.alloc_sbuf_tensor("bk1_s", [128, 2], F32)
    bv1_s = nc.alloc_sbuf_tensor("bv1_s", [128, 8], F32)
    lng_s = nc.alloc_sbuf_tensor("lng_s", [128, 8], F32)
    lnb_s = nc.alloc_sbuf_tensor("lnb_s", [128, 8], F32)
    bvp_s = nc.alloc_sbuf_tensor("bvp_s", [128, 8], F32)
    b2r = nc.alloc_sbuf_tensor("b2r", [1, 64], F32)
    taur = nc.alloc_sbuf_tensor("taur", [1, 4], F32)
    bvr = nc.alloc_sbuf_tensor("bvr", [1, D], F32)
    bor = nc.alloc_sbuf_tensor("bor", [1, D], F32)
    s0f = nc.alloc_sbuf_tensor("s0f", [64, D], F32)
    load_cols(bq_s, bq, 8, ("bq_s",))
    load_cols(bk_s, bk, 8, ("bk_s",))
    load_cols(b1_s, b1, 4, ("b1_s",))
    load_cols(bih_s, bih, 24, ("bih_s",))
    load_cols(bhh_s, bhh, 24, ("bhh_s",))
    load_cols(bq1_s, bq1, 2, ("bq1_s",))
    load_cols(bk1_s, bk1, 2, ("bk1_s",))
    load_cols(bv1_s, bv1, 8, ("bv1_s",))
    load_cols(lng_s, ln_g, 8, ("lng_s",))
    load_cols(lnb_s, ln_b, 8, ("lnb_s",))
    load_cols(bvp_s, bvp, 8, ("bvp_s",))
    p.dma("qb", lambda e: e.dma_start(out=b2r[:],
                                    in_=b2.rearrange("(a b) -> a b", a=1)),
          writes=[("b2r",)])
    p.dma("qb", lambda e: e.dma_start(out=taur[:, 0:1],
                                    in_=tau.rearrange("(a b) -> a b", a=1)),
          writes=[("taur",)])
    p.dma("qb", lambda e: e.dma_start(out=bvr[:],
                                    in_=bv.rearrange("(a b) -> a b", a=1)),
          writes=[("bvr",)])
    p.dma("qb", lambda e: e.dma_start(out=bor[:],
                                    in_=bo.rearrange("(a b) -> a b", a=1)),
          writes=[("bor",)])
    p.dma("qb", lambda e: e.dma_start(out=s0f[:], in_=slot_init), writes=[("s0f",)])

    bvb = nc.alloc_sbuf_tensor("bvb", [128, D], F32)
    bob = nc.alloc_sbuf_tensor("bob", [128, D], F32)
    b2b = nc.alloc_sbuf_tensor("b2b", [128, 64], F32)
    sbc = nc.alloc_sbuf_tensor("sbc", [128, 1], F32)
    for half in range(2):
        p.op("tensor", lambda e, half=half: e.matmul(
            pBC1[:], cm.onesf[0:1, 0:128], bvr[:, half * 512:(half + 1) * 512],
            start=True, stop=True),
            reads=[("bvr",), ("c_onesf",)], writes=[("pBC1",)])
        p.op("vector", lambda e, half=half: e.tensor_copy(
            out=bvb[:, half * 512:(half + 1) * 512], in_=pBC1[:]),
            reads=[("pBC1",)], writes=[("bvb",)])
        p.op("tensor", lambda e, half=half: e.matmul(
            pBC1[:], cm.onesf[0:1, 0:128], bor[:, half * 512:(half + 1) * 512],
            start=True, stop=True),
            reads=[("bor",), ("c_onesf",)], writes=[("pBC1",)])
        p.op("vector", lambda e, half=half: e.tensor_copy(
            out=bob[:, half * 512:(half + 1) * 512], in_=pBC1[:]),
            reads=[("pBC1",)], writes=[("bob",)])
    p.op("tensor", lambda e: e.matmul(
        pBC1[:, 0:64], cm.onesf[0:1, 0:128], b2r[:], start=True, stop=True),
        reads=[("b2r",), ("c_onesf",)], writes=[("pBC1",)])
    p.op("vector", lambda e: e.tensor_copy(out=b2b[:], in_=pBC1[:, 0:64]),
         reads=[("pBC1",)], writes=[("b2b",)])
    p.op("scalar", lambda e: e.activation(taur[:, 1:2], taur[:, 0:1], AF.Abs),
         reads=[("taur",)], writes=[("taur",)], hazard=True)
    p.op("vector", lambda e: e.tensor_scalar_add(taur[:, 2:3], taur[:, 1:2], 0.1),
         reads=[("taur",)], writes=[("taur",)], hazard=True)
    p.op("vector", lambda e: e.reciprocal(taur[:, 3:4], taur[:, 2:3]),
         reads=[("taur",)], writes=[("taur",)], hazard=True)
    p.op("tensor", lambda e: e.matmul(
        pBC1[:, 0:1], cm.onesf[0:1, 0:128], taur[:, 3:4], start=True, stop=True),
        reads=[("taur",), ("c_onesf",)], writes=[("pBC1",)])
    p.op("vector", lambda e: e.tensor_copy(out=sbc[:], in_=pBC1[:, 0:1]),
         reads=[("pBC1",)], writes=[("sbc",)])
    g_pBC1.__exit__(None, None, None)
    p.barrier()

    if own_only:
        g_x2acc = nc.sbuf_tensor("x2acc", [128, 16 * D], F32, side="right")
        x2acc = g_x2acc.__enter__()
    alphaTm = nc.alloc_sbuf_tensor("alphaTm", [128, 32 * 64], BF16)
    alphaT = nc.alloc_sbuf_tensor("alphaT", [64, SEQ], BF16)
    g_xn1T_own = nc.sbuf_tensor("xn1T_own", [128, 8 * HALF], BF16, side="right")
    xn1T_own = g_xn1T_own.__enter__()
    if not own_only:
        g_xn1T_par = nc.sbuf_tensor("xn1T_par", [128, 8 * HALF], BF16,
                                    side="right")
        xn1T_par = g_xn1T_par.__enter__()
    else:
        g_xn1T_par = xn1T_par = None
    g_xn1_tm = nc.sbuf_tensor("xn1_tm", [128, NLT * D], BF16, side="right")
    xn1_tm = g_xn1_tm.__enter__()

    g_pT0, pT0 = palloc("pT0", [128, 128], BF16, key=("pT", 0))
    g_pT1, pT1 = palloc("pT1", [128, 128], BF16, key=("pT", 1))
    lng = ln_tiles(nc, p, cm, "ln1", x1, NLT, 16, xn1T_own, xn1T_par,
                   [pT0, pT1], qx, invn=None, out_tm=xn1_tm, src_rk=x1_rk)
    free_guards(lng)
    g_pT1.__exit__(None, None, None)
    g_pT0.__exit__(None, None, None)
    p.barrier()
    tapmap["xn1T_own"] = (xn1T_own, [("xno", i) for i in range(4)])
    tapmap["xn1_tm"] = (xn1_tm, [("xtm", i) for i in range(8)])

    def xn1_rhs(dc, t0, w):
        if t0 < HALF:
            return (xn1T_own[:, dc * HALF + t0:dc * HALF + t0 + w],
                    ("xno", t0 // 512))
        t = t0 - HALF
        return (xn1T_par[:, dc * HALF + t:dc * HALF + t + w], ("xnp", t // 512))


    mid1 = []

    def m1a(nm, shape, dtype):
        g = nc.sbuf_tensor(nm, shape, dtype, side="right")
        t = g.__enter__()
        mid1.append(g)
        return t

    w1t = m1a("w1t", [128, 4 * 512], BF16)
    w1b = m1a("w1b", [128, 4 * 512], BF16)
    w2s = m1a("w2s", [128, 4 * 64], BF16)
    p.dma(qw, lambda e: e.dma_start(
        out=w1t[:].rearrange("p (dc n) -> p dc n", dc=4)
        .rearrange("p dc n -> p (dc n)")
        .rearrange("p (dc n) -> p dc n", dc=4),
        in_=w1[0:D, :].rearrange("(dc p) n -> p dc n", p=128)[:, 0:4, :])
        if False else e.dma_start(
        out=w1t[:].rearrange("p (dc n) -> p dc n", dc=4),
        in_=w1[0:512, :].rearrange("(dc p) n -> p dc n", p=128))
        if False else e.dma_start(
        out=w1t[:].rearrange("p (dc n) -> p dc n", dc=4),
        in_=w1[0:D, :].rearrange("(dc p) n -> p dc n", p=128)[:, 0:4, :]),
        writes=[("w1t",)])
    p.dma(qw, lambda e: e.dma_start(
        out=w1b[:].rearrange("p (dc n) -> p dc n", dc=4),
        in_=w1[D:2 * D, :].rearrange("(dc p) n -> p dc n", p=128)[:, 0:4, :]),
        writes=[("w1b",)])
    p.dma(qw, lambda e: e.dma_start(
        out=w2s[:].rearrange("p (hc n) -> p hc n", hc=4),
        in_=w2.rearrange("(hc p) n -> p hc n", p=128)),
        writes=[("w2s",)])

    smv = m1a("smv", [128, 8], BF16)
    hbias = m1a("hbias", [128, 4], F32)
    s0b = nc.alloc_sbuf_tensor("s0b", [64, D], BF16)
    o64 = nc.alloc_sbuf_tensor("o64", [64, 1], BF16)
    g_pSM, pSM = palloc("pSM", [128, 2], F32)
    p.op("vector", lambda e: e.memset(o64[:], 1.0 / 64.0), writes=[("o64",)])
    p.op("vector", lambda e: e.tensor_copy(out=s0b[:], in_=s0f[:]),
         reads=[("s0f",)], writes=[("s0b",)])
    for c in range(8):
        p.op("tensor", lambda e, c=c: e.matmul(
            pSM[:, 0:1], s0b[0:64, c * 128:(c + 1) * 128], o64[:],
            start=True, stop=True),
            reads=[("s0b",), ("o64",)], writes=[("pSM",)])
        p.op("vector", lambda e, c=c: e.tensor_copy(out=smv[:, c:c + 1],
                                                    in_=pSM[:, 0:1]),
             reads=[("pSM",)], writes=[("smv",)])
    for jc in range(4):
        for dc in range(8):
            p.op("tensor", lambda e, jc=jc, dc=dc: e.matmul(
                pSM[:, 1:2],
                w1b[:, dc * 512 + jc * 128:dc * 512 + jc * 128 + 128]
                if dc < 4 else
                w1b[:, (dc - 4) * 512 + jc * 128:(dc - 4) * 512 + jc * 128 + 128],
                smv[:, dc:dc + 1], start=(dc == 0), stop=(dc == 7)),
                reads=[("w1b",), ("smv",)], writes=[("pSM",)])
        p.op("vector", lambda e, jc=jc: e.tensor_add(
            hbias[:, jc:jc + 1], b1_s[:, jc:jc + 1], pSM[:, 1:2]),
            reads=[("pSM",), ("b1_s",)], writes=[("hbias",)])

    hT = m1a("hT", [128, 4 * 512], BF16)
    logt = m1a("logt", [128, 64], F32)
    m8L = m1a("m8L", [128, 8], F32)
    maskL = m1a("maskL", [128, 64], F32)
    expL = m1a("expL", [128, 64], F32)
    nbt = m1a("nbt", [128, 2], F32)
    g_pH0, pH0 = palloc("pH0", [128, 512], F32)
    g_pH1, pH1 = palloc("pH1", [128, 512], F32)
    g_pL, pL = palloc("pL", [128, 64], F32)
    g_pAT, pAT = palloc("pAT", [64, 128], BF16)

    for wnd in range(NWND):
        t0 = wnd * 512
        for hc in range(4):
            ph, phk = (pH0, ("pH0",)) if hc % 2 == 0 else (pH1, ("pH1",))
            for dc in range(8):
                rhs, rk = xn1_rhs(dc, t0, 512)
                wcol = dc * 512 + hc * 128 if dc < 4 else (dc - 4) * 512 + hc * 128
                p.op("tensor", lambda e, hc=hc, dc=dc, rhs=rhs, ph=ph: e.matmul(
                    ph[:], w1t[:, (dc % 4) * 512 + hc * 128:(dc % 4) * 512 + hc * 128 + 128]
                    if False else w1t[:, hc * 512 + 0:hc * 512 + 512][:, 0:128]
                    if False else w1t[:, hc * 128 + 0:hc * 128 + 128],
                    rhs, start=(dc == 0), stop=(dc == 7)),
                    reads=[("w1t",), rk], writes=[phk])
            p.op("scalar", lambda e, hc=hc, ph=ph: e.activation(
                hT[:, hc * 512:(hc + 1) * 512], ph[:], AF.Gelu,
                bias=hbias[:, hc:hc + 1]),
                reads=[phk, ("hbias",)], writes=[("hT", hc)])
        for it in range(4):
            ti = wnd * 4 + it
            for hc in range(4):
                p.op("tensor", lambda e, hc=hc, it=it: e.matmul(
                    pL[:], hT[:, hc * 512 + it * 128:hc * 512 + it * 128 + 128],
                    w2s[:, hc * 64:(hc + 1) * 64],
                    start=(hc == 0), stop=(hc == 3)),
                    reads=[("hT", hc), ("w2s",)], writes=[("pL",)])
            p.op("vector", lambda e: e.tensor_add(logt[:], pL[:], b2b[:]),
                 reads=[("pL",), ("b2b",)], writes=[("logt",)])
            p.op("vector", lambda e: e.memset(m8L[:], -1e30), writes=[("m8L",)])
            p.op("vector", lambda e: e.max(out=m8L[:], in_=logt[:]),
                 reads=[("logt",), ("m8L",)], writes=[("m8L",)], hazard=True)
            p.op("vector", lambda e: e.tensor_scalar(
                maskL[:], logt[:], m8L[:, 7:8], None, ALU.is_ge),
                reads=[("logt",), ("m8L",)], writes=[("maskL",)])
            p.op("vector", lambda e: e.scalar_tensor_tensor(
                nbt[:, 0:1], m8L[:, 0:1], -1.0, sbc[:], ALU.mult, ALU.mult),
                reads=[("m8L",), ("sbc",)], writes=[("nbt",)])
            p.op("scalar", lambda e: e.activation(
                expL[:], logt[:], AF.Exp, bias=nbt[:, 0:1], scale=sbc[:]),
                reads=[("logt",), ("nbt",), ("sbc",)], writes=[("expL",)])
            p.op("vector", lambda e: e.tensor_mul(expL[:], expL[:], maskL[:]),
                 reads=[("expL",), ("maskL",)], writes=[("expL",)], hazard=True)
            p.op("vector", lambda e: e.reduce_sum(nbt[:, 1:2], expL[:], axis=X),
                 reads=[("expL",)], writes=[("nbt",)], hazard=True)
            p.op("vector", lambda e: e.reciprocal(nbt[:, 1:2], nbt[:, 1:2]),
                 reads=[("nbt",)], writes=[("nbt",)], hazard=True)
            p.op("vector", lambda e, ti=ti: e.tensor_scalar_mul(
                alphaTm[:, ti * 64:(ti + 1) * 64], expL[:], nbt[:, 1:2]),
                reads=[("expL",), ("nbt",)], writes=[("aTm", ti)])
            p.op("tensor", lambda e, ti=ti: e.transpose(
                pAT[:], alphaTm[:, ti * 64:(ti + 1) * 64], cm.identb[:]),
                reads=[("aTm", ti), ("c_identb",)], writes=[("pAT",)])
            p.op("vector", lambda e, ti=ti: e.tensor_copy(
                out=alphaT[0:64, ti * 128:(ti + 1) * 128], in_=pAT[:]),
                reads=[("pAT",)], writes=[("aT1", ti)])
    tapmap["alphaTm"] = (alphaTm, [("aTm", t) for t in range(NLT)])

    slotin = nc.alloc_sbuf_tensor("slotin", [64, D], F32)
    denom = nc.alloc_sbuf_tensor("denom", [64, 2], F32)
    ob1 = nc.alloc_sbuf_tensor("ob1", [128, 1], BF16)
    g_pSI0, pSI0 = palloc("pSI0", [64, 512], F32)
    g_pSI1, pSI1 = palloc("pSI1", [64, 512], F32)
    p.op("vector", lambda e: e.memset(ob1[:], 1.0), writes=[("ob1",)])
    NST = NLT
    for dh in range(2):
        ps, psk = (pSI0, ("pSI0",)) if dh == 0 else (pSI1, ("pSI1",))
        for ti in range(NST):
            p.op("tensor", lambda e, dh=dh, ti=ti, ps=ps: e.matmul(
                ps[:], alphaTm[:, ti * 64:(ti + 1) * 64],
                xn1_tm[:, ti * D + dh * 512:ti * D + dh * 512 + 512],
                start=(ti == 0), stop=(ti == NST - 1)),
                reads=[("aTm", ti), ("xtm", ti // 4)], writes=[psk])
        p.op("vector", lambda e, dh=dh, ps=ps: e.tensor_copy(
            out=slotin[:, dh * 512:(dh + 1) * 512], in_=ps[:]),
            reads=[psk], writes=[("slotin",)])
    for ti in range(NST):
        p.op("tensor", lambda e, ti=ti: e.matmul(
            pSM[:, 0:1][0:64, :], alphaTm[:, ti * 64:(ti + 1) * 64], ob1[:],
            start=(ti == 0), stop=(ti == NST - 1)),
            reads=[("aTm", ti), ("ob1",)], writes=[("pSM",)])
    if own_only:
        ccstg = nc.alloc_sbuf_tensor("ccstg", [64, 1032], F32)
        p.op("vector", lambda e: e.tensor_copy(out=ccstg[:, 0:D],
                                               in_=slotin[:]),
             reads=[("slotin",)], writes=[("ccstg",)])
        p.op("vector", lambda e: e.tensor_copy(out=ccstg[:, D:D + 1],
                                               in_=pSM[0:64, 0:1]),
             reads=[("pSM",)], writes=[("ccstg",)])
        p.dma(qx, lambda e: e.dma_start(out=cc["cc_in"], in_=ccstg[:, 0:1025]),
              reads=[("ccstg",)], writes=[("cc_in",)])
        p.dma(cc["queue"], lambda e: e.collective_compute(
            "AllReduce", ALU.add, cc["groups"], [cc["cc_in"].opt()],
            [cc["cc_out"].opt()]),
            reads=[("cc_in",)], writes=[("cc_out",)], inc=1)
    else:
        p.op("vector", lambda e: e.tensor_scalar_add(denom[:, 0:1],
                                                     pSM[0:64, 0:1], 1e-8),
             reads=[("pSM",)], writes=[("denom",)], hazard=True)
        p.op("vector", lambda e: e.reciprocal(denom[:, 1:2], denom[:, 0:1]),
             reads=[("denom",)], writes=[("denom",)], hazard=True)
        p.op("vector", lambda e: e.tensor_scalar_mul(slotin[:], slotin[:],
                                                     denom[:, 1:2]),
             reads=[("slotin",), ("denom",)], writes=[("slotin",)],
             hazard=True)
    tapmap["slotin"] = (slotin, [("slotin",)])

    for g in (g_pSI1, g_pSI0, g_pAT, g_pL, g_pH1, g_pH0, g_pSM):
        g.__exit__(None, None, None)
    for g in reversed(mid1):
        g.__exit__(None, None, None)
    g_xn1_tm.__exit__(None, None, None)
    if g_xn1T_par is not None:
        g_xn1T_par.__exit__(None, None, None)
    p.barrier()

    if not own_only:
        g_x2acc = nc.sbuf_tensor("x2acc", [128, 16 * D], F32, side="right")
        x2acc = g_x2acc.__enter__()
    g_wqa = nc.sbuf_tensor("wqa", [128, 8 * D], BF16, side="right")
    wqa = g_wqa.__enter__()
    g_wka = nc.sbuf_tensor("wka", [128, 8 * D], BF16, side="right")
    wka = g_wka.__enter__()
    g_wva = nc.sbuf_tensor("wva", [128, 8 * D], BF16, side="right")
    wva = g_wva.__enter__()
    for wt, wn, wsrc in ((wqa, "wqa", Wq), (wka, "wka", Wk), (wva, "wva", Wv)):
        p.dma(qw, lambda e, wt=wt, wsrc=wsrc: e.dma_start(
            out=wt[:].rearrange("p (dc n) -> p dc n", dc=8),
            in_=wsrc.rearrange("(dc p) n -> p dc n", p=128)),
            writes=[(wn,)])
    att_guards = []

    def ata(nm, shape, dtype):
        g = nc.sbuf_tensor(nm, shape, dtype, side="right")
        t = g.__enter__()
        att_guards.append(g)
        return t

    q1T = ata("q1T", [128, 8 * 256], BF16)
    k1T = ata("k1T", [128, 8 * 256], BF16)
    v1t = ata("v1t", [128, 2 * 16 * 65], BF16)
    sq1 = ata("sq1", [128, 256], BF16)
    nrm1 = ata("nrm1", [1, 8], F32)
    negM1 = ata("negM1", [128, 1], F32)
    ptb1 = ata("ptb1", [128, 64], BF16)
    ssum1 = ata("ssum1", [128, 2], F32)
    g_pQ1, pQ1 = palloc("pQ1", [128, 512], F32)
    g_pK1, pK1 = palloc("pK1", [128, 512], F32)
    g_pV1a, pV1a = palloc("pV1a", [128, 512], F32)
    g_pV1b, pV1b = palloc("pV1b", [128, 512], F32)
    g_pN1, pN1 = palloc("pN1", [1, 256], F32)
    g_pM1, pM1 = palloc("pM1", [128, 1], F32)
    g_pSL, pSL = palloc("pSL", [128, 64], F32)
    g_pOL, pOL = palloc("pOL", [128, 65], F32)
    v1_r = v1t[:].rearrange("p (i hh c) -> p i hh c", i=2, hh=16)
    p.op("vector", lambda e: e.memset(v1_r[:, :, :, 64:65], 1.0),
         writes=[("v1o",)])

    for g4 in range(8):
        t0 = g4 * 256
        for tl in range(2):
            ti = g4 * 2 + tl
            p.dma(qx, lambda e, ti=ti: e.dma_start(
                out=x2acc[:, ti * D:(ti + 1) * D],
                in_=x1[ti * 128:(ti + 1) * 128, :]),
                reads=(x1_rk(ti) if x1_rk else []), writes=[("x2d", ti)])
            p.op("vector", lambda e, ti=ti: e.tensor_add(
                x2acc[:, ti * D:(ti + 1) * D], x2acc[:, ti * D:(ti + 1) * D],
                bob[:]),
                reads=[("x2d", ti), ("bob",)], writes=[("x2", ti)])
        for w_, wk_, ps_, psk_, dst, dk, bias_s, bk_ in (
                (wqa, "wqa", pQ1, ("pQ1",), q1T, "q1T", bq_s, "bq_s"),
                (wka, "wka", pK1, ("pK1",), k1T, "k1T", bk_s, "bk_s")):
            for oc in range(8):
                for dc in range(8):
                    rhs, rk = xn1_rhs(dc, t0, 256)
                    p.op("tensor", lambda e, oc=oc, dc=dc, rhs=rhs, w_=w_,
                         wk_=wk_, ps_=ps_: e.matmul(
                        ps_[:, 0:256],
                        w_[:, dc * D + oc * 128:dc * D + oc * 128 + 128],
                        rhs, start=(dc == 0), stop=(dc == 7)),
                        reads=[(wk_,), rk], writes=[psk_])
                p.op("scalar", lambda e, oc=oc, ps_=ps_, dst=dst, dk=dk,
                     bias_s=bias_s, bk_=bk_: e.activation(
                    dst[:, oc * 256:(oc + 1) * 256], ps_[:, 0:256], AF.Identity,
                    bias=bias_s[:, oc:oc + 1]),
                    reads=[psk_, (bk_,)], writes=[(dk, oc)])
        for tl in range(2):
            ti = g4 * 2 + tl
            for half in range(2):
                pv, pvk = (pV1a, ("pV1a",)) if half == 0 else (pV1b, ("pV1b",))
                for dc in range(8):
                    lhs, rk = xn1_rhs(dc, ti * 128, 128)
                    p.op("tensor", lambda e, half=half, dc=dc, lhs=lhs, pv=pv:
                         e.matmul(
                        pv[:], lhs,
                        wva[:, dc * D + half * 512:dc * D + half * 512 + 512],
                        start=(dc == 0), stop=(dc == 7)),
                        reads=[("wva",), rk], writes=[pvk])
                p.op("vector", lambda e, tl=tl, half=half, pv=pv: e.tensor_add(
                    v1_r[:, tl, half * 8:(half + 1) * 8, 0:64],
                    pv[:].rearrange("p (hh c) -> p hh c", hh=8),
                    bvb[:, half * 512:(half + 1) * 512].rearrange(
                        "p (hh c) -> p hh c", hh=8)),
                    reads=[pvk, ("bvb",)], writes=[("v1", tl, half)])
        qk1 = [("q1T", oc) for oc in range(8)]
        kk1 = [("k1T", oc) for oc in range(8)]
        for w_, wk_, col in ((q1T, qk1, 0), (k1T, kk1, 1)):
            for oc in range(8):
                p.op("vector", lambda e, oc=oc, w_=w_, wk_=wk_: e.tensor_mul(
                    sq1[:], w_[:, oc * 256:(oc + 1) * 256],
                    w_[:, oc * 256:(oc + 1) * 256]),
                    reads=wk_, writes=[("sq1",)])
                p.op("tensor", lambda e, oc=oc: e.matmul(
                    pN1[:], cm.onesb[:, 0:1], sq1[:],
                    start=(oc == 0), stop=(oc == 7)),
                    reads=[("sq1",), ("c_onesb",)], writes=[("pN1",)])
            p.op("vector", lambda e, col=col: e.reduce_max(
                nrm1[:, col:col + 1], pN1[:], axis=X),
                reads=[("pN1",)], writes=[("nrm1",)], hazard=True)
        p.op("vector", lambda e: e.tensor_mul(nrm1[:, 2:3], nrm1[:, 0:1],
                                              nrm1[:, 1:2]),
             reads=[("nrm1",)], writes=[("nrm1",)], hazard=True)
        p.op("scalar", lambda e: e.activation(nrm1[:, 3:4], nrm1[:, 2:3],
                                              AF.Sqrt),
             reads=[("nrm1",)], writes=[("nrm1",)], hazard=True)
        p.op("vector", lambda e: e.tensor_scalar_mul(nrm1[:, 3:4],
                                                     nrm1[:, 3:4], -SCALE),
             reads=[("nrm1",)], writes=[("nrm1",)], hazard=True)
        p.op("tensor", lambda e: e.matmul(
            pM1[:], cm.onesf[0:1, 0:128], nrm1[:, 3:4], start=True, stop=True),
            reads=[("nrm1",), ("c_onesf",)], writes=[("pM1",)])
        p.op("vector", lambda e: e.tensor_copy(out=negM1[:], in_=pM1[:]),
             reads=[("pM1",)], writes=[("negM1",)])
        for tl in range(2):
            ti = g4 * 2 + tl
            for hh in range(16):
                oc = hh // 2
                rb = 64 * (hh % 2)
                c0 = oc * 256 + tl * 128
                for blk in range(2):
                    p.op("tensor", lambda e, oc=oc, rb=rb, blk=blk, c0=c0:
                         e.matmul(
                        pSL[blk * 64:(blk + 1) * 64, :],
                        k1T[rb:rb + 64, c0 + blk * 64:c0 + blk * 64 + 64],
                        q1T[rb:rb + 64, c0 + blk * 64:c0 + blk * 64 + 64],
                        start=True, stop=True, tile_position=(rb, blk * 64)),
                        reads=[("q1T", oc), ("k1T", oc)], writes=[("pSL",)])
                p.op("scalar", lambda e: e.activation(
                    ptb1[:], pSL[:], AF.Exp, bias=negM1[:, 0:1], scale=SCALE),
                    reads=[("pSL",), ("negM1",)], writes=[("ptb1",)])
                for blk in range(2):
                    p.op("tensor", lambda e, tl=tl, hh=hh, blk=blk: e.matmul(
                        pOL[blk * 64:(blk + 1) * 64, :],
                        ptb1[blk * 64:(blk + 1) * 64, :],
                        v1t[blk * 64:(blk + 1) * 64,
                            tl * 1040 + hh * 65:tl * 1040 + hh * 65 + 65],
                        start=True, stop=True, tile_position=(blk * 64, blk * 64)),
                        reads=[("ptb1",), ("v1", tl, hh // 8), ("v1o",)],
                        writes=[("pOL",)])
                p.op("scalar", lambda e: e.copy(out=ssum1[:, 0:1],
                                                in_=pOL[:, 64:65]),
                     reads=[("pOL",)], writes=[("ssum1",)], hazard=True)
                p.op("vector", lambda e: e.reciprocal(ssum1[:, 1:2],
                                                      ssum1[:, 0:1]),
                     reads=[("ssum1",)], writes=[("ssum1",)], hazard=True)
                p.op("vector", lambda e, ti=ti, hh=hh: e.scalar_tensor_tensor(
                    x2acc[:, ti * D + hh * 64:ti * D + hh * 64 + 64],
                    pOL[:, 0:64], ssum1[:, 1:2],
                    x2acc[:, ti * D + hh * 64:ti * D + hh * 64 + 64],
                    ALU.mult, ALU.add),
                    reads=[("pOL",), ("ssum1",), ("x2", ti)],
                    writes=[("x2", ti)], hazard=True)
    tapmap["x2acc"] = (x2acc, [("x2", i) for i in range(16)])

    for g in (g_pOL, g_pSL, g_pM1, g_pN1, g_pV1b, g_pV1a, g_pK1, g_pQ1):
        g.__exit__(None, None, None)
    for g in reversed(att_guards):
        g.__exit__(None, None, None)
    g_wva.__exit__(None, None, None)
    g_wka.__exit__(None, None, None)
    g_wqa.__exit__(None, None, None)
    if own_only:
        g_xn1T_own.__exit__(None, None, None)
    p.barrier()

    if own_only:
        p.dma(qx, lambda e: e.dma_start(out=ccstg[:, 0:1025], in_=cc["cc_out"]),
              reads=[("cc_out",)], writes=[("ccstg",)])
        p.op("vector", lambda e: e.tensor_copy(out=slotin[:],
                                               in_=ccstg[:, 0:D]),
             reads=[("ccstg",)], writes=[("slotin",)])
        p.op("vector", lambda e: e.tensor_scalar_add(
            denom[:, 0:1], ccstg[:, D:D + 1], 1e-8),
            reads=[("ccstg",)], writes=[("denom",)], hazard=True)
        p.op("vector", lambda e: e.reciprocal(denom[:, 1:2], denom[:, 0:1]),
             reads=[("denom",)], writes=[("denom",)], hazard=True)
        p.op("vector", lambda e: e.tensor_scalar_mul(slotin[:], slotin[:],
                                                     denom[:, 1:2]),
             reads=[("slotin",), ("denom",)], writes=[("slotin",)], hazard=True)
    mid2 = []

    def m2a(nm, shape, dtype):
        g = nc.sbuf_tensor(nm, shape, dtype, side="right")
        t = g.__enter__()
        mid2.append(g)
        return t

    siT = m2a("siT", [128, 8 * 64], BF16)
    s0T = m2a("s0T", [128, 8 * 64], BF16)
    sib = m2a("sib", [64, D], BF16)
    giT = m2a("giT", [128, 24 * 64], F32)
    ghT = m2a("ghT", [128, 24 * 64], F32)
    g_pTR, pTR = palloc("pTR", [128, 128], BF16)
    p.op("vector", lambda e: e.tensor_copy(out=sib[:], in_=slotin[:]),
         reads=[("slotin",)], writes=[("sib",)])
    for c in range(8):
        p.op("tensor", lambda e, c=c: e.transpose(
            pTR[:, 0:64], sib[0:64, c * 128:(c + 1) * 128],
            cm.identb[0:64, 0:64]),
            reads=[("sib",), ("c_identb",)], writes=[("pTR",)])
        p.op("vector", lambda e, c=c: e.tensor_copy(
            out=siT[:, c * 64:(c + 1) * 64], in_=pTR[:, 0:64]),
            reads=[("pTR",)], writes=[("siT",)])
        p.op("tensor", lambda e, c=c: e.transpose(
            pTR[:, 0:64], s0b[0:64, c * 128:(c + 1) * 128],
            cm.identb[0:64, 0:64]),
            reads=[("s0b",), ("c_identb",)], writes=[("pTR",)])
        p.op("vector", lambda e, c=c: e.tensor_copy(
            out=s0T[:, c * 64:(c + 1) * 64], in_=pTR[:, 0:64]),
            reads=[("pTR",)], writes=[("s0T",)])

    wblk = [m2a(f"wblk{j}", [128, D], BF16) for j in range(2)]
    wtT = [m2a(f"wtT{j}", [128, 128], BF16) for j in range(2)]
    g_pWT, pWT = palloc("pWT", [128, 128], BF16)
    g_pGI, pGI = palloc("pGI", [128, 64], F32)
    for wsrc, dstT, bias_s, bk_, dk in ((Wih, giT, bih_s, "bih_s", "giT"),
                                        (Whh, ghT, bhh_s, "bhh_s", "ghT")):
        rhsT = siT if dk == "giT" else s0T
        rhk = ("siT",) if dk == "giT" else ("s0T",)
        for jc in range(24):
            jb = jc % 2
            p.dma(qw, lambda e, jc=jc, jb=jb, wsrc=wsrc: e.dma_start(
                out=wblk[jb][:], in_=wsrc[jc * 128:(jc + 1) * 128, :]),
                writes=[("wblk", jb)], tag=f"{dk}_blk{jc}")
            for dc in range(8):
                db = dc % 2
                p.op("tensor", lambda e, jb=jb, dc=dc: e.transpose(
                    pWT[:], wblk[jb][:, dc * 128:(dc + 1) * 128], cm.identb[:]),
                    reads=[("wblk", jb), ("c_identb",)], writes=[("pWT",)])
                p.op("vector", lambda e, db=db: e.tensor_copy(
                    out=wtT[db][:], in_=pWT[:]),
                    reads=[("pWT",)], writes=[("wtTx", db)])
                p.op("tensor", lambda e, dc=dc, db=db, rhsT=rhsT, rhk=rhk:
                     e.matmul(
                    pGI[:], wtT[db][:], rhsT[:, dc * 64:(dc + 1) * 64],
                    start=(dc == 0), stop=(dc == 7)),
                    reads=[("wtTx", db), rhk], writes=[("pGI",)])
            p.op("vector", lambda e, jc=jc, dstT=dstT, bias_s=bias_s, bk_=bk_,
                 dk=dk: e.tensor_scalar_add(
                dstT[:, jc * 64:(jc + 1) * 64], pGI[:], bias_s[:, jc:jc + 1]),
                reads=[("pGI",), (bk_,)], writes=[(dk, jc)])

    rz = m2a("rz", [128, 16 * 64], F32)
    snT = m2a("snT", [128, 8 * 64], F32)
    snTb = m2a("snTb", [128, 8 * 64], BF16)
    tmp1 = m2a("tmp1", [128, 64], F32)
    for c in range(16):
        p.op("vector", lambda e, c=c: e.tensor_add(
            tmp1[:], giT[:, c * 64:(c + 1) * 64], ghT[:, c * 64:(c + 1) * 64]),
            reads=[("giT", c), ("ghT", c)], writes=[("tmp1",)], hazard=True)
        p.op("scalar", lambda e, c=c: e.activation(
            rz[:, c * 64:(c + 1) * 64], tmp1[:], AF.Sigmoid),
            reads=[("tmp1",)], writes=[("rz", c)])
    for c in range(8):
        p.op("vector", lambda e, c=c: e.tensor_mul(
            tmp1[:], rz[:, c * 64:(c + 1) * 64],
            ghT[:, (16 + c) * 64:(17 + c) * 64]),
            reads=[("rz", c), ("ghT", 16 + c)], writes=[("tmp1",)], hazard=True)
        p.op("vector", lambda e, c=c: e.tensor_add(
            tmp1[:], tmp1[:], giT[:, (16 + c) * 64:(17 + c) * 64]),
            reads=[("tmp1",), ("giT", 16 + c)], writes=[("tmp1",)], hazard=True)
        p.op("scalar", lambda e, c=c: e.activation(
            snT[:, c * 64:(c + 1) * 64], tmp1[:], AF.Tanh),
            reads=[("tmp1",)], writes=[("snT", c)], hazard=True)
        p.op("vector", lambda e, c=c: e.tensor_sub(
            tmp1[:], s0T[:, c * 64:(c + 1) * 64], snT[:, c * 64:(c + 1) * 64]),
            reads=[("s0T",), ("snT", c)], writes=[("tmp1",)], hazard=True)
        p.op("vector", lambda e, c=c: e.tensor_mul(
            tmp1[:], tmp1[:], rz[:, (8 + c) * 64:(9 + c) * 64]),
            reads=[("tmp1",), ("rz", 8 + c)], writes=[("tmp1",)], hazard=True)
        p.op("vector", lambda e, c=c: e.tensor_add(
            snT[:, c * 64:(c + 1) * 64], snT[:, c * 64:(c + 1) * 64], tmp1[:]),
            reads=[("tmp1",), ("snT", c)], writes=[("snT", c)], hazard=True)
        p.op("vector", lambda e, c=c: e.tensor_copy(
            out=snTb[:, c * 64:(c + 1) * 64], in_=snT[:, c * 64:(c + 1) * 64]),
            reads=[("snT", c)], writes=[("snTb", c)])
    tapmap["snT"] = (snT, [("snT", c) for c in range(8)])

    qsT = m2a("qsT", [128, 2 * 64], BF16)
    ksT = m2a("ksT", [128, 2 * 64], BF16)
    vsT = m2a("vsT", [128, 8 * 64], BF16)
    wqk = m2a("wqk", [128, 8 * 256], BF16)
    wvv = m2a("wvv", [128, 8 * 1024], BF16)
    snk = [("snTb", c) for c in range(8)]
    g_pQS, pQS = palloc("pQS", [128, 64], F32)
    for wsrc, dst, dstk, bias_s, bkk, ncol in ((wq1, qsT, "qsT", bq1_s, "bq1_s", 2),
                                               (wk1, ksT, "ksT", bk1_s, "bk1_s", 2)):
        p.dma(qw, lambda e, wsrc=wsrc: e.dma_start(
            out=wqk[:].rearrange("p (dc n) -> p dc n", dc=8),
            in_=wsrc.rearrange("(dc p) n -> p dc n", p=128)),
            reads=[("wqk",)], writes=[("wqk",)])
        for qc in range(ncol):
            for dc in range(8):
                p.op("tensor", lambda e, qc=qc, dc=dc: e.matmul(
                    pQS[:], wqk[:, dc * 256 + qc * 128:dc * 256 + qc * 128 + 128],
                    snTb[:, dc * 64:(dc + 1) * 64],
                    start=(dc == 0), stop=(dc == 7)),
                    reads=[("wqk",)] + snk, writes=[("pQS",)])
            p.op("vector", lambda e, qc=qc, dst=dst, dstk=dstk, bias_s=bias_s,
                 bkk=bkk: e.tensor_scalar_add(
                dst[:, qc * 64:(qc + 1) * 64], pQS[:], bias_s[:, qc:qc + 1]),
                reads=[("pQS",), (bkk,)], writes=[(dstk,)])
    p.dma(qw, lambda e: e.dma_start(
        out=wvv[:].rearrange("p (dc n) -> p dc n", dc=8),
        in_=wv1.rearrange("(dc p) n -> p dc n", p=128)),
        reads=[("wvv",)], writes=[("wvv",)])
    for vc in range(8):
        for dc in range(8):
            p.op("tensor", lambda e, vc=vc, dc=dc: e.matmul(
                pQS[:], wvv[:, dc * 1024 + vc * 128:dc * 1024 + vc * 128 + 128],
                snTb[:, dc * 64:(dc + 1) * 64], start=(dc == 0), stop=(dc == 7)),
                reads=[("wvv",)] + snk, writes=[("pQS",)])
        p.op("vector", lambda e, vc=vc: e.tensor_scalar_add(
            vsT[:, vc * 64:(vc + 1) * 64], pQS[:], bv1_s[:, vc:vc + 1]),
            reads=[("pQS",), ("bv1_s",)], writes=[("vsT",)])

    satp = m2a("satp", [64, 64], BF16)
    satt = m2a("satt", [64, 64], BF16)
    sats = m2a("sats", [64, 4], F32)
    g_pSA, pSA = palloc("pSA", [64, 64], F32)
    for qc in range(2):
        p.op("tensor", lambda e, qc=qc: e.matmul(
            pSA[:], qsT[:, qc * 64:(qc + 1) * 64], ksT[:, qc * 64:(qc + 1) * 64],
            start=(qc == 0), stop=(qc == 1)),
            reads=[("qsT",), ("ksT",)], writes=[("pSA",)])
    p.op("vector", lambda e: e.reduce_max(sats[:, 0:1], pSA[:], axis=X,
                                          negate=True),
         reads=[("pSA",)], writes=[("sats",)], hazard=True)
    p.op("vector", lambda e: e.tensor_scalar_mul(sats[:, 1:2], sats[:, 0:1],
                                                 1.0 / 16.0),
         reads=[("sats",)], writes=[("sats",)], hazard=True)
    p.op("scalar", lambda e: e.activation(satp[:], pSA[:], AF.Exp,
                                          bias=sats[:, 1:2], scale=1.0 / 16.0),
         reads=[("pSA",), ("sats",)], writes=[("satp",)], hazard=True)
    p.op("vector", lambda e: e.reduce_sum(sats[:, 2:3], satp[:], axis=X),
         reads=[("satp",)], writes=[("sats",)], hazard=True)
    p.op("vector", lambda e: e.reciprocal(sats[:, 3:4], sats[:, 2:3]),
         reads=[("sats",)], writes=[("sats",)], hazard=True)
    p.op("vector", lambda e: e.tensor_scalar_mul(satp[:], satp[:],
                                                 sats[:, 3:4]),
         reads=[("satp",), ("sats",)], writes=[("satp",)], hazard=True)
    p.op("tensor", lambda e: e.transpose(pTR[0:64, 0:64], satp[:],
                                         cm.identb[0:64, 0:64]),
         reads=[("satp",), ("c_identb",)], writes=[("pTR",)])
    p.op("vector", lambda e: e.tensor_copy(out=satt[:], in_=pTR[0:64, 0:64]),
         reads=[("pTR",)], writes=[("satt",)])
    vssm = m2a("vssm", [64, 8 * 128], BF16)
    for c in range(8):
        p.op("tensor", lambda e, c=c: e.transpose(
            pTR[0:64, :], vsT[:, c * 64:(c + 1) * 64], cm.identb[:]),
            reads=[("vsT",), ("c_identb",)], writes=[("pTR",)])
        p.op("vector", lambda e, c=c: e.tensor_copy(
            out=vssm[:, c * 128:(c + 1) * 128], in_=pTR[0:64, 0:128]),
            reads=[("pTR",)], writes=[("vssm",)])

    yT = m2a("yT", [128, 8 * 64], F32)
    yTb = m2a("yTb", [128, 8 * 64], BF16)
    ysq = m2a("ysq", [128, 8 * 64], BF16)
    g_pSC, pSC = palloc("pSC", [128, 64], F32)
    for c in range(8):
        p.op("tensor", lambda e, c=c: e.matmul(
            pSC[:], vssm[:, c * 128:(c + 1) * 128], satt[:],
            start=True, stop=True),
            reads=[("vssm",), ("satt",)], writes=[("pSC",)])
        p.op("vector", lambda e, c=c: e.tensor_add(
            yT[:, c * 64:(c + 1) * 64], snT[:, c * 64:(c + 1) * 64], pSC[:]),
            reads=[("pSC",), ("snT", c)], writes=[("yT", c)])
        p.op("vector", lambda e, c=c: e.tensor_copy(
            out=yTb[:, c * 64:(c + 1) * 64], in_=yT[:, c * 64:(c + 1) * 64]),
            reads=[("yT", c)], writes=[("yTb", c)])
        p.op("scalar", lambda e, c=c: e.activation(
            ysq[:, c * 64:(c + 1) * 64], yT[:, c * 64:(c + 1) * 64], AF.Square),
            reads=[("yT", c)], writes=[("ysq", c)])
    lnstat = m2a("lnstat", [1, 8 * 64], F32)
    g_pLN, pLN = palloc("pLN", [1, 64], F32)
    ykeys = [("yTb", c) for c in range(8)]
    sqkeys = [("ysq", c) for c in range(8)]
    for which, srck, col0 in ((yTb, ykeys, 0), (ysq, sqkeys, 64)):
        for c in range(8):
            p.op("tensor", lambda e, c=c, which=which: e.matmul(
                pLN[:], cm.onesb[:, 0:1], which[:, c * 64:(c + 1) * 64],
                start=(c == 0), stop=(c == 7)),
                reads=srck + [("c_onesb",)], writes=[("pLN",)])
        p.op("vector", lambda e, col0=col0: e.tensor_copy(
            out=lnstat[:, col0:col0 + 64], in_=pLN[:]),
            reads=[("pLN",)], writes=[("lnstat",)], hazard=True)
    p.op("vector", lambda e: e.tensor_scalar_mul(
        lnstat[:, 128:192], lnstat[:, 0:64], 1.0 / D),
        reads=[("lnstat",)], writes=[("lnstat",)], hazard=True)
    p.op("vector", lambda e: e.scalar_tensor_tensor(
        lnstat[:, 192:256], lnstat[:, 128:192], 1.0, lnstat[:, 128:192],
        ALU.mult, ALU.mult),
        reads=[("lnstat",)], writes=[("lnstat",)], hazard=True)
    p.op("vector", lambda e: e.scalar_tensor_tensor(
        lnstat[:, 256:320], lnstat[:, 64:128], 1.0 / D, lnstat[:, 192:256],
        ALU.mult, ALU.subtract),
        reads=[("lnstat",)], writes=[("lnstat",)], hazard=True)
    p.op("vector", lambda e: e.tensor_scalar_add(
        lnstat[:, 256:320], lnstat[:, 256:320], 1e-5),
        reads=[("lnstat",)], writes=[("lnstat",)], hazard=True)
    p.op("scalar", lambda e: e.activation(
        lnstat[:, 320:384], lnstat[:, 256:320], AF.Sqrt),
        reads=[("lnstat",)], writes=[("lnstat",)], hazard=True)
    p.op("vector", lambda e: e.reciprocal(
        lnstat[:, 320:384], lnstat[:, 320:384]),
        reads=[("lnstat",)], writes=[("lnstat",)], hazard=True)
    p.op("vector", lambda e: e.scalar_tensor_tensor(
        lnstat[:, 384:448], lnstat[:, 128:192], -1.0, lnstat[:, 320:384],
        ALU.mult, ALU.mult),
        reads=[("lnstat",)], writes=[("lnstat",)], hazard=True)
    abb = m2a("abb", [128, 128], F32)
    g_pAB, pAB = palloc("pAB", [128, 128], F32)
    p.op("tensor", lambda e: e.matmul(
        pAB[:, 0:64], cm.onesf[0:1, 0:128], lnstat[:, 320:384],
        start=True, stop=True),
        reads=[("lnstat",), ("c_onesf",)], writes=[("pAB",)])
    p.op("tensor", lambda e: e.matmul(
        pAB[:, 64:128], cm.onesf[0:1, 0:128], lnstat[:, 384:448],
        start=True, stop=True),
        reads=[("lnstat",), ("c_onesf",)], writes=[("pAB",)])
    p.op("vector", lambda e: e.tensor_copy(out=abb[:], in_=pAB[:]),
         reads=[("pAB",)], writes=[("abb",)])
    ynTb = m2a("ynTb", [128, 8 * 64], BF16)
    for c in range(8):
        p.op("vector", lambda e, c=c: e.tensor_mul(
            tmp1[:], yT[:, c * 64:(c + 1) * 64], abb[:, 0:64]),
            reads=[("yT", c), ("abb",)], writes=[("tmp1",)], hazard=True)
        p.op("vector", lambda e, c=c: e.tensor_add(tmp1[:], tmp1[:],
                                                   abb[:, 64:128]),
             reads=[("tmp1",), ("abb",)], writes=[("tmp1",)], hazard=True)
        p.op("scalar", lambda e, c=c: e.activation(
            ynTb[:, c * 64:(c + 1) * 64], tmp1[:], AF.Identity,
            bias=lnb_s[:, c:c + 1], scale=lng_s[:, c:c + 1]),
            reads=[("tmp1",), ("lnb_s",), ("lng_s",)], writes=[("ynTb", c)])
    svT = m2a("svT", [128, 8 * 64], BF16)
    ynk = [("ynTb", c) for c in range(8)]
    p.dma(qw, lambda e: e.dma_start(
        out=wvv[:].rearrange("p (dc n) -> p dc n", dc=8),
        in_=wvp.rearrange("(dc p) n -> p dc n", p=128)),
        reads=[("wvv",)], writes=[("wvv",)])
    for vc in range(8):
        for dc in range(8):
            p.op("tensor", lambda e, vc=vc, dc=dc: e.matmul(
                pQS[:], wvv[:, dc * 1024 + vc * 128:dc * 1024 + vc * 128 + 128],
                ynTb[:, dc * 64:(dc + 1) * 64], start=(dc == 0), stop=(dc == 7)),
                reads=[("wvv",)] + ynk, writes=[("pQS",)])
        p.op("vector", lambda e, vc=vc: e.tensor_scalar_add(
            svT[:, vc * 64:(vc + 1) * 64], pQS[:], bvp_s[:, vc:vc + 1]),
            reads=[("pQS",), ("bvp_s",)], writes=[("svT",)])
    svsm = nc.alloc_sbuf_tensor("svsm", [64, 8 * 128], BF16)
    for c in range(8):
        p.op("tensor", lambda e, c=c: e.transpose(
            pTR[0:64, :], svT[:, c * 64:(c + 1) * 64], cm.identb[:]),
            reads=[("svT",), ("c_identb",)], writes=[("pTR",)])
        p.op("vector", lambda e, c=c: e.tensor_copy(
            out=svsm[:, c * 128:(c + 1) * 128], in_=pTR[0:64, 0:128]),
            reads=[("pTR",)], writes=[("svsm",)])
    tapmap["svsm"] = (svsm, [("svsm",)])

    for g in (g_pAB, g_pLN, g_pSC, g_pSA, g_pQS, g_pGI, g_pWT, g_pTR):
        g.__exit__(None, None, None)
    for g in reversed(mid2):
        g.__exit__(None, None, None)
    p.barrier()

    g_outtokT = nc.sbuf_tensor("outtokT", [128, 8 * HALF], BF16, side="right")
    outtokT = g_outtokT.__enter__()
    g_woa = nc.sbuf_tensor("woa", [128, 8 * D], BF16, side="right")
    woa = g_woa.__enter__()
    p.dma(qw, lambda e: e.dma_start(
        out=woa[:].rearrange("p (dc n) -> p dc n", dc=8),
        in_=wo.rearrange("(dc p) n -> p dc n", p=128)),
        writes=[("woa",)])
    g_pOT0, pOT0 = palloc("pOT0", [128, 512], F32)
    g_pOT1, pOT1 = palloc("pOT1", [128, 512], F32)
    for wnd in range(4):
        for dc in range(8):
            ps, psk = (pOT0, ("pOT0",)) if dc % 2 == 0 else (pOT1, ("pOT1",))
            p.op("tensor", lambda e, wnd=wnd, dc=dc, ps=ps: e.matmul(
                ps[:], svsm[:, dc * 128:(dc + 1) * 128],
                alphaT[0:64, wnd * 512:(wnd + 1) * 512],
                start=True, stop=True),
                reads=[("svsm",)] + [("aT1", t) for t in range(wnd * 4, wnd * 4 + 4)],
                writes=[psk])
            p.op("vector", lambda e, wnd=wnd, dc=dc, ps=ps: e.tensor_copy(
                out=outtokT[:, dc * HALF + wnd * 512:dc * HALF + wnd * 512 + 512],
                in_=ps[:]),
                reads=[psk], writes=[("otT", dc, wnd)])
    for ti in range(16):
        for half in range(2):
            ps, psk = (pOT0, ("pOT0",)) if half == 0 else (pOT1, ("pOT1",))
            for dc in range(8):
                p.op("tensor", lambda e, ti=ti, half=half, dc=dc, ps=ps:
                     e.matmul(
                    ps[:, 0:128] if False else ps[:],
                    outtokT[:, dc * HALF + ti * 128:dc * HALF + ti * 128 + 128],
                    woa[:, dc * D + half * 512:dc * D + half * 512 + 512],
                    start=(dc == 0), stop=(dc == 7)),
                    reads=[("otT", dc, ti // 4), ("woa",)], writes=[psk])
            p.op("vector", lambda e, ti=ti, half=half, ps=ps: e.tensor_add(
                x2acc[:, ti * D + half * 512:ti * D + half * 512 + 512],
                x2acc[:, ti * D + half * 512:ti * D + half * 512 + 512],
                ps[:]),
                reads=[psk, ("x2", ti)], writes=[("x2", ti)], hazard=True)
        p.dma(qx, lambda e, ti=ti: e.dma_start(
            out=y[ti * 128:(ti + 1) * 128, :], in_=x2acc[:, ti * D:(ti + 1) * D]),
            reads=[("x2", ti)], writes=[("yout", ti)])

    g_pOT1.__exit__(None, None, None)
    g_pOT0.__exit__(None, None, None)
    g_woa.__exit__(None, None, None)
    g_outtokT.__exit__(None, None, None)
    g_x2acc.__exit__(None, None, None)
    return


def build_l1(taps=()):
    nc = bass.Bass("TRN2", target_bir_lowering=False)
    x1 = nc.dram_tensor("x1", [SEQ, D], F32, kind="ExternalInput").ap()
    ins = l1_dram_ins(nc)
    y = nc.dram_tensor("y", [HALF, D], F32, kind="ExternalOutput").ap()
    p = _mk_prog(nc)
    tapmap = {}
    cm = Common(nc, p)
    emit_l1(nc, p, cm, tapmap, ins, x1, y, own_only=False)
    return _do_finish(nc, p, tapmap, taps)


def build_fused(taps=()):
    nc = bass.Bass("TRN2", target_bir_lowering=False)
    ins0 = l0_dram_ins(nc)
    ins1 = l1_dram_ins(nc)
    x1s = nc.dram_tensor("x1s", [HALF, D], F32).ap()
    cc_in = nc.dram_tensor("cc_in", [64, 1025], F32).ap()
    cc_out = nc.dram_tensor("cc_out", [64, 1025], F32).ap()
    y = nc.dram_tensor("y", [HALF, D], F32, kind="ExternalOutput").ap()
    p = _mk_prog(nc)
    p.dma_queue("qc", "gpsimd")
    tapmap = {}
    cm = Common(nc, p)
    emit_l0(nc, p, cm, tapmap, ins0, x1s)
    nc1 = NPfx(nc, "z_")
    p1 = KeyPrefix(p, "z_", exempt=("y1out", "c_identf", "c_identb",
                                    "c_onesf", "c_onesb", "c_hmask"))
    cc = dict(cc_in=cc_in, cc_out=cc_out, queue="qc",
              groups=[[0, 1], [2, 3], [4, 5], [6, 7]])
    emit_l1(nc1, p1, cm, tapmap, ins1, x1s, y, own_only=True,
            x1_rk=lambda i: [("y1out", i)], cc=cc)
    return _do_finish(nc, p, tapmap, taps)



_CACHE = {}


def _rot(xseq, h):
    if h == 0:
        return np.ascontiguousarray(xseq)
    return np.ascontiguousarray(
        np.concatenate([xseq[HALF:], xseq[:HALF]], axis=0))


def kernel(**inputs):
    if "fused" not in _CACHE:
        _CACHE["fused"] = build_fused()[0]
    nc = _CACHE["fused"]
    x = np.asarray(inputs["L0_attn_x"], dtype=np.float32)
    w = dict(
        w_qkv=inputs["L0_attn_w_qkv"], w_proj=inputs["L0_attn_w_proj"],
        b_proj=inputs["L0_attn_b_proj"], gate_w=inputs["L0_ffn_gate_w"],
        w_gate=inputs["L0_ffn_w_gate"], w_up=inputs["L0_ffn_w_up"],
        w_down=inputs["L0_ffn_w_down"], sw_gate=inputs["L0_ffn_sw_gate"],
        sw_up=inputs["L0_ffn_sw_up"], sw_down=inputs["L0_ffn_sw_down"],
        Wq=inputs["L1_attn_Wq"], bq=inputs["L1_attn_bq"],
        Wk=inputs["L1_attn_Wk"], bk=inputs["L1_attn_bk"],
        Wv=inputs["L1_attn_Wv"], bv=inputs["L1_attn_bv"],
        slot_init=inputs["L1_ffn_slot_init"], w1=inputs["L1_ffn_w1"],
        b1=inputs["L1_ffn_b1"], w2=inputs["L1_ffn_w2"], b2=inputs["L1_ffn_b2"],
        tau=inputs["L1_ffn_tau"], Wih=inputs["L1_ffn_Wih"],
        Whh=inputs["L1_ffn_Whh"], bih=inputs["L1_ffn_bih"],
        bhh=inputs["L1_ffn_bhh"], wq1=inputs["L1_ffn_wq"],
        bq1=inputs["L1_ffn_bq"], wk1=inputs["L1_ffn_wk"],
        bk1=inputs["L1_ffn_bk"], wv1=inputs["L1_ffn_wv"],
        bv1=inputs["L1_ffn_bv"], ln_g=inputs["L1_ffn_ln_g"],
        ln_b=inputs["L1_ffn_ln_b"], wvp=inputs["L1_ffn_wvp"],
        bvp=inputs["L1_ffn_bvp"], wo=inputs["L1_ffn_wo"],
        bo=inputs["L1_ffn_bo"])
    w = {k: np.ascontiguousarray(np.asarray(v, dtype=np.float32))
         for k, v in w.items()}
    in_maps = []
    for core in range(8):
        b, h = core // 2, core % 2
        im = {"x": _rot(x[b], h)}
        im.update(w)
        in_maps.append(im)
    res = run_bass_kernel_spmd(nc, in_maps, core_ids=list(range(8)))
    y = np.empty((4, SEQ, D), dtype=np.float32)
    for core in range(8):
        b, h = core // 2, core % 2
        y[b, h * HALF:(h + 1) * HALF] = res.results[core]["y"]
    return y


# version 12
# speedup vs baseline: 1.0176x; 1.0121x over previous
import numpy as np
import concourse.bass as bass
import concourse.mybir as mybir
from concourse.bass_utils import run_bass_kernel_spmd

F32 = mybir.dt.float32
BF16 = mybir.dt.bfloat16
AF = mybir.ActivationFunctionType
ALU = mybir.AluOpType
X = mybir.AxisListType.X

D = 1024
SEQ = 4096
HALF = 2048
NH = 16
HD = 64
SCALE = HD ** -0.5


ENGINE_NAMES = ("tensor", "vector", "scalar", "gpsimd", "sync")
CHAIN_ALL_QUEUES = False


class _Op:
    __slots__ = ("idx", "engine", "emit", "reads", "writes", "queue", "tag",
                 "deps", "need_inc", "inc_val", "waits", "hazard", "inc_amt")

    def __init__(self, idx, engine, emit, reads, writes, queue, tag, hazard=False):
        self.idx = idx
        self.engine = engine
        self.emit = emit
        self.reads = reads
        self.writes = writes
        self.queue = queue
        self.tag = tag
        self.hazard = hazard
        self.deps = []
        self.need_inc = False
        self.inc_val = None
        self.waits = []


class _Queue:
    __slots__ = ("name", "engine", "count")

    def __init__(self, name, engine):
        self.name = name
        self.engine = engine
        self.count = 0


class Prog:
    def __init__(self, nc):
        self.nc = nc
        self.ops = []
        self.queues = {}
        self._last_write = {}
        self._readers = {}
        self._barrier = []
        self._last_dma = {}
        self.psum_prev = {}
        self.pg_scr = None
        self.qsplit = {}
        self._rr = {}
        self.chain = set()

    def palloc(self, nc, nm, shape, dtype, key=None):
        b0 = nc.psum_base
        g = nc.psum_tensor(nm, shape, dtype)
        t = g.__enter__()
        b1 = nc.psum_base
        key = key if key is not None else (nm,)
        prev = set()
        for b in range(b0, b1):
            pk = self.psum_prev.get(b)
            if pk is not None and pk != key:
                prev.add(pk)
            self.psum_prev[b] = key
        if prev and self.pg_scr is not None:
            scr = self.pg_scr
            self.op("vector", lambda e: e.memset(scr[0:1, 0:2], 0.0),
                    writes=sorted(prev) + [key], tag=f"pg_{nm}")
        return g, t

    def barrier(self):
        last = {}
        for o in self.ops:
            key = ("q", o.queue) if o.queue is not None else ("e", o.engine)
            last[key] = o.idx
        self._barrier = sorted(last.values())

    def dma_queue(self, name, engine):
        q = _Queue(name, engine)
        self.queues[name] = q
        return name

    def op(self, engine, emit, reads=(), writes=(), tag="", hazard=False):
        o = _Op(len(self.ops), engine, emit, tuple(reads), tuple(writes), None,
                tag, hazard=hazard)
        self.ops.append(o)
        self._track(o)
        return o

    def dma(self, queue, emit, reads=(), writes=(), tag="", inc=16):
        if queue in self.qsplit:
            lst = self.qsplit[queue]
            i = self._rr.get(queue, 0)
            self._rr[queue] = (i + 1) % len(lst)
            queue = lst[i]
        q = self.queues[queue]
        o = _Op(len(self.ops), q.engine, emit, tuple(reads), tuple(writes),
                queue, tag)
        o.inc_amt = inc
        self.ops.append(o)
        self._track(o)
        if q.engine == "gpsimd" or queue in self.chain or CHAIN_ALL_QUEUES:
            prev = self._last_dma.get(queue)
            if prev is not None and prev not in o.deps:
                o.deps.append(prev)
                self.ops[prev].need_inc = True
        self._last_dma[queue] = o.idx
        return o

    def _track(self, o):
        deps = set()
        for r in o.reads:
            w = self._last_write.get(r)
            if w is not None:
                deps.add(w)
            self._readers.setdefault(r, []).append(o.idx)
        for w in o.writes:
            pw = self._last_write.get(w)
            if pw is not None:
                deps.add(pw)
            for rd in self._readers.get(w, ()):
                deps.add(rd)
            self._last_write[w] = o.idx
            self._readers[w] = []
        deps.discard(o.idx)
        deps.update(self._barrier)
        deps.discard(o.idx)
        real = []
        for d in sorted(deps):
            pr = self.ops[d]
            if (pr.queue is None and pr.engine == o.engine
                    and pr.engine == "tensor" and not pr.hazard):
                continue
            real.append(d)
        o.deps = real
        for d in real:
            self.ops[d].need_inc = True

    def emit(self, final_engine="sync"):
        nc = self.nc
        eng_count = {e: 0 for e in ENGINE_NAMES}
        for o in self.ops:
            if o.queue is not None:
                q = self.queues[o.queue]
                q.count += getattr(o, "inc_amt", 16)
                o.inc_val = q.count
                o.need_inc = True
            elif o.need_inc:
                eng_count[o.engine] += 1
                o.inc_val = eng_count[o.engine]
        hw = {}
        for o in self.ops:
            for d in o.deps:
                pr = self.ops[d]
                if pr.queue is not None:
                    sem_key = ("q", pr.queue)
                else:
                    sem_key = ("e", pr.engine)
                    if pr.inc_val is None:
                        continue
                k = (o.engine, sem_key)
                if hw.get(k, 0) < pr.inc_val:
                    hw[k] = pr.inc_val
                    o.waits.append((sem_key, pr.inc_val))
        sems = {}
        for e in ENGINE_NAMES:
            if eng_count[e] > 0:
                sems[("e", e)] = nc.alloc_semaphore(f"sem_{e}")
        for qn, q in self.queues.items():
            if q.count > 0:
                sems[("q", qn)] = nc.alloc_semaphore(f"sem_q_{qn}")

        per_engine = {e: [] for e in ENGINE_NAMES}
        for o in self.ops:
            per_engine[o.engine].append(o)

        final_waits = []
        for qn, q in self.queues.items():
            if q.count > 0:
                final_waits.append((("q", qn), q.count))
        for e in ENGINE_NAMES:
            if eng_count[e] > 0 and e != final_engine:
                final_waits.append((("e", e), eng_count[e]))

        with nc.Block() as block:
            for e in ENGINE_NAMES:
                ops = per_engine[e]
                if not ops and e != final_engine:
                    continue

                def make_body(eng_name, eng_ops):
                    def body(eng):
                        for o in eng_ops:
                            for sem_key, val in o.waits:
                                eng.wait_ge(sems[sem_key], val)
                            inst = o.emit(eng)
                            if o.queue is not None:
                                inst.then_inc(sems[("q", o.queue)],
                                              getattr(o, "inc_amt", 16))
                            elif o.inc_val is not None:
                                inst.then_inc(sems[("e", eng_name)], 1)
                        if eng_name == final_engine:
                            for sem_key, val in final_waits:
                                eng.wait_ge(sems[sem_key], val)
                    return body

                getattr(block, e)(make_body(e, ops))

    def stats(self):
        n_inc = sum(1 for o in self.ops if o.inc_val is not None)
        n_wait = sum(len(o.waits) for o in self.ops)
        per_eng = {}
        for o in self.ops:
            per_eng[o.engine] = per_eng.get(o.engine, 0) + 1
        return dict(n_ops=len(self.ops), n_inc=n_inc, n_wait=n_wait,
                    per_engine=per_eng)


class KeyPrefix:

    def __init__(self, p, pfx, exempt=()):
        self.p = p
        self.pfx = pfx
        self.exempt = set(exempt)

    def _k(self, keys):
        out = []
        for k in keys:
            if k[0] in self.exempt:
                out.append(k)
            else:
                out.append((self.pfx + k[0],) + tuple(k[1:]))
        return out

    def op(self, engine, emit, reads=(), writes=(), **kw):
        return self.p.op(engine, emit, self._k(reads), self._k(writes), **kw)

    def dma(self, queue, emit, reads=(), writes=(), **kw):
        return self.p.dma(queue, emit, self._k(reads), self._k(writes), **kw)

    def palloc(self, nc, nm, shape, dtype, key=None):
        key = self._k([key if key is not None else (nm,)])[0]
        return self.p.palloc(nc, nm, shape, dtype, key=key)

    def barrier(self):
        self.p.barrier()

    @property
    def queues(self):
        return self.p.queues


class NPfx:

    def __init__(self, nc, pfx):
        self._nc = nc
        self._pfx = pfx

    def alloc_sbuf_tensor(self, name, shape, dtype, **kw):
        return self._nc.alloc_sbuf_tensor(self._pfx + name, shape, dtype, **kw)

    def sbuf_tensor(self, name, shape, dtype, **kw):
        return self._nc.sbuf_tensor(self._pfx + name, shape, dtype, **kw)

    def psum_tensor(self, name, shape, dtype, **kw):
        return self._nc.psum_tensor(self._pfx + name, shape, dtype, **kw)

    def dram_tensor(self, name, *a, **kw):
        return self._nc.dram_tensor(self._pfx + name, *a, **kw)

    def __getattr__(self, a):
        return getattr(self._nc, a)




def make_identity(nc, ident):
    nc.gpsimd.memset(ident, 0.0)
    return nc.gpsimd.affine_select(
        out=ident, in_=ident, compare_op=ALU.not_equal,
        fill=1.0, base=0, pattern=[[-1, ident.shape[0]]], channel_multiplier=1)


class Common:
    def __init__(self, nc, p):
        self.identf = nc.alloc_sbuf_tensor("c_identf", [128, 128], F32)
        self.identb = nc.alloc_sbuf_tensor("c_identb", [128, 128], BF16)
        self.onesf = nc.alloc_sbuf_tensor("c_onesf", [128, 128], F32)
        self.onesb = nc.alloc_sbuf_tensor("c_onesb", [128, 128], BF16)
        self.hmask = nc.alloc_sbuf_tensor("c_hmask", [128, 2], BF16)
        self.pgscr = nc.alloc_sbuf_tensor("c_pgscr", [1, 2], F32)
        p.pg_scr = self.pgscr
        for t, k in ((self.identf, "c_identf"), (self.identb, "c_identb")):
            p.op("gpsimd", lambda e, t=t: e.memset(t[:], 0.0),
                 writes=[(k,)], hazard=True)
            p.op("gpsimd", lambda e, t=t: e.affine_select(
                out=t[:], in_=t[:], compare_op=ALU.not_equal,
                fill=1.0, base=0, pattern=[[-1, 128]], channel_multiplier=1),
                reads=[(k,)], writes=[(k,)], hazard=True)
        p.op("vector", lambda e: e.memset(self.onesf[:], 1.0), writes=[("c_onesf",)])
        p.op("vector", lambda e: e.memset(self.onesb[:], 1.0), writes=[("c_onesb",)])
        p.op("vector", lambda e: e.memset(self.hmask[:], 0.0), writes=[("c_hmask",)])
        p.op("vector", lambda e: e.memset(self.hmask[0:64, 0:1], 1.0),
             writes=[("c_hmask",)])
        p.op("vector", lambda e: e.memset(self.hmask[64:128, 1:2], 1.0),
             writes=[("c_hmask",)])


def ln_tiles(nc, p, cm, name, x_ap, n_tiles, own_tiles, xnT_own, xnT_par,
             pT, qdma, invn=None, eps=1e-5, out_tm=None, src_rk=None):
    guards = []

    def ra(nm, shape, dtype):
        g = nc.sbuf_tensor(nm, shape, dtype)
        t = g.__enter__()
        guards.append(g)
        return t

    xt = [ra(f"{name}_xt{j}", [128, D], F32) for j in range(2)]
    xm = [ra(f"{name}_xm{j}", [128, D], BF16) for j in range(2)]
    dmy = ra(f"{name}_dmy", [128, D], F32)
    st = ra(f"{name}_st", [128, 8], F32)
    SK = (f"{name}_st",)
    for i in range(n_tiles):
        j = i % 2
        xk = (f"{name}_xt", j)
        p.dma(qdma, lambda e, i=i, j=j: e.dma_start(
            out=xt[j][:], in_=x_ap[i * 128:(i + 1) * 128, :]),
            reads=(src_rk(i) if src_rk else []), writes=[xk])
        p.op("vector", lambda e, j=j: e.reduce_sum(st[:, 0:1], xt[j][:], axis=X),
             reads=[xk], writes=[SK], hazard=True)
        p.op("scalar", lambda e, j=j: e.activation(dmy[:], xt[j][:], AF.Square,
                                                   accum_out=st[:, 1:2]),
             reads=[xk], writes=[SK, (f"{name}_dmy",)], hazard=True)
        p.op("vector", lambda e: e.tensor_scalar_mul(st[:, 2:3], st[:, 0:1],
                                                     1.0 / D),
             reads=[SK], writes=[SK])
        p.op("vector", lambda e: e.scalar_tensor_tensor(
            st[:, 6:7], st[:, 2:3], 1.0, st[:, 2:3], ALU.mult, ALU.mult),
            reads=[SK], writes=[SK])
        p.op("vector", lambda e: e.scalar_tensor_tensor(
            st[:, 3:4], st[:, 1:2], 1.0 / D, st[:, 6:7], ALU.mult, ALU.subtract),
            reads=[SK], writes=[SK])
        p.op("vector", lambda e: e.tensor_scalar_add(st[:, 6:7], st[:, 3:4], eps),
             reads=[SK], writes=[SK])
        p.op("scalar", lambda e: e.activation(st[:, 7:8], st[:, 6:7], AF.Sqrt),
             reads=[SK], writes=[SK], hazard=True)
        p.op("vector", lambda e: e.reciprocal(st[:, 4:5], st[:, 7:8]),
             reads=[SK], writes=[SK], hazard=True)
        p.op("vector", lambda e: e.scalar_tensor_tensor(
            st[:, 5:6], st[:, 2:3], -1.0, st[:, 4:5], ALU.mult, ALU.mult),
            reads=[SK], writes=[SK])
        p.op("scalar", lambda e, j=j: e.activation(
            xm[j][:], xt[j][:], AF.Identity, bias=st[:, 5:6], scale=st[:, 4:5]),
            reads=[xk, SK], writes=[(f"{name}_xm", j)])
        if invn is not None and i < own_tiles:
            p.op("vector", lambda e: e.tensor_scalar_mul(st[:, 7:8], st[:, 3:4],
                                                         float(D)),
                 reads=[SK], writes=[SK])
            p.op("vector", lambda e: e.reciprocal(st[:, 7:8], st[:, 7:8]),
                 reads=[SK], writes=[SK], hazard=True)
            p.op("vector", lambda e: e.tensor_mul(st[:, 7:8], st[:, 7:8],
                                                  st[:, 6:7]),
                 reads=[SK], writes=[SK])
            p.op("scalar", lambda e, i=i: e.activation(invn[:, i:i + 1],
                                                       st[:, 7:8], AF.Sqrt),
                 reads=[SK], writes=[("invn",)], hazard=True)
        if out_tm is not None:
            p.op("vector", lambda e, j=j, i=i: e.tensor_copy(
                out=out_tm[:, i * D:(i + 1) * D], in_=xm[j][:]),
                reads=[(f"{name}_xm", j)], writes=[("xtm", i // 4)])
        dst = xnT_own if i < own_tiles else xnT_par
        dk = "xno" if i < own_tiles else "xnp"
        loc = i if i < own_tiles else i - own_tiles
        for c in range(8):
            s = c % 2
            p.op("tensor", lambda e, j=j, c=c, s=s: e.transpose(
                pT[s][:], xm[j][:, c * 128:(c + 1) * 128], cm.identb[:]),
                reads=[(f"{name}_xm", j), ("c_identb",)], writes=[("pT", s)])
            p.op("vector", lambda e, c=c, s=s, loc=loc, dst=dst: e.tensor_copy(
                out=dst[:, c * HALF + loc * 128:c * HALF + loc * 128 + 128],
                in_=pT[s][:]),
                reads=[("pT", s)], writes=[(dk, loc // 4)])
    return guards


def free_guards(guards):
    for g in reversed(guards):
        g.__exit__(None, None, None)




def emit_l0(nc, p, cm, tapmap, ins, y1, stop_after="full"):
    _stops = ["consts", "ln1t", "ln", "pair_k", "pair_q", "pair_v", "pair_a", "pairs", "proj", "gating", "ffn", "full"]
    def _past(ph):
        return _stops.index(stop_after) <= _stops.index(ph)
    x = ins["x"]
    w_qkv = ins["w_qkv"]
    w_proj = ins["w_proj"]
    b_proj = ins["b_proj"]
    gate_w = ins["gate_w"]
    w_gate = ins["w_gate"]
    w_up = ins["w_up"]
    w_down = ins["w_down"]
    sw_gate = ins["sw_gate"]
    sw_up = ins["sw_up"]
    sw_down = ins["sw_down"]
    qx, qw, qm = "qx", "qw", "qm"

    g_dacc = nc.sbuf_tensor("dacc", [128, 8 * HALF], F32, side="right")
    dacc = g_dacc.__enter__()
    g_xnT_own = nc.sbuf_tensor("xnT_own", [128, 8 * HALF], BF16, side="right")
    xnT_own = g_xnT_own.__enter__()
    g_attnoT = nc.sbuf_tensor("attnoT", [128, 8 * HALF], BF16, side="right")
    attnoT = g_attnoT.__enter__()
    g_xnT_par = nc.sbuf_tensor("xnT_par", [128, 8 * HALF], BF16, side="right")
    xnT_par = g_xnT_par.__enter__()
    invn = nc.alloc_sbuf_tensor("invn", [128, 16], F32)
    negM = nc.alloc_sbuf_tensor("negM", [128, 16], F32)
    bpr = nc.alloc_sbuf_tensor("bpr", [128, 8], F32)

    def palloc(nm, shape, dtype, key=None):
        return p.palloc(nc, nm, shape, dtype, key=key)

    g_pT0, pT0 = palloc("pT0", [128, 128], BF16, key=("pT", 0))
    g_pT1, pT1 = palloc("pT1", [128, 128], BF16, key=("pT", 1))

    for c in range(8):
        p.dma(qx, lambda e, c=c: e.dma_start(
            out=bpr[:, c:c + 1],
            in_=b_proj[c * 128:(c + 1) * 128].rearrange("(a b) -> a b", b=1)),
            writes=[("bpr",)])

    if _past("consts"):
        p.dma(qx, lambda e: e.dma_start(out=y1[0:128, 0:8], in_=bpr[:]),
              reads=[("bpr",)])
        return True

    tapmap["xnT_own"] = (xnT_own, [("xno", i) for i in range(4)])
    tapmap["xnT_par"] = (xnT_par, [("xnp", i) for i in range(4)])
    tapmap["invn"] = (invn, [("invn",)])
    tapmap["negM"] = (negM, [("negM",)])

    n_ln_tiles = 1 if stop_after == "ln1t" else 32
    lng = ln_tiles(nc, p, cm, "ln0", x, n_ln_tiles, 16, xnT_own, xnT_par,
                   [pT0, pT1], qx, invn=invn)
    free_guards(lng)
    p.barrier()
    if _past("ln1t"):
        p.dma(qx, lambda e: e.dma_start(out=y1[0:128, 0:8], in_=bpr[:]),
              reads=[("bpr",)])
        return True

    if _past("ln"):
        p.dma(qx, lambda e: e.dma_start(out=y1[0:128, 0:8], in_=bpr[:]),
              reads=[("bpr",)])
        return True

    g_pT1.__exit__(None, None, None)
    g_pT0.__exit__(None, None, None)
    g_pKV0, pKV0 = palloc("pKV0", [128, 512], F32)
    g_pKV1, pKV1 = palloc("pKV1", [128, 512], F32)
    g_pS0, pS0 = palloc("pS0", [128, 512], F32)
    g_pS1, pS1 = palloc("pS1", [128, 512], F32)
    g_pO, pO = palloc("pO", [128, 512], F32)
    g_pB, pB = palloc("pB", [64, 512], F32)


    def xn_rhs(dc, t0, w):
        if t0 < HALF:
            return (xnT_own[:, dc * HALF + t0:dc * HALF + t0 + w],
                    ("xno", t0 // 512))
        t = t0 - HALF
        return (xnT_par[:, dc * HALF + t:dc * HALF + t + w], ("xnp", t // 512))

    pair_guards = []

    def rpa(nm, shape, dtype):
        g = nc.sbuf_tensor(nm, shape, dtype, side="right")
        t = g.__enter__()
        pair_guards.append(g)
        return t

    sqs = rpa("sqs", [128, 512], BF16)
    nsc = rpa("nsc", [2, 16], F32)
    diag = rpa("diag", [2, 2], F32)
    pts = [rpa(f"pts{j}", [128, 512], BF16) for j in range(4)]
    rown = rpa("rown", [128, 512], F32)
    bcs = rpa("bcs", [64, 512], F32)
    ostg = rpa("ostg", [64, 512], BF16)
    wq_g = rpa("wq_g", [128, 8 * 128], BF16)
    wk_g = rpa("wk_g", [128, 8 * 128], BF16)
    wv_g = rpa("wv_g", [128, 8 * 128], BF16)
    kT = rpa("kT", [128, SEQ], BF16)
    qT = rpa("qT", [128, HALF], BF16)
    vt = rpa("vt", [128, 32 * 256], BF16)

    tapmap["kT"] = (kT, [("kT", i) for i in range(4)])
    tapmap["qT"] = (qT, [("qT", i) for i in range(4)])
    tapmap["vt"] = (vt, [("vt", i) for i in range(32)] + [("vt_ones",)])
    vt_r = vt[:].rearrange("p (i hh c) -> p i hh c", i=32, hh=2)
    p.op("vector", lambda e: e.memset(vt[:], 0.0), writes=[("vt_ones",)])
    p.op("vector", lambda e: e.memset(vt_r[:, :, :, 64:65], 1.0),
         reads=[("vt_ones",)], writes=[("vt_ones",)], hazard=True)

    import os as _os
    n_pairs = 1 if stop_after.startswith("pair_") else 8
    for kc in range(n_pairs):
        g_pN, pN = palloc(f"pN{kc}", [2, 512], F32)
        g_pM, pM = palloc(f"pM{kc}", [128, 2], F32)
        for wt, wn, col0 in ((wq_g, "wq_g", kc * 128),
                             (wk_g, "wk_g", D + kc * 128),
                             (wv_g, "wv_g", 2 * D + kc * 128)):
            p.dma(qw, lambda e, wt=wt, col0=col0: e.dma_start(
                out=wt[:].rearrange("p (dc n) -> p dc n", dc=8),
                in_=w_qkv[:, col0:col0 + 128].rearrange("(dc p) n -> p dc n",
                                                        p=128)),
                writes=[(wn,)], tag=f"load_{wn}{kc}")

        p.op("vector", lambda e: e.memset(nsc[:, 0:2], 0.0), writes=[("nsc",)])
        for t in range(int(_os.environ.get("KWIN", "8"))):
            ps, pk = (pKV0, ("pKV0",)) if t % 2 == 0 else (pKV1, ("pKV1",))
            _kdc = int(_os.environ.get("KDC", "8"))
            for dc in range(_kdc):
                rhs, rk = xn_rhs(dc, t * 512, 512)
                p.op("tensor", lambda e, dc=dc, rhs=rhs, ps=ps, _kdc=_kdc: e.matmul(
                    ps[:], wk_g[:, dc * 128:(dc + 1) * 128],
                    rhs, start=(dc == 0), stop=(dc == _kdc - 1)),
                    reads=[("wk_g",), rk], writes=[pk])
            _kp = int(_os.environ.get("KPART", "5"))
            if _kp >= 2:
                p.op("scalar", lambda e, t=t, ps=ps: e.copy(
                    out=kT[:, t * 512:t * 512 + 512], in_=ps[:]),
                    reads=[pk], writes=[("kT", t // 2)])
            if _kp >= 3:
                p.op("vector", lambda e, t=t: e.tensor_mul(
                    sqs[:], kT[:, t * 512:t * 512 + 512],
                    kT[:, t * 512:t * 512 + 512]),
                    reads=[("kT", t // 2)], writes=[("sqs",)])
            if _kp >= 4:
                p.op("tensor", lambda e, pN=pN: e.matmul(
                    pN[:], cm.hmask[:], sqs[:], start=True, stop=True),
                    reads=[("sqs",), ("c_hmask",)], writes=[(f"pN{kc}",)])
            if _kp >= 5:
                p.op("vector", lambda e, pN=pN: e.reduce_max(
                    nsc[:, 2:3], pN[:], axis=X),
                    reads=[(f"pN{kc}",)], writes=[("nsc",)], hazard=True)
                p.op("vector", lambda e: e.tensor_max(nsc[:, 1:2], nsc[:, 1:2],
                                                      nsc[:, 2:3]),
                     reads=[("nsc",)], writes=[("nsc",)])
        for t in range(4 if not _past("pair_k") else 0):
            ps, pk = (pKV0, ("pKV0",)) if t % 2 == 0 else (pKV1, ("pKV1",))
            for dc in range(8):
                rhs, rk = xn_rhs(dc, t * 512, 512)
                p.op("tensor", lambda e, dc=dc, rhs=rhs, ps=ps: e.matmul(
                    ps[:], wq_g[:, dc * 128:(dc + 1) * 128],
                    rhs, start=(dc == 0), stop=(dc == 7)),
                    reads=[("wq_g",), rk], writes=[pk])
            p.op("scalar", lambda e, t=t, ps=ps: e.copy(
                out=qT[:, t * 512:t * 512 + 512], in_=ps[:]),
                reads=[pk], writes=[("qT", t)])
            p.op("vector", lambda e, t=t: e.tensor_mul(
                sqs[:], qT[:, t * 512:t * 512 + 512],
                qT[:, t * 512:t * 512 + 512]),
                reads=[("qT", t)], writes=[("sqs",)])
            p.op("tensor", lambda e, pN=pN: e.matmul(pN[:], cm.hmask[:], sqs[:],
                                              start=True, stop=True),
                 reads=[("sqs",), ("c_hmask",)], writes=[(f"pN{kc}",)])
            p.op("vector", lambda e, pN=pN: e.reduce_max(nsc[:, 3:4], pN[:], axis=X),
                 reads=[(f"pN{kc}",)], writes=[("nsc",)], hazard=True)
            p.op("vector", lambda e: e.tensor_max(nsc[:, 0:1], nsc[:, 0:1],
                                                  nsc[:, 3:4]),
                 reads=[("nsc",)], writes=[("nsc",)])
        if _past("pair_q"):
            continue
        p.op("vector", lambda e: e.tensor_mul(nsc[:, 4:5], nsc[:, 0:1],
                                              nsc[:, 1:2]),
             reads=[("nsc",)], writes=[("nsc",)])
        p.op("scalar", lambda e: e.activation(nsc[:, 5:6], nsc[:, 4:5], AF.Sqrt),
             reads=[("nsc",)], writes=[("nsc",)], hazard=True)
        p.op("vector", lambda e: e.tensor_scalar_mul(nsc[:, 5:6], nsc[:, 5:6],
                                                     -SCALE),
             reads=[("nsc",)], writes=[("nsc",)], hazard=True)
        p.op("vector", lambda e: e.tensor_mul(
            diag[:], nsc[:, 5:6].to_broadcast([2, 2]), cm.identf[0:2, 0:2]),
            reads=[("nsc",), ("c_identf",)], writes=[("diag",)])
        h0 = 2 * kc
        p.op("tensor", lambda e: e.matmul(
            pM[:], cm.onesf[0:2, 0:128], diag[:], start=True, stop=True),
            reads=[("diag",), ("c_onesf",)], writes=[(f"pM{kc}",)])
        p.op("vector", lambda e, h0=h0, pM=pM, kc=kc: e.tensor_copy(
            out=negM[:, h0:h0 + 2], in_=pM[:]),
            reads=[(f"pM{kc}",)], writes=[("negM",)])
        g_pM.__exit__(None, None, None)
        g_pN.__exit__(None, None, None)
        g_pOB, pOB = palloc(f"pOB{kc}", [128, 512], F32)

        for i in range(32 if not _past("pair_v") else 0):
            ps, pk = (pKV0, ("pKV0",)) if i % 2 == 0 else (pKV1, ("pKV1",))
            for dc in range(8):
                lhs, rk = xn_rhs(dc, i * 128, 128)
                p.op("tensor", lambda e, lhs=lhs, dc=dc, ps=ps: e.matmul(
                    ps[:, 0:128], lhs, wv_g[:, dc * 128:(dc + 1) * 128],
                    start=(dc == 0), stop=(dc == 7)),
                    reads=[("wv_g",), rk], writes=[pk])
            p.op("vector", lambda e, i=i, ps=ps: e.tensor_copy(
                out=vt_r[:, i, :, 0:64],
                in_=ps[:, 0:128].rearrange("p (hh c) -> p hh c", hh=2)),
                reads=[pk], writes=[("vt", i)])

        if _past("pair_a"):
            g_pOB.__exit__(None, None, None)
            continue
        pSa = (pS0, pS1)
        pSak = (("pS0",), ("pS1",))
        pSb = (pKV0, pKV1)
        pSbk = (("pKV0",), ("pKV1",))
        pOk = ("pO",)
        pOBk = (f"pOB{kc}",)

        def emit_S2(tc, q0, qb):
            sa, sak = pSa[tc % 2], pSak[tc % 2]
            sb, sbk = pSb[tc % 2], pSbk[tc % 2]
            p.op("tensor", lambda e, tc=tc, q0=q0, sa=sa: e.matmul(
                sa[:], kT[0:64, tc * 128:tc * 128 + 128],
                qT[0:64, q0:q0 + 512],
                start=True, stop=True, tile_position=(0, 0)),
                reads=[("kT", tc // 8), ("qT", qb)], writes=[sak])
            p.op("tensor", lambda e, tc=tc, q0=q0, sb=sb: e.matmul(
                sb[:], kT[64:128, tc * 128:tc * 128 + 128],
                qT[64:128, q0:q0 + 512],
                start=True, stop=True, tile_position=(64, 0)),
                reads=[("kT", tc // 8), ("qT", qb)], writes=[sbk])

        hA, hB = 2 * kc, 2 * kc + 1
        for qb in range(4):
            q0 = qb * 512
            emit_S2(0, q0, qb)
            for tc in range(32):
                sa, sak = pSa[tc % 2], pSak[tc % 2]
                sb, sbk = pSb[tc % 2], pSbk[tc % 2]
                pa, pak = pts[tc % 2], ("pts", tc % 2)
                pb_, pbk = pts[2 + tc % 2], ("pts", 2 + tc % 2)
                p.op("scalar", lambda e, sa=sa, pa=pa, hA=hA: e.activation(
                    pa[:], sa[:], AF.Exp, bias=negM[:, hA:hA + 1], scale=SCALE),
                    reads=[sak, ("negM",)], writes=[pak])
                p.op("scalar", lambda e, sb=sb, pb_=pb_, hB=hB: e.activation(
                    pb_[:], sb[:], AF.Exp, bias=negM[:, hB:hB + 1], scale=SCALE),
                    reads=[sbk, ("negM",)], writes=[pbk])
                if tc + 1 < 32:
                    emit_S2(tc + 1, q0, qb)
                p.op("tensor", lambda e, tc=tc, pa=pa: e.matmul(
                    pO[:], vt[:, tc * 256:tc * 256 + 128],
                    pa[:], start=(tc == 0), stop=(tc == 31)),
                    reads=[pak, ("vt", tc), ("vt_ones",)], writes=[pOk])
                p.op("tensor", lambda e, tc=tc, pb_=pb_, pOB=pOB: e.matmul(
                    pOB[:], vt[:, tc * 256 + 128:tc * 256 + 256],
                    pb_[:], start=(tc == 0), stop=(tc == 31)),
                    reads=[pbk, ("vt", tc), ("vt_ones",)], writes=[pOBk])
            p.op("scalar", lambda e: e.copy(out=rown[64:65, :],
                                            in_=pO[64:65, :]),
                 reads=[pOk], writes=[("rown",)])
            p.op("vector", lambda e: e.reciprocal(rown[64:65, :],
                                                  rown[64:65, :]),
                 reads=[("rown",)], writes=[("rown",)], hazard=True)
            p.op("tensor", lambda e: e.matmul(
                pB[:], cm.onesf[64:65, 0:64], rown[64:65, :],
                start=True, stop=True, tile_position=(64, 0)),
                reads=[("rown",), ("c_onesf",)], writes=[("pB",)])
            p.op("scalar", lambda e: e.copy(out=bcs[:], in_=pB[:]),
                 reads=[("pB",)], writes=[("bcs",)])
            p.op("vector", lambda e, q0=q0, kc=kc: e.tensor_mul(
                attnoT[0:64, kc * HALF + q0:kc * HALF + q0 + 512],
                pO[0:64, :], bcs[:]),
                reads=[pOk, ("bcs",)], writes=[("aT", kc, 0, q0)])
            p.op("scalar", lambda e, pOB=pOB: e.copy(out=rown[64:65, :],
                                                     in_=pOB[64:65, :]),
                 reads=[pOBk, ("aT", kc, 0, q0)], writes=[("rown",)])
            p.op("vector", lambda e: e.reciprocal(rown[64:65, :],
                                                  rown[64:65, :]),
                 reads=[("rown",)], writes=[("rown",)], hazard=True)
            p.op("tensor", lambda e: e.matmul(
                pB[:], cm.onesf[64:65, 0:64], rown[64:65, :],
                start=True, stop=True, tile_position=(64, 0)),
                reads=[("rown",), ("c_onesf",)], writes=[("pB",)])
            p.op("scalar", lambda e: e.copy(out=bcs[:], in_=pB[:]),
                 reads=[("pB",)], writes=[("bcs",)])
            p.op("vector", lambda e, pOB=pOB: e.tensor_mul(
                ostg[:], pOB[0:64, :], bcs[:]),
                reads=[pOBk, ("bcs",)], writes=[("ostg",)])
            p.dma(qm, lambda e, q0=q0, kc=kc: e.dma_start(
                out=attnoT[64:128, kc * HALF + q0:kc * HALF + q0 + 512],
                in_=ostg[:]),
                reads=[("ostg",)], writes=[("aT", kc, 1, q0)])
        g_pOB.__exit__(None, None, None)

    if _past("pairs"):
        p.dma(qx, lambda e: e.dma_start(out=y1[0:128, 0:8], in_=bpr[:]),
              reads=[("bpr",)])
        return True

    if _past("pair_a") and not _past("pairs"):
        pass
    if stop_after.startswith("pair_"):
        p.dma(qx, lambda e: e.dma_start(out=y1[0:128, 0:8], in_=bpr[:]),
              reads=[("bpr",)])
        return True

    for g in (g_pB, g_pO, g_pS1, g_pS0, g_pKV1, g_pKV0):
        g.__exit__(None, None, None)
    free_guards(pair_guards)
    if not _os.environ.get("KEEP_PAR"):
        g_xnT_par.__exit__(None, None, None)
    p.barrier()
    g_pP0, pP0 = palloc("pP0", [128, 512], F32)
    g_pP1, pP1 = palloc("pP1", [128, 512], F32)
    tapmap["dacc"] = (dacc, [("dacc", c, w) for c in range(8) for w in range(4)])
    g_wpj = nc.sbuf_tensor("wpj", [128, 8 * 1024], BF16, side="right")
    wpj = g_wpj.__enter__()
    p.dma(qw, lambda e: e.dma_start(
        out=wpj[:].rearrange("p (c n) -> p c n", c=8),
        in_=w_proj.rearrange("(c p) n -> p c n", p=128)),
        writes=[("wpj",)])
    aT_keys = [("aT", c, par, q0) for c in range(8) for par in range(2)
               for q0 in (0, 512, 1024, 1536)]
    tapmap["attnoT"] = (attnoT, aT_keys)
    for qb in range(4):
        q0 = qb * 512
        for dc8 in range(8):
            ps, pk = (pP0, ("pP0",)) if dc8 % 2 == 0 else (pP1, ("pP1",))
            rks = [("aT", c, par, q0) for c in range(8) for par in range(2)]
            for c in range(8):
                p.op("tensor", lambda e, c=c, dc8=dc8, q0=q0, ps=ps: e.matmul(
                    ps[:], wpj[:, c * 1024 + dc8 * 128:c * 1024 + dc8 * 128 + 128],
                    attnoT[:, c * HALF + q0:c * HALF + q0 + 512],
                    start=(c == 0), stop=(c == 7)),
                    reads=[("wpj",)] + rks, writes=[pk])
            p.op("scalar", lambda e, dc8=dc8, q0=q0, ps=ps: e.activation(
                dacc[:, dc8 * HALF + q0:dc8 * HALF + q0 + 512], ps[:], AF.Identity,
                bias=bpr[:, dc8:dc8 + 1]),
                reads=[pk, ("bpr",)], writes=[("dacc", dc8, qb)])

    g_wpj.__exit__(None, None, None)
    if not _os.environ.get("KEEP_PAR"):
        g_attnoT.__exit__(None, None, None)
    p.barrier()

    if _past("proj"):
        p.dma(qx, lambda e: e.dma_start(out=y1[0:128, 0:8], in_=bpr[:]),
              reads=[("bpr",)])
        return True

    g_pP1.__exit__(None, None, None)
    g_pP0.__exit__(None, None, None)
    g_pG, pG = palloc("pG", [128, 16], F32)
    g_pGW, pGW = palloc("pGW", [128, 16], BF16)
    g_pCT, pCT = palloc("pCT", [16, 128], BF16)
    gat_guards = []

    def gga(nm, shape, dtype):
        g = nc.sbuf_tensor(nm, shape, dtype, side="right")
        t = g.__enter__()
        gat_guards.append(g)
        return t

    gst = gga("gst", [16, 4], F32)
    gwn = gga("gwn", [16, D], BF16)
    gwnT = gga("gwnT", [128, 128], BF16)
    scor = gga("scor", [128, 16], F32)
    m8 = gga("m8", [128, 8], F32)
    s8 = gga("s8", [128, 1], F32)
    r8 = gga("r8", [128, 1], F32)
    zap = gga("zap", [128, 16], F32)
    cmbf = gga("cmbf", [128, 16], BF16)
    combT = gga("combT", [16, HALF], BF16)
    tapmap["combT"] = (combT, [("combT", i) for i in range(4)])
    sel = gga("sel", [16, 16 * 128], BF16)
    g_gwf = nc.sbuf_tensor("gwf", [16, D], F32, side="right")
    gwf = g_gwf.__enter__()
    g_gsq = nc.sbuf_tensor("gsq", [16, D], F32, side="right")
    gsq = g_gsq.__enter__()

    p.dma(qx, lambda e: e.dma_start(out=gwf[:], in_=gate_w), writes=[("gwf",)])
    p.op("scalar", lambda e: e.activation(gsq[:], gwf[:], AF.Square,
                                          accum_out=gst[:, 0:1]),
         reads=[("gwf",)], writes=[("gsq",), ("gst",)], hazard=True)
    p.op("vector", lambda e: e.reciprocal(gst[:, 1:2], gst[:, 0:1]),
         reads=[("gst",)], writes=[("gst",)], hazard=True)
    p.op("scalar", lambda e: e.activation(gst[:, 2:3], gst[:, 1:2], AF.Sqrt),
         reads=[("gst",)], writes=[("gst",)], hazard=True)
    p.op("vector", lambda e: e.tensor_scalar_mul(gwn[:], gwf[:], gst[:, 2:3]),
         reads=[("gwf",), ("gst",)], writes=[("gwn",)])
    for c in range(8):
        p.op("tensor", lambda e, c=c: e.transpose(
            pGW[:], gwn[0:16, c * 128:(c + 1) * 128], cm.identb[0:16, 0:16]),
            reads=[("gwn",), ("c_identb",)], writes=[("pGW",)])
        p.op("vector", lambda e, c=c: e.tensor_copy(
            out=gwnT[:, c * 16:(c + 1) * 16], in_=pGW[:]),
            reads=[("pGW",)], writes=[("gwnT",)])
    g_gsq.__exit__(None, None, None)
    g_gwf.__exit__(None, None, None)
    p.barrier()

    def make_sel(e):
        nc.gpsimd.memset(sel[:], 0.0)
        return nc.gpsimd.affine_select(
            out=sel[:].rearrange("p (eb c) -> p eb c", eb=16),
            in_=sel[:].rearrange("p (eb c) -> p eb c", eb=16),
            compare_op=ALU.not_equal, fill=1.0, base=0,
            pattern=[[1, 16], [0, 128]], channel_multiplier=-1)
    p.op("gpsimd", lambda e: make_sel(e), writes=[("sel",)])

    for i in range(16):
        for dc in range(8):
            p.op("tensor", lambda e, i=i, dc=dc: e.matmul(
                pG[:], xnT_own[:, dc * HALF + i * 128:dc * HALF + i * 128 + 128],
                gwnT[:, dc * 16:(dc + 1) * 16],
                start=(dc == 0), stop=(dc == 7)),
                reads=[("gwnT",), ("xno", i // 4)], writes=[("pG",)])
        p.op("scalar", lambda e, i=i: e.activation(
            scor[:], pG[:], AF.Sigmoid, scale=invn[:, i:i + 1]),
            reads=[("pG",), ("invn",)], writes=[("scor",)])
        p.op("vector", lambda e: e.max(out=m8[:], in_=scor[:]),
             reads=[("scor",)], writes=[("m8",)], hazard=True)
        p.op("vector", lambda e: e.reduce_sum(s8[:], m8[:], axis=X),
             reads=[("m8",)], writes=[("s8",)], hazard=True)
        p.op("vector", lambda e: e.reciprocal(r8[:], s8[:]),
             reads=[("s8",)], writes=[("r8",)], hazard=True)
        p.op("vector", lambda e: e.match_replace(out=zap[:], in_to_replace=m8[:],
                                                 in_values=scor[:], imm_value=0.0),
             reads=[("scor",), ("m8",)], writes=[("zap",), ("m8",)],
             hazard=True)
        p.op("vector", lambda e: e.tensor_sub(zap[:], scor[:], zap[:]),
             reads=[("scor",), ("zap",)], writes=[("zap",)])
        p.op("vector", lambda e: e.tensor_scalar_mul(cmbf[:], zap[:], r8[:]),
             reads=[("zap",), ("r8",)], writes=[("cmbf",)])
        p.op("tensor", lambda e: e.transpose(
            pCT[:], cmbf[:], cm.identb[:]),
            reads=[("cmbf",), ("c_identb",)], writes=[("pCT",)])
        p.op("vector", lambda e, i=i: e.tensor_copy(
            out=combT[0:16, i * 128:(i + 1) * 128], in_=pCT[:]),
            reads=[("pCT",)], writes=[("combT", i // 4)])

    if _past("gating"):
        p.dma(qx, lambda e: e.dma_start(out=y1[0:128, 0:8], in_=bpr[:]),
              reads=[("bpr",)])
        return True

    p.barrier()
    g_pCT.__exit__(None, None, None)
    g_pGW.__exit__(None, None, None)
    g_pG.__exit__(None, None, None)
    g_pGU0, pGU0 = palloc("pGU0", [128, 512], F32)
    g_pGU1, pGU1 = palloc("pGU1", [128, 512], F32)
    g_pBC, pBC = palloc("pBC", [128, 512], F32)
    _psRg = [palloc(f"pR{j}", [128, 512], F32, key=("psR", j))
             for j in range(4)]
    psR = [t for _, t in _psRg]
    ffn_guards = []

    def ffa(nm, shape, dtype):
        g = nc.sbuf_tensor(nm, shape, dtype, side="right")
        t = g.__enter__()
        ffn_guards.append(g)
        return t

    wg = [ffa(f"wg{j}", [128, 8 * 512], BF16) for j in range(2)]
    wu = [ffa(f"wu{j}", [128, 8 * 512], BF16) for j in range(2)]
    wd = [ffa(f"wd{j}", [128, 4 * 1024], BF16) for j in range(2)]
    fbc = ffa("fbc", [128, 512], BF16)
    gsil = ffa("gsil", [128, 4 * 512], BF16)
    ub = ffa("ub", [128, 4 * 512], BF16)
    hb = ffa("hb", [128, 4 * 512], BF16)

    def load_expert(e_, j):
        if e_ < 16:
            srcs = (w_gate[e_], w_up[e_], w_down[e_])
        else:
            srcs = (sw_gate, sw_up, sw_down)
        p.dma(qw, lambda e, s=srcs[0], j=j: e.dma_start(
            out=wg[j][:].rearrange("p (dc n) -> p dc n", dc=8),
            in_=s.rearrange("(dc p) n -> p dc n", p=128)),
            writes=[("wg", j)])
        p.dma(qw, lambda e, s=srcs[1], j=j: e.dma_start(
            out=wu[j][:].rearrange("p (dc n) -> p dc n", dc=8),
            in_=s.rearrange("(dc p) n -> p dc n", p=128)),
            writes=[("wu", j)])
        p.dma(qw, lambda e, s=srcs[2], j=j: e.dma_start(
            out=wd[j][:].rearrange("p (hc n) -> p hc n", hc=4),
            in_=s.rearrange("(hc p) n -> p hc n", p=128)),
            writes=[("wd", j)])

    load_expert(0, 0)
    for ex in range(17):
        j = ex % 2
        if ex + 1 < 17:
            load_expert(ex + 1, (ex + 1) % 2)
        for w4 in range(4):
            t0 = w4 * 512
            if ex < 16:
                p.op("tensor", lambda e, ex=ex, t0=t0: e.matmul(
                    pBC[:], sel[:, ex * 128:(ex + 1) * 128],
                    combT[0:16, t0:t0 + 512], start=True, stop=True),
                    reads=[("sel",), ("combT", w4)], writes=[("pBC",)])
                p.op("scalar", lambda e: e.copy(out=fbc[:], in_=pBC[:]),
                     reads=[("pBC",)], writes=[("fbc",)])
            for hc in range(4):
                for dc in range(8):
                    p.op("tensor", lambda e, j=j, hc=hc, dc=dc, t0=t0: e.matmul(
                        pGU0[:], wg[j][:, dc * 512 + hc * 128:dc * 512 + hc * 128 + 128],
                        xnT_own[:, dc * HALF + t0:dc * HALF + t0 + 512],
                        start=(dc == 0), stop=(dc == 7)),
                        reads=[("wg", j), ("xno", w4)], writes=[("pGU0",)])
                p.op("scalar", lambda e, hc=hc: e.activation(
                    gsil[:, hc * 512:(hc + 1) * 512], pGU0[:], AF.Silu),
                    reads=[("pGU0",)], writes=[("gsil", hc)])
                for dc in range(8):
                    p.op("tensor", lambda e, j=j, hc=hc, dc=dc, t0=t0: e.matmul(
                        pGU1[:], wu[j][:, dc * 512 + hc * 128:dc * 512 + hc * 128 + 128],
                        xnT_own[:, dc * HALF + t0:dc * HALF + t0 + 512],
                        start=(dc == 0), stop=(dc == 7)),
                        reads=[("wu", j), ("xno", w4)], writes=[("pGU1",)])
                if ex < 16:
                    p.op("vector", lambda e, hc=hc: e.tensor_mul(
                        ub[:, hc * 512:(hc + 1) * 512], pGU1[:], fbc[:]),
                        reads=[("pGU1",), ("fbc",)], writes=[("ub", hc)])
                else:
                    p.op("scalar", lambda e, hc=hc: e.copy(
                        out=ub[:, hc * 512:(hc + 1) * 512], in_=pGU1[:]),
                        reads=[("pGU1",)], writes=[("ub", hc)])
                p.op("vector", lambda e, hc=hc: e.tensor_mul(
                    hb[:, hc * 512:(hc + 1) * 512],
                    gsil[:, hc * 512:(hc + 1) * 512],
                    ub[:, hc * 512:(hc + 1) * 512]),
                    reads=[("gsil", hc), ("ub", hc)], writes=[("hb", hc)])
            for half in range(2):
                for d4 in range(4):
                    dc8 = half * 4 + d4
                    for hc in range(4):
                        p.op("tensor", lambda e, j=j, hc=hc, dc8=dc8, d4=d4:
                             e.matmul(
                            psR[d4][:],
                            wd[j][:, hc * 1024 + dc8 * 128:hc * 1024 + dc8 * 128 + 128],
                            hb[:, hc * 512:(hc + 1) * 512],
                            start=(hc == 0), stop=(hc == 3)),
                            reads=[("wd", j), ("hb", hc)], writes=[("psR", d4)])
                for d4 in range(4):
                    dc8 = half * 4 + d4
                    p.op("vector", lambda e, dc8=dc8, t0=t0, d4=d4: e.tensor_add(
                        dacc[:, dc8 * HALF + t0:dc8 * HALF + t0 + 512],
                        dacc[:, dc8 * HALF + t0:dc8 * HALF + t0 + 512],
                        psR[d4][:]),
                        reads=[("psR", d4), ("dacc", dc8, w4)],
                        writes=[("dacc", dc8, w4)])

    if _past("ffn"):
        p.dma(qx, lambda e: e.dma_start(out=y1[0:128, 0:8], in_=bpr[:]),
              reads=[("bpr",)])
        return True

    for _g, _t in reversed(_psRg):
        _g.__exit__(None, None, None)
    g_pBC.__exit__(None, None, None)
    g_pGU1.__exit__(None, None, None)
    g_pGU0.__exit__(None, None, None)
    for g in reversed(ffn_guards):
        g.__exit__(None, None, None)
    for g in reversed(gat_guards):
        g.__exit__(None, None, None)
    g_xnT_own.__exit__(None, None, None)
    p.barrier()
    g_pX0, pX0 = palloc("pX0", [128, 128], F32)
    g_pX1, pX1 = palloc("pX1", [128, 128], F32)
    x1g = [nc.sbuf_tensor(f"x1t{j}", [128, D], F32, side="right") for j in range(2)]
    x1t = [g.__enter__() for g in x1g]
    y1g = [nc.sbuf_tensor(f"y1t{j}", [128, D], F32, side="right") for j in range(2)]
    y1t = [g.__enter__() for g in y1g]
    for i in range(16):
        j = i % 2
        p.dma(qx, lambda e, i=i, j=j: e.dma_start(
            out=x1t[j][:], in_=x[i * 128:(i + 1) * 128, :]),
            writes=[("x1t", j)])
        for c in range(8):
            s = c % 2
            px = pX0 if s == 0 else pX1
            pxk = ("pX0",) if s == 0 else ("pX1",)
            p.op("tensor", lambda e, c=c, i=i, px=px: e.transpose(
                px[:], dacc[:, c * HALF + i * 128:c * HALF + i * 128 + 128],
                cm.identf[:]),
                reads=[("dacc", c, i // 4), ("c_identf",)], writes=[pxk])
            p.op("vector", lambda e, c=c, j=j, px=px, pxk=pxk: e.tensor_add(
                y1t[j][:, c * 128:(c + 1) * 128], x1t[j][:, c * 128:(c + 1) * 128],
                px[:]),
                reads=[pxk, ("x1t", j)], writes=[("y1t", j)])
        p.dma(qx, lambda e, i=i, j=j: e.dma_start(
            out=y1[i * 128:(i + 1) * 128, :], in_=y1t[j][:]),
            reads=[("y1t", j)], writes=[("y1out", i)])

    g_pX1.__exit__(None, None, None)
    g_pX0.__exit__(None, None, None)
    for g in reversed(y1g):
        g.__exit__(None, None, None)
    for g in reversed(x1g):
        g.__exit__(None, None, None)
    g_dacc.__exit__(None, None, None)
    p.barrier()
    return False



def _mk_prog(nc):
    p = Prog(nc)
    p.dma_queue("qx", "sync")
    p.dma_queue("qw", "gpsimd")
    p.dma_queue("qwB", "gpsimd")
    p.dma_queue("qm", "scalar")
    p.dma_queue("qb", "sync")
    p.chain = {"qb"}
    p.qsplit = {"qw": ["qw", "qwB"]}
    return p


def _do_finish(nc, p, tapmap, taps):
    for tname in taps:
        if tname not in tapmap:
            continue
        t, keys = tapmap[tname]
        o = nc.dram_tensor("tap_" + tname, list(t.shape), t.dtype,
                           kind="ExternalOutput").ap()
        p.dma("qx", lambda e, t=t, o=o: e.dma_start(out=o, in_=t[:]),
              reads=keys)
    p.emit()
    return nc, p.stats()


def l0_dram_ins(nc):
    return dict(
        x=nc.dram_tensor("x", [SEQ, D], F32, kind="ExternalInput").ap(),
        w_qkv=nc.dram_tensor("w_qkv", [D, 3 * D], F32, kind="ExternalInput").ap(),
        w_proj=nc.dram_tensor("w_proj", [D, D], F32, kind="ExternalInput").ap(),
        b_proj=nc.dram_tensor("b_proj", [D], F32, kind="ExternalInput").ap(),
        gate_w=nc.dram_tensor("gate_w", [16, D], F32, kind="ExternalInput").ap(),
        w_gate=nc.dram_tensor("w_gate", [16, D, 512], F32,
                              kind="ExternalInput").ap(),
        w_up=nc.dram_tensor("w_up", [16, D, 512], F32, kind="ExternalInput").ap(),
        w_down=nc.dram_tensor("w_down", [16, 512, D], F32,
                              kind="ExternalInput").ap(),
        sw_gate=nc.dram_tensor("sw_gate", [D, 512], F32,
                               kind="ExternalInput").ap(),
        sw_up=nc.dram_tensor("sw_up", [D, 512], F32, kind="ExternalInput").ap(),
        sw_down=nc.dram_tensor("sw_down", [512, D], F32,
                               kind="ExternalInput").ap())


def build_l0(taps=(), stop_after="full"):
    nc = bass.Bass("TRN2", target_bir_lowering=False)
    ins = l0_dram_ins(nc)
    y1 = nc.dram_tensor("y1", [HALF, D], F32, kind="ExternalOutput").ap()
    p = _mk_prog(nc)
    tapmap = {}
    cm = Common(nc, p)
    emit_l0(nc, p, cm, tapmap, ins, y1, stop_after=stop_after)
    return _do_finish(nc, p, tapmap, taps)




def l1_dram_ins(nc, own_only=False):
    d = dict(
        Wq=nc.dram_tensor("Wq", [D, D], F32, kind="ExternalInput").ap())
    for nm, shape in (("bq", [D]), ("bk", [D]), ("bv", [D]),
                      ("b1", [512]), ("b2", [64]), ("tau", [1]),
                      ("bih", [3 * D]), ("bhh", [3 * D]), ("bq1", [256]),
                      ("bk1", [256]), ("bv1", [D]), ("ln_g", [D]),
                      ("ln_b", [D]), ("bvp", [D]), ("bo", [D])):
        d[nm] = nc.dram_tensor(nm, shape, F32, kind="ExternalInput").ap()
    for nm, shape in (("Wk", [D, D]), ("Wv", [D, D]), ("slot_init", [64, D]),
                      ("w1", [2 * D, 512]), ("w2", [512, 64]),
                      ("Wih", [3 * D, D]), ("Whh", [3 * D, D]),
                      ("wq1", [D, 256]), ("wk1", [D, 256]), ("wv1", [D, D]),
                      ("wvp", [D, D]), ("wo", [D, D])):
        d[nm] = nc.dram_tensor(nm, shape, F32, kind="ExternalInput").ap()
    return d


def emit_l1(nc, p, cm, tapmap, ins, x1, y, own_only=False, x1_rk=None,
            cc=None):
    (Wq, bq, Wk, bk, Wv, bv, slot_init, w1, b1, w2, b2, tau, Wih, Whh, bih,
     bhh, wq1, bq1, wk1, bk1, wv1, bv1, ln_g, ln_b, wvp, bvp, wo, bo) = (
        ins["Wq"], ins["bq"], ins["Wk"], ins["bk"], ins["Wv"], ins["bv"],
        ins["slot_init"], ins["w1"], ins["b1"], ins["w2"], ins["b2"],
        ins["tau"], ins["Wih"], ins["Whh"], ins["bih"], ins["bhh"],
        ins["wq1"], ins["bq1"], ins["wk1"], ins["bk1"], ins["wv1"],
        ins["bv1"], ins["ln_g"], ins["ln_b"], ins["wvp"], ins["bvp"],
        ins["wo"], ins["bo"])
    qx, qw = "qx", "qw"
    NLT = 16 if own_only else 32
    NWND = 4 if own_only else 8

    def palloc(nm, shape, dtype, key=None):
        return p.palloc(nc, nm, shape, dtype, key=key)

    bstg = [nc.alloc_sbuf_tensor(f"bstg{j}", [24, 128], F32) for j in range(2)]
    g_pBC1, pBC1 = palloc("pBC1", [128, 512], F32)
    _lc_n = [0]

    def load_cols(dst, src_1d, n_chunks, key):
        j = _lc_n[0] & 1
        _lc_n[0] += 1
        sk = (f"bstg{j}",)
        p.dma("qb", lambda e, src_1d=src_1d, n_chunks=n_chunks, j=j: e.dma_start(
            out=bstg[j][0:n_chunks, :],
            in_=src_1d.rearrange("(c p) -> c p", p=128)),
            writes=[sk])
        p.op("tensor", lambda e, n_chunks=n_chunks, j=j: e.matmul(
            pBC1[:, 0:n_chunks], bstg[j][0:n_chunks, :],
            cm.identf[0:n_chunks, 0:n_chunks], start=True, stop=True),
            reads=[sk, ("c_identf",)], writes=[("pBC1",)])
        p.op("vector", lambda e, dst=dst, n_chunks=n_chunks: e.tensor_copy(
            out=dst[:, 0:n_chunks], in_=pBC1[:, 0:n_chunks]),
            reads=[("pBC1",)], writes=[key])

    bq_s = nc.alloc_sbuf_tensor("bq_s", [128, 8], F32)
    bk_s = nc.alloc_sbuf_tensor("bk_s", [128, 8], F32)
    b1_s = nc.alloc_sbuf_tensor("b1_s", [128, 4], F32)
    bih_s = nc.alloc_sbuf_tensor("bih_s", [128, 24], F32)
    bhh_s = nc.alloc_sbuf_tensor("bhh_s", [128, 24], F32)
    bq1_s = nc.alloc_sbuf_tensor("bq1_s", [128, 2], F32)
    bk1_s = nc.alloc_sbuf_tensor("bk1_s", [128, 2], F32)
    bv1_s = nc.alloc_sbuf_tensor("bv1_s", [128, 8], F32)
    lng_s = nc.alloc_sbuf_tensor("lng_s", [128, 8], F32)
    lnb_s = nc.alloc_sbuf_tensor("lnb_s", [128, 8], F32)
    bvp_s = nc.alloc_sbuf_tensor("bvp_s", [128, 8], F32)
    b2r = nc.alloc_sbuf_tensor("b2r", [1, 64], F32)
    taur = nc.alloc_sbuf_tensor("taur", [1, 4], F32)
    bvr = nc.alloc_sbuf_tensor("bvr", [1, D], F32)
    bor = nc.alloc_sbuf_tensor("bor", [1, D], F32)
    s0f = nc.alloc_sbuf_tensor("s0f", [64, D], F32)
    load_cols(bq_s, bq, 8, ("bq_s",))
    load_cols(bk_s, bk, 8, ("bk_s",))
    load_cols(b1_s, b1, 4, ("b1_s",))
    load_cols(bih_s, bih, 24, ("bih_s",))
    load_cols(bhh_s, bhh, 24, ("bhh_s",))
    load_cols(bq1_s, bq1, 2, ("bq1_s",))
    load_cols(bk1_s, bk1, 2, ("bk1_s",))
    load_cols(bv1_s, bv1, 8, ("bv1_s",))
    load_cols(lng_s, ln_g, 8, ("lng_s",))
    load_cols(lnb_s, ln_b, 8, ("lnb_s",))
    load_cols(bvp_s, bvp, 8, ("bvp_s",))
    p.dma("qb", lambda e: e.dma_start(out=b2r[:],
                                    in_=b2.rearrange("(a b) -> a b", a=1)),
          writes=[("b2r",)])
    p.dma("qb", lambda e: e.dma_start(out=taur[:, 0:1],
                                    in_=tau.rearrange("(a b) -> a b", a=1)),
          writes=[("taur",)])
    p.dma("qb", lambda e: e.dma_start(out=bvr[:],
                                    in_=bv.rearrange("(a b) -> a b", a=1)),
          writes=[("bvr",)])
    p.dma("qb", lambda e: e.dma_start(out=bor[:],
                                    in_=bo.rearrange("(a b) -> a b", a=1)),
          writes=[("bor",)])
    p.dma("qb", lambda e: e.dma_start(out=s0f[:], in_=slot_init), writes=[("s0f",)])

    bvb = nc.alloc_sbuf_tensor("bvb", [128, D], F32)
    bob = nc.alloc_sbuf_tensor("bob", [128, D], F32)
    b2b = nc.alloc_sbuf_tensor("b2b", [128, 64], F32)
    sbc = nc.alloc_sbuf_tensor("sbc", [128, 1], F32)
    for half in range(2):
        p.op("tensor", lambda e, half=half: e.matmul(
            pBC1[:], cm.onesf[0:1, 0:128], bvr[:, half * 512:(half + 1) * 512],
            start=True, stop=True),
            reads=[("bvr",), ("c_onesf",)], writes=[("pBC1",)])
        p.op("vector", lambda e, half=half: e.tensor_copy(
            out=bvb[:, half * 512:(half + 1) * 512], in_=pBC1[:]),
            reads=[("pBC1",)], writes=[("bvb",)])
        p.op("tensor", lambda e, half=half: e.matmul(
            pBC1[:], cm.onesf[0:1, 0:128], bor[:, half * 512:(half + 1) * 512],
            start=True, stop=True),
            reads=[("bor",), ("c_onesf",)], writes=[("pBC1",)])
        p.op("vector", lambda e, half=half: e.tensor_copy(
            out=bob[:, half * 512:(half + 1) * 512], in_=pBC1[:]),
            reads=[("pBC1",)], writes=[("bob",)])
    p.op("tensor", lambda e: e.matmul(
        pBC1[:, 0:64], cm.onesf[0:1, 0:128], b2r[:], start=True, stop=True),
        reads=[("b2r",), ("c_onesf",)], writes=[("pBC1",)])
    p.op("vector", lambda e: e.tensor_copy(out=b2b[:], in_=pBC1[:, 0:64]),
         reads=[("pBC1",)], writes=[("b2b",)])
    p.op("scalar", lambda e: e.activation(taur[:, 1:2], taur[:, 0:1], AF.Abs),
         reads=[("taur",)], writes=[("taur",)], hazard=True)
    p.op("vector", lambda e: e.tensor_scalar_add(taur[:, 2:3], taur[:, 1:2], 0.1),
         reads=[("taur",)], writes=[("taur",)], hazard=True)
    p.op("vector", lambda e: e.reciprocal(taur[:, 3:4], taur[:, 2:3]),
         reads=[("taur",)], writes=[("taur",)], hazard=True)
    p.op("tensor", lambda e: e.matmul(
        pBC1[:, 0:1], cm.onesf[0:1, 0:128], taur[:, 3:4], start=True, stop=True),
        reads=[("taur",), ("c_onesf",)], writes=[("pBC1",)])
    p.op("vector", lambda e: e.tensor_copy(out=sbc[:], in_=pBC1[:, 0:1]),
         reads=[("pBC1",)], writes=[("sbc",)])
    g_pBC1.__exit__(None, None, None)
    p.barrier()

    if own_only:
        g_x2acc = nc.sbuf_tensor("x2acc", [128, 16 * D], F32, side="right")
        x2acc = g_x2acc.__enter__()
    g_wqa = nc.sbuf_tensor("wqa", [128, 8 * D], BF16, side="right")
    wqa = g_wqa.__enter__()
    p.dma(qw, lambda e: e.dma_start(
        out=wqa[:].rearrange("p (dc n) -> p dc n", dc=8),
        in_=Wq.rearrange("(dc p) n -> p dc n", p=128)),
        writes=[("wqa",)])
    alphaTm = nc.alloc_sbuf_tensor("alphaTm", [128, 32 * 64], BF16)
    alphaT = nc.alloc_sbuf_tensor("alphaT", [64, SEQ], BF16)
    g_xn1T_own = nc.sbuf_tensor("xn1T_own", [128, 8 * HALF], BF16, side="right")
    xn1T_own = g_xn1T_own.__enter__()
    if not own_only:
        g_xn1T_par = nc.sbuf_tensor("xn1T_par", [128, 8 * HALF], BF16,
                                    side="right")
        xn1T_par = g_xn1T_par.__enter__()
    else:
        g_xn1T_par = xn1T_par = None
    g_xn1_tm = nc.sbuf_tensor("xn1_tm", [128, NLT * D], BF16, side="right")
    xn1_tm = g_xn1_tm.__enter__()

    g_pT0, pT0 = palloc("pT0", [128, 128], BF16, key=("pT", 0))
    g_pT1, pT1 = palloc("pT1", [128, 128], BF16, key=("pT", 1))
    lng = ln_tiles(nc, p, cm, "ln1", x1, NLT, 16, xn1T_own, xn1T_par,
                   [pT0, pT1], qx, invn=None, out_tm=xn1_tm, src_rk=x1_rk)
    free_guards(lng)
    g_pT1.__exit__(None, None, None)
    g_pT0.__exit__(None, None, None)
    p.barrier()
    tapmap["xn1T_own"] = (xn1T_own, [("xno", i) for i in range(4)])
    tapmap["xn1_tm"] = (xn1_tm, [("xtm", i) for i in range(8)])

    def xn1_rhs(dc, t0, w):
        if t0 < HALF:
            return (xn1T_own[:, dc * HALF + t0:dc * HALF + t0 + w],
                    ("xno", t0 // 512))
        t = t0 - HALF
        return (xn1T_par[:, dc * HALF + t:dc * HALF + t + w], ("xnp", t // 512))


    mid1 = []

    def m1a(nm, shape, dtype):
        g = nc.sbuf_tensor(nm, shape, dtype, side="right")
        t = g.__enter__()
        mid1.append(g)
        return t

    w1t = m1a("w1t", [128, 4 * 512], BF16)
    w1b = m1a("w1b", [128, 4 * 512], BF16)
    w2s = m1a("w2s", [128, 4 * 64], BF16)
    p.dma(qw, lambda e: e.dma_start(
        out=w1t[:].rearrange("p (dc n) -> p dc n", dc=4)
        .rearrange("p dc n -> p (dc n)")
        .rearrange("p (dc n) -> p dc n", dc=4),
        in_=w1[0:D, :].rearrange("(dc p) n -> p dc n", p=128)[:, 0:4, :])
        if False else e.dma_start(
        out=w1t[:].rearrange("p (dc n) -> p dc n", dc=4),
        in_=w1[0:512, :].rearrange("(dc p) n -> p dc n", p=128))
        if False else e.dma_start(
        out=w1t[:].rearrange("p (dc n) -> p dc n", dc=4),
        in_=w1[0:D, :].rearrange("(dc p) n -> p dc n", p=128)[:, 0:4, :]),
        writes=[("w1t",)])
    p.dma(qw, lambda e: e.dma_start(
        out=w1b[:].rearrange("p (dc n) -> p dc n", dc=4),
        in_=w1[D:2 * D, :].rearrange("(dc p) n -> p dc n", p=128)[:, 0:4, :]),
        writes=[("w1b",)])
    p.dma(qw, lambda e: e.dma_start(
        out=w2s[:].rearrange("p (hc n) -> p hc n", hc=4),
        in_=w2.rearrange("(hc p) n -> p hc n", p=128)),
        writes=[("w2s",)])

    smv = m1a("smv", [128, 8], BF16)
    hbias = m1a("hbias", [128, 4], F32)
    s0b = nc.alloc_sbuf_tensor("s0b", [64, D], BF16)
    o64 = nc.alloc_sbuf_tensor("o64", [64, 1], BF16)
    g_pSM, pSM = palloc("pSM", [128, 2], F32)
    p.op("vector", lambda e: e.memset(o64[:], 1.0 / 64.0), writes=[("o64",)])
    p.op("vector", lambda e: e.tensor_copy(out=s0b[:], in_=s0f[:]),
         reads=[("s0f",)], writes=[("s0b",)])
    for c in range(8):
        p.op("tensor", lambda e, c=c: e.matmul(
            pSM[:, 0:1], s0b[0:64, c * 128:(c + 1) * 128], o64[:],
            start=True, stop=True),
            reads=[("s0b",), ("o64",)], writes=[("pSM",)])
        p.op("vector", lambda e, c=c: e.tensor_copy(out=smv[:, c:c + 1],
                                                    in_=pSM[:, 0:1]),
             reads=[("pSM",)], writes=[("smv",)])
    for jc in range(4):
        for dc in range(8):
            p.op("tensor", lambda e, jc=jc, dc=dc: e.matmul(
                pSM[:, 1:2],
                w1b[:, dc * 512 + jc * 128:dc * 512 + jc * 128 + 128]
                if dc < 4 else
                w1b[:, (dc - 4) * 512 + jc * 128:(dc - 4) * 512 + jc * 128 + 128],
                smv[:, dc:dc + 1], start=(dc == 0), stop=(dc == 7)),
                reads=[("w1b",), ("smv",)], writes=[("pSM",)])
        p.op("vector", lambda e, jc=jc: e.tensor_add(
            hbias[:, jc:jc + 1], b1_s[:, jc:jc + 1], pSM[:, 1:2]),
            reads=[("pSM",), ("b1_s",)], writes=[("hbias",)])

    hT = m1a("hT", [128, 4 * 512], BF16)
    logt = m1a("logt", [128, 64], F32)
    m8L = m1a("m8L", [128, 8], F32)
    maskL = m1a("maskL", [128, 64], F32)
    expL = m1a("expL", [128, 64], F32)
    nbt = m1a("nbt", [128, 2], F32)
    g_pH0, pH0 = palloc("pH0", [128, 512], F32)
    g_pH1, pH1 = palloc("pH1", [128, 512], F32)
    g_pL, pL = palloc("pL", [128, 64], F32)
    g_pAT, pAT = palloc("pAT", [64, 128], BF16)

    for wnd in range(NWND):
        t0 = wnd * 512
        for hc in range(4):
            ph, phk = (pH0, ("pH0",)) if hc % 2 == 0 else (pH1, ("pH1",))
            for dc in range(8):
                rhs, rk = xn1_rhs(dc, t0, 512)
                wcol = dc * 512 + hc * 128 if dc < 4 else (dc - 4) * 512 + hc * 128
                p.op("tensor", lambda e, hc=hc, dc=dc, rhs=rhs, ph=ph: e.matmul(
                    ph[:], w1t[:, (dc % 4) * 512 + hc * 128:(dc % 4) * 512 + hc * 128 + 128]
                    if False else w1t[:, hc * 512 + 0:hc * 512 + 512][:, 0:128]
                    if False else w1t[:, hc * 128 + 0:hc * 128 + 128],
                    rhs, start=(dc == 0), stop=(dc == 7)),
                    reads=[("w1t",), rk], writes=[phk])
            p.op("scalar", lambda e, hc=hc, ph=ph: e.activation(
                hT[:, hc * 512:(hc + 1) * 512], ph[:], AF.Gelu,
                bias=hbias[:, hc:hc + 1]),
                reads=[phk, ("hbias",)], writes=[("hT", hc)])
        for it in range(4):
            ti = wnd * 4 + it
            for hc in range(4):
                p.op("tensor", lambda e, hc=hc, it=it: e.matmul(
                    pL[:], hT[:, hc * 512 + it * 128:hc * 512 + it * 128 + 128],
                    w2s[:, hc * 64:(hc + 1) * 64],
                    start=(hc == 0), stop=(hc == 3)),
                    reads=[("hT", hc), ("w2s",)], writes=[("pL",)])
            p.op("vector", lambda e: e.tensor_add(logt[:], pL[:], b2b[:]),
                 reads=[("pL",), ("b2b",)], writes=[("logt",)])
            p.op("vector", lambda e: e.memset(m8L[:], -1e30), writes=[("m8L",)])
            p.op("vector", lambda e: e.max(out=m8L[:], in_=logt[:]),
                 reads=[("logt",), ("m8L",)], writes=[("m8L",)], hazard=True)
            p.op("vector", lambda e: e.tensor_scalar(
                maskL[:], logt[:], m8L[:, 7:8], None, ALU.is_ge),
                reads=[("logt",), ("m8L",)], writes=[("maskL",)])
            p.op("vector", lambda e: e.scalar_tensor_tensor(
                nbt[:, 0:1], m8L[:, 0:1], -1.0, sbc[:], ALU.mult, ALU.mult),
                reads=[("m8L",), ("sbc",)], writes=[("nbt",)])
            p.op("scalar", lambda e: e.activation(
                expL[:], logt[:], AF.Exp, bias=nbt[:, 0:1], scale=sbc[:]),
                reads=[("logt",), ("nbt",), ("sbc",)], writes=[("expL",)])
            p.op("vector", lambda e: e.tensor_mul(expL[:], expL[:], maskL[:]),
                 reads=[("expL",), ("maskL",)], writes=[("expL",)], hazard=True)
            p.op("vector", lambda e: e.reduce_sum(nbt[:, 1:2], expL[:], axis=X),
                 reads=[("expL",)], writes=[("nbt",)], hazard=True)
            p.op("vector", lambda e: e.reciprocal(nbt[:, 1:2], nbt[:, 1:2]),
                 reads=[("nbt",)], writes=[("nbt",)], hazard=True)
            p.op("vector", lambda e, ti=ti: e.tensor_scalar_mul(
                alphaTm[:, ti * 64:(ti + 1) * 64], expL[:], nbt[:, 1:2]),
                reads=[("expL",), ("nbt",)], writes=[("aTm", ti)])
            p.op("tensor", lambda e, ti=ti: e.transpose(
                pAT[:], alphaTm[:, ti * 64:(ti + 1) * 64], cm.identb[:]),
                reads=[("aTm", ti), ("c_identb",)], writes=[("pAT",)])
            p.op("vector", lambda e, ti=ti: e.tensor_copy(
                out=alphaT[0:64, ti * 128:(ti + 1) * 128], in_=pAT[:]),
                reads=[("pAT",)], writes=[("aT1", ti)])
    tapmap["alphaTm"] = (alphaTm, [("aTm", t) for t in range(NLT)])

    slotin = nc.alloc_sbuf_tensor("slotin", [64, D], F32)
    denom = nc.alloc_sbuf_tensor("denom", [64, 2], F32)
    ob1 = nc.alloc_sbuf_tensor("ob1", [128, 1], BF16)
    g_pSI0, pSI0 = palloc("pSI0", [64, 512], F32)
    g_pSI1, pSI1 = palloc("pSI1", [64, 512], F32)
    p.op("vector", lambda e: e.memset(ob1[:], 1.0), writes=[("ob1",)])
    NST = NLT
    for dh in range(2):
        ps, psk = (pSI0, ("pSI0",)) if dh == 0 else (pSI1, ("pSI1",))
        for ti in range(NST):
            p.op("tensor", lambda e, dh=dh, ti=ti, ps=ps: e.matmul(
                ps[:], alphaTm[:, ti * 64:(ti + 1) * 64],
                xn1_tm[:, ti * D + dh * 512:ti * D + dh * 512 + 512],
                start=(ti == 0), stop=(ti == NST - 1)),
                reads=[("aTm", ti), ("xtm", ti // 4)], writes=[psk])
        p.op("vector", lambda e, dh=dh, ps=ps: e.tensor_copy(
            out=slotin[:, dh * 512:(dh + 1) * 512], in_=ps[:]),
            reads=[psk], writes=[("slotin",)])
    for ti in range(NST):
        p.op("tensor", lambda e, ti=ti: e.matmul(
            pSM[:, 0:1][0:64, :], alphaTm[:, ti * 64:(ti + 1) * 64], ob1[:],
            start=(ti == 0), stop=(ti == NST - 1)),
            reads=[("aTm", ti), ("ob1",)], writes=[("pSM",)])
    if own_only:
        ccstg = nc.alloc_sbuf_tensor("ccstg", [64, 1032], F32)
        p.op("vector", lambda e: e.tensor_copy(out=ccstg[:, 0:D],
                                               in_=slotin[:]),
             reads=[("slotin",)], writes=[("ccstg",)])
        p.op("vector", lambda e: e.tensor_copy(out=ccstg[:, D:D + 1],
                                               in_=pSM[0:64, 0:1]),
             reads=[("pSM",)], writes=[("ccstg",)])
        p.dma(qx, lambda e: e.dma_start(out=cc["cc_in"], in_=ccstg[:, 0:1025]),
              reads=[("ccstg",)], writes=[("cc_in",)])
        p.dma(cc["queue"], lambda e: e.collective_compute(
            "AllReduce", ALU.add, cc["groups"], [cc["cc_in"].opt()],
            [cc["cc_out"].opt()]),
            reads=[("cc_in",)], writes=[("cc_out",)], inc=1)
    else:
        p.op("vector", lambda e: e.tensor_scalar_add(denom[:, 0:1],
                                                     pSM[0:64, 0:1], 1e-8),
             reads=[("pSM",)], writes=[("denom",)], hazard=True)
        p.op("vector", lambda e: e.reciprocal(denom[:, 1:2], denom[:, 0:1]),
             reads=[("denom",)], writes=[("denom",)], hazard=True)
        p.op("vector", lambda e: e.tensor_scalar_mul(slotin[:], slotin[:],
                                                     denom[:, 1:2]),
             reads=[("slotin",), ("denom",)], writes=[("slotin",)],
             hazard=True)
    tapmap["slotin"] = (slotin, [("slotin",)])

    for g in (g_pSI1, g_pSI0, g_pAT, g_pL, g_pH1, g_pH0, g_pSM):
        g.__exit__(None, None, None)
    for g in reversed(mid1):
        g.__exit__(None, None, None)
    g_xn1_tm.__exit__(None, None, None)
    if g_xn1T_par is not None:
        g_xn1T_par.__exit__(None, None, None)
    p.barrier()

    if not own_only:
        g_x2acc = nc.sbuf_tensor("x2acc", [128, 16 * D], F32, side="right")
        x2acc = g_x2acc.__enter__()
    g_wka = nc.sbuf_tensor("wka", [128, 8 * D], BF16, side="right")
    wka = g_wka.__enter__()
    g_wva = nc.sbuf_tensor("wva", [128, 8 * D], BF16, side="right")
    wva = g_wva.__enter__()
    for wt, wn, wsrc in ((wka, "wka", Wk), (wva, "wva", Wv)):
        p.dma(qw, lambda e, wt=wt, wsrc=wsrc: e.dma_start(
            out=wt[:].rearrange("p (dc n) -> p dc n", dc=8),
            in_=wsrc.rearrange("(dc p) n -> p dc n", p=128)),
            writes=[(wn,)])
    att_guards = []

    def ata(nm, shape, dtype):
        g = nc.sbuf_tensor(nm, shape, dtype, side="right")
        t = g.__enter__()
        att_guards.append(g)
        return t

    q1T = ata("q1T", [128, 8 * 256], BF16)
    k1T = ata("k1T", [128, 8 * 256], BF16)
    v1t = ata("v1t", [128, 2 * 16 * 65], BF16)
    sq1 = ata("sq1", [128, 256], BF16)
    nrm1 = ata("nrm1", [1, 8], F32)
    negM1 = ata("negM1", [128, 1], F32)
    ptb1 = ata("ptb1", [128, 64], BF16)
    ssum1 = ata("ssum1", [128, 2], F32)
    g_pQ1, pQ1 = palloc("pQ1", [128, 512], F32)
    g_pK1, pK1 = palloc("pK1", [128, 512], F32)
    g_pV1a, pV1a = palloc("pV1a", [128, 512], F32)
    g_pV1b, pV1b = palloc("pV1b", [128, 512], F32)
    g_pN1, pN1 = palloc("pN1", [1, 256], F32)
    g_pM1, pM1 = palloc("pM1", [128, 1], F32)
    g_pSL, pSL = palloc("pSL", [128, 64], F32)
    g_pOL, pOL = palloc("pOL", [128, 65], F32)
    v1_r = v1t[:].rearrange("p (i hh c) -> p i hh c", i=2, hh=16)
    p.op("vector", lambda e: e.memset(v1_r[:, :, :, 64:65], 1.0),
         writes=[("v1o",)])

    for g4 in range(8):
        t0 = g4 * 256
        for tl in range(2):
            ti = g4 * 2 + tl
            p.dma(qx, lambda e, ti=ti: e.dma_start(
                out=x2acc[:, ti * D:(ti + 1) * D],
                in_=x1[ti * 128:(ti + 1) * 128, :]),
                reads=(x1_rk(ti) if x1_rk else []), writes=[("x2d", ti)])
            p.op("vector", lambda e, ti=ti: e.tensor_add(
                x2acc[:, ti * D:(ti + 1) * D], x2acc[:, ti * D:(ti + 1) * D],
                bob[:]),
                reads=[("x2d", ti), ("bob",)], writes=[("x2", ti)])
        for w_, wk_, ps_, psk_, dst, dk, bias_s, bk_ in (
                (wqa, "wqa", pQ1, ("pQ1",), q1T, "q1T", bq_s, "bq_s"),
                (wka, "wka", pK1, ("pK1",), k1T, "k1T", bk_s, "bk_s")):
            for oc in range(8):
                for dc in range(8):
                    rhs, rk = xn1_rhs(dc, t0, 256)
                    p.op("tensor", lambda e, oc=oc, dc=dc, rhs=rhs, w_=w_,
                         wk_=wk_, ps_=ps_: e.matmul(
                        ps_[:, 0:256],
                        w_[:, dc * D + oc * 128:dc * D + oc * 128 + 128],
                        rhs, start=(dc == 0), stop=(dc == 7)),
                        reads=[(wk_,), rk], writes=[psk_])
                p.op("scalar", lambda e, oc=oc, ps_=ps_, dst=dst, dk=dk,
                     bias_s=bias_s, bk_=bk_: e.activation(
                    dst[:, oc * 256:(oc + 1) * 256], ps_[:, 0:256], AF.Identity,
                    bias=bias_s[:, oc:oc + 1]),
                    reads=[psk_, (bk_,)], writes=[(dk, oc)])
        for tl in range(2):
            ti = g4 * 2 + tl
            for half in range(2):
                pv, pvk = (pV1a, ("pV1a",)) if half == 0 else (pV1b, ("pV1b",))
                for dc in range(8):
                    lhs, rk = xn1_rhs(dc, ti * 128, 128)
                    p.op("tensor", lambda e, half=half, dc=dc, lhs=lhs, pv=pv:
                         e.matmul(
                        pv[:], lhs,
                        wva[:, dc * D + half * 512:dc * D + half * 512 + 512],
                        start=(dc == 0), stop=(dc == 7)),
                        reads=[("wva",), rk], writes=[pvk])
                p.op("vector", lambda e, tl=tl, half=half, pv=pv: e.tensor_add(
                    v1_r[:, tl, half * 8:(half + 1) * 8, 0:64],
                    pv[:].rearrange("p (hh c) -> p hh c", hh=8),
                    bvb[:, half * 512:(half + 1) * 512].rearrange(
                        "p (hh c) -> p hh c", hh=8)),
                    reads=[pvk, ("bvb",)], writes=[("v1", tl, half)])
        qk1 = [("q1T", oc) for oc in range(8)]
        kk1 = [("k1T", oc) for oc in range(8)]
        for w_, wk_, col in ((q1T, qk1, 0), (k1T, kk1, 1)):
            for oc in range(8):
                p.op("vector", lambda e, oc=oc, w_=w_, wk_=wk_: e.tensor_mul(
                    sq1[:], w_[:, oc * 256:(oc + 1) * 256],
                    w_[:, oc * 256:(oc + 1) * 256]),
                    reads=wk_, writes=[("sq1",)])
                p.op("tensor", lambda e, oc=oc: e.matmul(
                    pN1[:], cm.onesb[:, 0:1], sq1[:],
                    start=(oc == 0), stop=(oc == 7)),
                    reads=[("sq1",), ("c_onesb",)], writes=[("pN1",)])
            p.op("vector", lambda e, col=col: e.reduce_max(
                nrm1[:, col:col + 1], pN1[:], axis=X),
                reads=[("pN1",)], writes=[("nrm1",)], hazard=True)
        p.op("vector", lambda e: e.tensor_mul(nrm1[:, 2:3], nrm1[:, 0:1],
                                              nrm1[:, 1:2]),
             reads=[("nrm1",)], writes=[("nrm1",)], hazard=True)
        p.op("scalar", lambda e: e.activation(nrm1[:, 3:4], nrm1[:, 2:3],
                                              AF.Sqrt),
             reads=[("nrm1",)], writes=[("nrm1",)], hazard=True)
        p.op("vector", lambda e: e.tensor_scalar_mul(nrm1[:, 3:4],
                                                     nrm1[:, 3:4], -SCALE),
             reads=[("nrm1",)], writes=[("nrm1",)], hazard=True)
        p.op("tensor", lambda e: e.matmul(
            pM1[:], cm.onesf[0:1, 0:128], nrm1[:, 3:4], start=True, stop=True),
            reads=[("nrm1",), ("c_onesf",)], writes=[("pM1",)])
        p.op("vector", lambda e: e.tensor_copy(out=negM1[:], in_=pM1[:]),
             reads=[("pM1",)], writes=[("negM1",)])
        for tl in range(2):
            ti = g4 * 2 + tl
            for hh in range(16):
                oc = hh // 2
                rb = 64 * (hh % 2)
                c0 = oc * 256 + tl * 128
                for blk in range(2):
                    p.op("tensor", lambda e, oc=oc, rb=rb, blk=blk, c0=c0:
                         e.matmul(
                        pSL[blk * 64:(blk + 1) * 64, :],
                        k1T[rb:rb + 64, c0 + blk * 64:c0 + blk * 64 + 64],
                        q1T[rb:rb + 64, c0 + blk * 64:c0 + blk * 64 + 64],
                        start=True, stop=True, tile_position=(rb, blk * 64)),
                        reads=[("q1T", oc), ("k1T", oc)], writes=[("pSL",)])
                p.op("scalar", lambda e: e.activation(
                    ptb1[:], pSL[:], AF.Exp, bias=negM1[:, 0:1], scale=SCALE),
                    reads=[("pSL",), ("negM1",)], writes=[("ptb1",)])
                for blk in range(2):
                    p.op("tensor", lambda e, tl=tl, hh=hh, blk=blk: e.matmul(
                        pOL[blk * 64:(blk + 1) * 64, :],
                        ptb1[blk * 64:(blk + 1) * 64, :],
                        v1t[blk * 64:(blk + 1) * 64,
                            tl * 1040 + hh * 65:tl * 1040 + hh * 65 + 65],
                        start=True, stop=True, tile_position=(blk * 64, blk * 64)),
                        reads=[("ptb1",), ("v1", tl, hh // 8), ("v1o",)],
                        writes=[("pOL",)])
                p.op("scalar", lambda e: e.copy(out=ssum1[:, 0:1],
                                                in_=pOL[:, 64:65]),
                     reads=[("pOL",)], writes=[("ssum1",)], hazard=True)
                p.op("vector", lambda e: e.reciprocal(ssum1[:, 1:2],
                                                      ssum1[:, 0:1]),
                     reads=[("ssum1",)], writes=[("ssum1",)], hazard=True)
                p.op("vector", lambda e, ti=ti, hh=hh: e.scalar_tensor_tensor(
                    x2acc[:, ti * D + hh * 64:ti * D + hh * 64 + 64],
                    pOL[:, 0:64], ssum1[:, 1:2],
                    x2acc[:, ti * D + hh * 64:ti * D + hh * 64 + 64],
                    ALU.mult, ALU.add),
                    reads=[("pOL",), ("ssum1",), ("x2", ti)],
                    writes=[("x2", ti)], hazard=True)
    tapmap["x2acc"] = (x2acc, [("x2", i) for i in range(16)])

    for g in (g_pOL, g_pSL, g_pM1, g_pN1, g_pV1b, g_pV1a, g_pK1, g_pQ1):
        g.__exit__(None, None, None)
    for g in reversed(att_guards):
        g.__exit__(None, None, None)
    g_wva.__exit__(None, None, None)
    g_wka.__exit__(None, None, None)
    if own_only:
        g_xn1T_own.__exit__(None, None, None)
    g_wqa.__exit__(None, None, None)
    p.barrier()

    if own_only:
        p.dma(qx, lambda e: e.dma_start(out=ccstg[:, 0:1025], in_=cc["cc_out"]),
              reads=[("cc_out",)], writes=[("ccstg",)])
        p.op("vector", lambda e: e.tensor_copy(out=slotin[:],
                                               in_=ccstg[:, 0:D]),
             reads=[("ccstg",)], writes=[("slotin",)])
        p.op("vector", lambda e: e.tensor_scalar_add(
            denom[:, 0:1], ccstg[:, D:D + 1], 1e-8),
            reads=[("ccstg",)], writes=[("denom",)], hazard=True)
        p.op("vector", lambda e: e.reciprocal(denom[:, 1:2], denom[:, 0:1]),
             reads=[("denom",)], writes=[("denom",)], hazard=True)
        p.op("vector", lambda e: e.tensor_scalar_mul(slotin[:], slotin[:],
                                                     denom[:, 1:2]),
             reads=[("slotin",), ("denom",)], writes=[("slotin",)], hazard=True)
    mid2 = []

    def m2a(nm, shape, dtype):
        g = nc.sbuf_tensor(nm, shape, dtype, side="right")
        t = g.__enter__()
        mid2.append(g)
        return t

    siT = m2a("siT", [128, 8 * 64], BF16)
    s0T = m2a("s0T", [128, 8 * 64], BF16)
    sib = m2a("sib", [64, D], BF16)
    giT = m2a("giT", [128, 24 * 64], F32)
    ghT = m2a("ghT", [128, 24 * 64], F32)
    g_pTR, pTR = palloc("pTR", [128, 128], BF16)
    p.op("vector", lambda e: e.tensor_copy(out=sib[:], in_=slotin[:]),
         reads=[("slotin",)], writes=[("sib",)])
    for c in range(8):
        p.op("tensor", lambda e, c=c: e.transpose(
            pTR[:, 0:64], sib[0:64, c * 128:(c + 1) * 128],
            cm.identb[0:64, 0:64]),
            reads=[("sib",), ("c_identb",)], writes=[("pTR",)])
        p.op("vector", lambda e, c=c: e.tensor_copy(
            out=siT[:, c * 64:(c + 1) * 64], in_=pTR[:, 0:64]),
            reads=[("pTR",)], writes=[("siT",)])
        p.op("tensor", lambda e, c=c: e.transpose(
            pTR[:, 0:64], s0b[0:64, c * 128:(c + 1) * 128],
            cm.identb[0:64, 0:64]),
            reads=[("s0b",), ("c_identb",)], writes=[("pTR",)])
        p.op("vector", lambda e, c=c: e.tensor_copy(
            out=s0T[:, c * 64:(c + 1) * 64], in_=pTR[:, 0:64]),
            reads=[("pTR",)], writes=[("s0T",)])

    wblk = [m2a(f"wblk{j}", [128, D], BF16) for j in range(2)]
    wtT = [m2a(f"wtT{j}", [128, 128], BF16) for j in range(2)]
    g_pWT, pWT = palloc("pWT", [128, 128], BF16)
    g_pGI, pGI = palloc("pGI", [128, 64], F32)
    for wsrc, dstT, bias_s, bk_, dk in ((Wih, giT, bih_s, "bih_s", "giT"),
                                        (Whh, ghT, bhh_s, "bhh_s", "ghT")):
        rhsT = siT if dk == "giT" else s0T
        rhk = ("siT",) if dk == "giT" else ("s0T",)
        for jc in range(24):
            jb = jc % 2
            p.dma(qw, lambda e, jc=jc, jb=jb, wsrc=wsrc: e.dma_start(
                out=wblk[jb][:], in_=wsrc[jc * 128:(jc + 1) * 128, :]),
                writes=[("wblk", jb)], tag=f"{dk}_blk{jc}")
            for dc in range(8):
                db = dc % 2
                p.op("tensor", lambda e, jb=jb, dc=dc: e.transpose(
                    pWT[:], wblk[jb][:, dc * 128:(dc + 1) * 128], cm.identb[:]),
                    reads=[("wblk", jb), ("c_identb",)], writes=[("pWT",)])
                p.op("vector", lambda e, db=db: e.tensor_copy(
                    out=wtT[db][:], in_=pWT[:]),
                    reads=[("pWT",)], writes=[("wtTx", db)])
                p.op("tensor", lambda e, dc=dc, db=db, rhsT=rhsT, rhk=rhk:
                     e.matmul(
                    pGI[:], wtT[db][:], rhsT[:, dc * 64:(dc + 1) * 64],
                    start=(dc == 0), stop=(dc == 7)),
                    reads=[("wtTx", db), rhk], writes=[("pGI",)])
            p.op("vector", lambda e, jc=jc, dstT=dstT, bias_s=bias_s, bk_=bk_,
                 dk=dk: e.tensor_scalar_add(
                dstT[:, jc * 64:(jc + 1) * 64], pGI[:], bias_s[:, jc:jc + 1]),
                reads=[("pGI",), (bk_,)], writes=[(dk, jc)])

    rz = m2a("rz", [128, 16 * 64], F32)
    snT = m2a("snT", [128, 8 * 64], F32)
    snTb = m2a("snTb", [128, 8 * 64], BF16)
    tmp1 = m2a("tmp1", [128, 64], F32)
    for c in range(16):
        p.op("vector", lambda e, c=c: e.tensor_add(
            tmp1[:], giT[:, c * 64:(c + 1) * 64], ghT[:, c * 64:(c + 1) * 64]),
            reads=[("giT", c), ("ghT", c)], writes=[("tmp1",)], hazard=True)
        p.op("scalar", lambda e, c=c: e.activation(
            rz[:, c * 64:(c + 1) * 64], tmp1[:], AF.Sigmoid),
            reads=[("tmp1",)], writes=[("rz", c)])
    for c in range(8):
        p.op("vector", lambda e, c=c: e.tensor_mul(
            tmp1[:], rz[:, c * 64:(c + 1) * 64],
            ghT[:, (16 + c) * 64:(17 + c) * 64]),
            reads=[("rz", c), ("ghT", 16 + c)], writes=[("tmp1",)], hazard=True)
        p.op("vector", lambda e, c=c: e.tensor_add(
            tmp1[:], tmp1[:], giT[:, (16 + c) * 64:(17 + c) * 64]),
            reads=[("tmp1",), ("giT", 16 + c)], writes=[("tmp1",)], hazard=True)
        p.op("scalar", lambda e, c=c: e.activation(
            snT[:, c * 64:(c + 1) * 64], tmp1[:], AF.Tanh),
            reads=[("tmp1",)], writes=[("snT", c)], hazard=True)
        p.op("vector", lambda e, c=c: e.tensor_sub(
            tmp1[:], s0T[:, c * 64:(c + 1) * 64], snT[:, c * 64:(c + 1) * 64]),
            reads=[("s0T",), ("snT", c)], writes=[("tmp1",)], hazard=True)
        p.op("vector", lambda e, c=c: e.tensor_mul(
            tmp1[:], tmp1[:], rz[:, (8 + c) * 64:(9 + c) * 64]),
            reads=[("tmp1",), ("rz", 8 + c)], writes=[("tmp1",)], hazard=True)
        p.op("vector", lambda e, c=c: e.tensor_add(
            snT[:, c * 64:(c + 1) * 64], snT[:, c * 64:(c + 1) * 64], tmp1[:]),
            reads=[("tmp1",), ("snT", c)], writes=[("snT", c)], hazard=True)
        p.op("vector", lambda e, c=c: e.tensor_copy(
            out=snTb[:, c * 64:(c + 1) * 64], in_=snT[:, c * 64:(c + 1) * 64]),
            reads=[("snT", c)], writes=[("snTb", c)])
    tapmap["snT"] = (snT, [("snT", c) for c in range(8)])

    qsT = m2a("qsT", [128, 2 * 64], BF16)
    ksT = m2a("ksT", [128, 2 * 64], BF16)
    vsT = m2a("vsT", [128, 8 * 64], BF16)
    wqk = m2a("wqk", [128, 8 * 256], BF16)
    wvv = m2a("wvv", [128, 8 * 1024], BF16)
    snk = [("snTb", c) for c in range(8)]
    g_pQS, pQS = palloc("pQS", [128, 64], F32)
    for wsrc, dst, dstk, bias_s, bkk, ncol in ((wq1, qsT, "qsT", bq1_s, "bq1_s", 2),
                                               (wk1, ksT, "ksT", bk1_s, "bk1_s", 2)):
        p.dma(qw, lambda e, wsrc=wsrc: e.dma_start(
            out=wqk[:].rearrange("p (dc n) -> p dc n", dc=8),
            in_=wsrc.rearrange("(dc p) n -> p dc n", p=128)),
            reads=[("wqk",)], writes=[("wqk",)])
        for qc in range(ncol):
            for dc in range(8):
                p.op("tensor", lambda e, qc=qc, dc=dc: e.matmul(
                    pQS[:], wqk[:, dc * 256 + qc * 128:dc * 256 + qc * 128 + 128],
                    snTb[:, dc * 64:(dc + 1) * 64],
                    start=(dc == 0), stop=(dc == 7)),
                    reads=[("wqk",)] + snk, writes=[("pQS",)])
            p.op("vector", lambda e, qc=qc, dst=dst, dstk=dstk, bias_s=bias_s,
                 bkk=bkk: e.tensor_scalar_add(
                dst[:, qc * 64:(qc + 1) * 64], pQS[:], bias_s[:, qc:qc + 1]),
                reads=[("pQS",), (bkk,)], writes=[(dstk,)])
    p.dma(qw, lambda e: e.dma_start(
        out=wvv[:].rearrange("p (dc n) -> p dc n", dc=8),
        in_=wv1.rearrange("(dc p) n -> p dc n", p=128)),
        reads=[("wvv",)], writes=[("wvv",)])
    for vc in range(8):
        for dc in range(8):
            p.op("tensor", lambda e, vc=vc, dc=dc: e.matmul(
                pQS[:], wvv[:, dc * 1024 + vc * 128:dc * 1024 + vc * 128 + 128],
                snTb[:, dc * 64:(dc + 1) * 64], start=(dc == 0), stop=(dc == 7)),
                reads=[("wvv",)] + snk, writes=[("pQS",)])
        p.op("vector", lambda e, vc=vc: e.tensor_scalar_add(
            vsT[:, vc * 64:(vc + 1) * 64], pQS[:], bv1_s[:, vc:vc + 1]),
            reads=[("pQS",), ("bv1_s",)], writes=[("vsT",)])

    satp = m2a("satp", [64, 64], BF16)
    satt = m2a("satt", [64, 64], BF16)
    sats = m2a("sats", [64, 4], F32)
    g_pSA, pSA = palloc("pSA", [64, 64], F32)
    for qc in range(2):
        p.op("tensor", lambda e, qc=qc: e.matmul(
            pSA[:], qsT[:, qc * 64:(qc + 1) * 64], ksT[:, qc * 64:(qc + 1) * 64],
            start=(qc == 0), stop=(qc == 1)),
            reads=[("qsT",), ("ksT",)], writes=[("pSA",)])
    p.op("vector", lambda e: e.reduce_max(sats[:, 0:1], pSA[:], axis=X,
                                          negate=True),
         reads=[("pSA",)], writes=[("sats",)], hazard=True)
    p.op("vector", lambda e: e.tensor_scalar_mul(sats[:, 1:2], sats[:, 0:1],
                                                 1.0 / 16.0),
         reads=[("sats",)], writes=[("sats",)], hazard=True)
    p.op("scalar", lambda e: e.activation(satp[:], pSA[:], AF.Exp,
                                          bias=sats[:, 1:2], scale=1.0 / 16.0),
         reads=[("pSA",), ("sats",)], writes=[("satp",)], hazard=True)
    p.op("vector", lambda e: e.reduce_sum(sats[:, 2:3], satp[:], axis=X),
         reads=[("satp",)], writes=[("sats",)], hazard=True)
    p.op("vector", lambda e: e.reciprocal(sats[:, 3:4], sats[:, 2:3]),
         reads=[("sats",)], writes=[("sats",)], hazard=True)
    p.op("vector", lambda e: e.tensor_scalar_mul(satp[:], satp[:],
                                                 sats[:, 3:4]),
         reads=[("satp",), ("sats",)], writes=[("satp",)], hazard=True)
    p.op("tensor", lambda e: e.transpose(pTR[0:64, 0:64], satp[:],
                                         cm.identb[0:64, 0:64]),
         reads=[("satp",), ("c_identb",)], writes=[("pTR",)])
    p.op("vector", lambda e: e.tensor_copy(out=satt[:], in_=pTR[0:64, 0:64]),
         reads=[("pTR",)], writes=[("satt",)])
    vssm = m2a("vssm", [64, 8 * 128], BF16)
    for c in range(8):
        p.op("tensor", lambda e, c=c: e.transpose(
            pTR[0:64, :], vsT[:, c * 64:(c + 1) * 64], cm.identb[:]),
            reads=[("vsT",), ("c_identb",)], writes=[("pTR",)])
        p.op("vector", lambda e, c=c: e.tensor_copy(
            out=vssm[:, c * 128:(c + 1) * 128], in_=pTR[0:64, 0:128]),
            reads=[("pTR",)], writes=[("vssm",)])

    yT = m2a("yT", [128, 8 * 64], F32)
    yTb = m2a("yTb", [128, 8 * 64], BF16)
    ysq = m2a("ysq", [128, 8 * 64], BF16)
    g_pSC, pSC = palloc("pSC", [128, 64], F32)
    for c in range(8):
        p.op("tensor", lambda e, c=c: e.matmul(
            pSC[:], vssm[:, c * 128:(c + 1) * 128], satt[:],
            start=True, stop=True),
            reads=[("vssm",), ("satt",)], writes=[("pSC",)])
        p.op("vector", lambda e, c=c: e.tensor_add(
            yT[:, c * 64:(c + 1) * 64], snT[:, c * 64:(c + 1) * 64], pSC[:]),
            reads=[("pSC",), ("snT", c)], writes=[("yT", c)])
        p.op("vector", lambda e, c=c: e.tensor_copy(
            out=yTb[:, c * 64:(c + 1) * 64], in_=yT[:, c * 64:(c + 1) * 64]),
            reads=[("yT", c)], writes=[("yTb", c)])
        p.op("scalar", lambda e, c=c: e.activation(
            ysq[:, c * 64:(c + 1) * 64], yT[:, c * 64:(c + 1) * 64], AF.Square),
            reads=[("yT", c)], writes=[("ysq", c)])
    lnstat = m2a("lnstat", [1, 8 * 64], F32)
    g_pLN, pLN = palloc("pLN", [1, 64], F32)
    ykeys = [("yTb", c) for c in range(8)]
    sqkeys = [("ysq", c) for c in range(8)]
    for which, srck, col0 in ((yTb, ykeys, 0), (ysq, sqkeys, 64)):
        for c in range(8):
            p.op("tensor", lambda e, c=c, which=which: e.matmul(
                pLN[:], cm.onesb[:, 0:1], which[:, c * 64:(c + 1) * 64],
                start=(c == 0), stop=(c == 7)),
                reads=srck + [("c_onesb",)], writes=[("pLN",)])
        p.op("vector", lambda e, col0=col0: e.tensor_copy(
            out=lnstat[:, col0:col0 + 64], in_=pLN[:]),
            reads=[("pLN",)], writes=[("lnstat",)], hazard=True)
    p.op("vector", lambda e: e.tensor_scalar_mul(
        lnstat[:, 128:192], lnstat[:, 0:64], 1.0 / D),
        reads=[("lnstat",)], writes=[("lnstat",)], hazard=True)
    p.op("vector", lambda e: e.scalar_tensor_tensor(
        lnstat[:, 192:256], lnstat[:, 128:192], 1.0, lnstat[:, 128:192],
        ALU.mult, ALU.mult),
        reads=[("lnstat",)], writes=[("lnstat",)], hazard=True)
    p.op("vector", lambda e: e.scalar_tensor_tensor(
        lnstat[:, 256:320], lnstat[:, 64:128], 1.0 / D, lnstat[:, 192:256],
        ALU.mult, ALU.subtract),
        reads=[("lnstat",)], writes=[("lnstat",)], hazard=True)
    p.op("vector", lambda e: e.tensor_scalar_add(
        lnstat[:, 256:320], lnstat[:, 256:320], 1e-5),
        reads=[("lnstat",)], writes=[("lnstat",)], hazard=True)
    p.op("scalar", lambda e: e.activation(
        lnstat[:, 320:384], lnstat[:, 256:320], AF.Sqrt),
        reads=[("lnstat",)], writes=[("lnstat",)], hazard=True)
    p.op("vector", lambda e: e.reciprocal(
        lnstat[:, 320:384], lnstat[:, 320:384]),
        reads=[("lnstat",)], writes=[("lnstat",)], hazard=True)
    p.op("vector", lambda e: e.scalar_tensor_tensor(
        lnstat[:, 384:448], lnstat[:, 128:192], -1.0, lnstat[:, 320:384],
        ALU.mult, ALU.mult),
        reads=[("lnstat",)], writes=[("lnstat",)], hazard=True)
    abb = m2a("abb", [128, 128], F32)
    g_pAB, pAB = palloc("pAB", [128, 128], F32)
    p.op("tensor", lambda e: e.matmul(
        pAB[:, 0:64], cm.onesf[0:1, 0:128], lnstat[:, 320:384],
        start=True, stop=True),
        reads=[("lnstat",), ("c_onesf",)], writes=[("pAB",)])
    p.op("tensor", lambda e: e.matmul(
        pAB[:, 64:128], cm.onesf[0:1, 0:128], lnstat[:, 384:448],
        start=True, stop=True),
        reads=[("lnstat",), ("c_onesf",)], writes=[("pAB",)])
    p.op("vector", lambda e: e.tensor_copy(out=abb[:], in_=pAB[:]),
         reads=[("pAB",)], writes=[("abb",)])
    ynTb = m2a("ynTb", [128, 8 * 64], BF16)
    for c in range(8):
        p.op("vector", lambda e, c=c: e.tensor_mul(
            tmp1[:], yT[:, c * 64:(c + 1) * 64], abb[:, 0:64]),
            reads=[("yT", c), ("abb",)], writes=[("tmp1",)], hazard=True)
        p.op("vector", lambda e, c=c: e.tensor_add(tmp1[:], tmp1[:],
                                                   abb[:, 64:128]),
             reads=[("tmp1",), ("abb",)], writes=[("tmp1",)], hazard=True)
        p.op("scalar", lambda e, c=c: e.activation(
            ynTb[:, c * 64:(c + 1) * 64], tmp1[:], AF.Identity,
            bias=lnb_s[:, c:c + 1], scale=lng_s[:, c:c + 1]),
            reads=[("tmp1",), ("lnb_s",), ("lng_s",)], writes=[("ynTb", c)])
    svT = m2a("svT", [128, 8 * 64], BF16)
    ynk = [("ynTb", c) for c in range(8)]
    p.dma(qw, lambda e: e.dma_start(
        out=wvv[:].rearrange("p (dc n) -> p dc n", dc=8),
        in_=wvp.rearrange("(dc p) n -> p dc n", p=128)),
        reads=[("wvv",)], writes=[("wvv",)])
    for vc in range(8):
        for dc in range(8):
            p.op("tensor", lambda e, vc=vc, dc=dc: e.matmul(
                pQS[:], wvv[:, dc * 1024 + vc * 128:dc * 1024 + vc * 128 + 128],
                ynTb[:, dc * 64:(dc + 1) * 64], start=(dc == 0), stop=(dc == 7)),
                reads=[("wvv",)] + ynk, writes=[("pQS",)])
        p.op("vector", lambda e, vc=vc: e.tensor_scalar_add(
            svT[:, vc * 64:(vc + 1) * 64], pQS[:], bvp_s[:, vc:vc + 1]),
            reads=[("pQS",), ("bvp_s",)], writes=[("svT",)])
    svsm = nc.alloc_sbuf_tensor("svsm", [64, 8 * 128], BF16)
    for c in range(8):
        p.op("tensor", lambda e, c=c: e.transpose(
            pTR[0:64, :], svT[:, c * 64:(c + 1) * 64], cm.identb[:]),
            reads=[("svT",), ("c_identb",)], writes=[("pTR",)])
        p.op("vector", lambda e, c=c: e.tensor_copy(
            out=svsm[:, c * 128:(c + 1) * 128], in_=pTR[0:64, 0:128]),
            reads=[("pTR",)], writes=[("svsm",)])
    tapmap["svsm"] = (svsm, [("svsm",)])

    for g in (g_pAB, g_pLN, g_pSC, g_pSA, g_pQS, g_pGI, g_pWT, g_pTR):
        g.__exit__(None, None, None)
    for g in reversed(mid2):
        g.__exit__(None, None, None)
    p.barrier()

    g_outtokT = nc.sbuf_tensor("outtokT", [128, 8 * HALF], BF16, side="right")
    outtokT = g_outtokT.__enter__()
    g_woa = nc.sbuf_tensor("woa", [128, 8 * D], BF16, side="right")
    woa = g_woa.__enter__()
    p.dma(qw, lambda e: e.dma_start(
        out=woa[:].rearrange("p (dc n) -> p dc n", dc=8),
        in_=wo.rearrange("(dc p) n -> p dc n", p=128)),
        writes=[("woa",)])
    g_pOT0, pOT0 = palloc("pOT0", [128, 512], F32)
    g_pOT1, pOT1 = palloc("pOT1", [128, 512], F32)
    for wnd in range(4):
        for dc in range(8):
            ps, psk = (pOT0, ("pOT0",)) if dc % 2 == 0 else (pOT1, ("pOT1",))
            p.op("tensor", lambda e, wnd=wnd, dc=dc, ps=ps: e.matmul(
                ps[:], svsm[:, dc * 128:(dc + 1) * 128],
                alphaT[0:64, wnd * 512:(wnd + 1) * 512],
                start=True, stop=True),
                reads=[("svsm",)] + [("aT1", t) for t in range(wnd * 4, wnd * 4 + 4)],
                writes=[psk])
            p.op("vector", lambda e, wnd=wnd, dc=dc, ps=ps: e.tensor_copy(
                out=outtokT[:, dc * HALF + wnd * 512:dc * HALF + wnd * 512 + 512],
                in_=ps[:]),
                reads=[psk], writes=[("otT", dc, wnd)])
    for ti in range(16):
        for half in range(2):
            ps, psk = (pOT0, ("pOT0",)) if half == 0 else (pOT1, ("pOT1",))
            for dc in range(8):
                p.op("tensor", lambda e, ti=ti, half=half, dc=dc, ps=ps:
                     e.matmul(
                    ps[:, 0:128] if False else ps[:],
                    outtokT[:, dc * HALF + ti * 128:dc * HALF + ti * 128 + 128],
                    woa[:, dc * D + half * 512:dc * D + half * 512 + 512],
                    start=(dc == 0), stop=(dc == 7)),
                    reads=[("otT", dc, ti // 4), ("woa",)], writes=[psk])
            p.op("vector", lambda e, ti=ti, half=half, ps=ps: e.tensor_add(
                x2acc[:, ti * D + half * 512:ti * D + half * 512 + 512],
                x2acc[:, ti * D + half * 512:ti * D + half * 512 + 512],
                ps[:]),
                reads=[psk, ("x2", ti)], writes=[("x2", ti)], hazard=True)
        p.dma(qx, lambda e, ti=ti: e.dma_start(
            out=y[ti * 128:(ti + 1) * 128, :], in_=x2acc[:, ti * D:(ti + 1) * D]),
            reads=[("x2", ti)], writes=[("yout", ti)])

    g_pOT1.__exit__(None, None, None)
    g_pOT0.__exit__(None, None, None)
    g_woa.__exit__(None, None, None)
    g_outtokT.__exit__(None, None, None)
    g_x2acc.__exit__(None, None, None)
    return


def build_l1(taps=()):
    nc = bass.Bass("TRN2", target_bir_lowering=False)
    x1 = nc.dram_tensor("x1", [SEQ, D], F32, kind="ExternalInput").ap()
    ins = l1_dram_ins(nc)
    y = nc.dram_tensor("y", [HALF, D], F32, kind="ExternalOutput").ap()
    p = _mk_prog(nc)
    tapmap = {}
    cm = Common(nc, p)
    emit_l1(nc, p, cm, tapmap, ins, x1, y, own_only=False)
    return _do_finish(nc, p, tapmap, taps)


def build_fused(taps=()):
    nc = bass.Bass("TRN2", target_bir_lowering=False)
    ins0 = l0_dram_ins(nc)
    ins1 = l1_dram_ins(nc)
    x1s = nc.dram_tensor("x1s", [HALF, D], F32).ap()
    cc_in = nc.dram_tensor("cc_in", [64, 1025], F32).ap()
    cc_out = nc.dram_tensor("cc_out", [64, 1025], F32).ap()
    y = nc.dram_tensor("y", [HALF, D], F32, kind="ExternalOutput").ap()
    p = _mk_prog(nc)
    p.dma_queue("qc", "gpsimd")
    tapmap = {}
    cm = Common(nc, p)
    emit_l0(nc, p, cm, tapmap, ins0, x1s)
    nc1 = NPfx(nc, "z_")
    p1 = KeyPrefix(p, "z_", exempt=("y1out", "c_identf", "c_identb",
                                    "c_onesf", "c_onesb", "c_hmask"))
    cc = dict(cc_in=cc_in, cc_out=cc_out, queue="qc",
              groups=[[0, 1], [2, 3], [4, 5], [6, 7]])
    emit_l1(nc1, p1, cm, tapmap, ins1, x1s, y, own_only=True,
            x1_rk=lambda i: [("y1out", i)], cc=cc)
    return _do_finish(nc, p, tapmap, taps)



_CACHE = {}


def _rot(xseq, h):
    if h == 0:
        return np.ascontiguousarray(xseq)
    return np.ascontiguousarray(
        np.concatenate([xseq[HALF:], xseq[:HALF]], axis=0))


def kernel(**inputs):
    if "fused" not in _CACHE:
        _CACHE["fused"] = build_fused()[0]
    nc = _CACHE["fused"]
    x = np.asarray(inputs["L0_attn_x"], dtype=np.float32)
    w = dict(
        w_qkv=inputs["L0_attn_w_qkv"], w_proj=inputs["L0_attn_w_proj"],
        b_proj=inputs["L0_attn_b_proj"], gate_w=inputs["L0_ffn_gate_w"],
        w_gate=inputs["L0_ffn_w_gate"], w_up=inputs["L0_ffn_w_up"],
        w_down=inputs["L0_ffn_w_down"], sw_gate=inputs["L0_ffn_sw_gate"],
        sw_up=inputs["L0_ffn_sw_up"], sw_down=inputs["L0_ffn_sw_down"],
        Wq=inputs["L1_attn_Wq"], bq=inputs["L1_attn_bq"],
        Wk=inputs["L1_attn_Wk"], bk=inputs["L1_attn_bk"],
        Wv=inputs["L1_attn_Wv"], bv=inputs["L1_attn_bv"],
        slot_init=inputs["L1_ffn_slot_init"], w1=inputs["L1_ffn_w1"],
        b1=inputs["L1_ffn_b1"], w2=inputs["L1_ffn_w2"], b2=inputs["L1_ffn_b2"],
        tau=inputs["L1_ffn_tau"], Wih=inputs["L1_ffn_Wih"],
        Whh=inputs["L1_ffn_Whh"], bih=inputs["L1_ffn_bih"],
        bhh=inputs["L1_ffn_bhh"], wq1=inputs["L1_ffn_wq"],
        bq1=inputs["L1_ffn_bq"], wk1=inputs["L1_ffn_wk"],
        bk1=inputs["L1_ffn_bk"], wv1=inputs["L1_ffn_wv"],
        bv1=inputs["L1_ffn_bv"], ln_g=inputs["L1_ffn_ln_g"],
        ln_b=inputs["L1_ffn_ln_b"], wvp=inputs["L1_ffn_wvp"],
        bvp=inputs["L1_ffn_bvp"], wo=inputs["L1_ffn_wo"],
        bo=inputs["L1_ffn_bo"])
    w = {k: np.ascontiguousarray(np.asarray(v, dtype=np.float32))
         for k, v in w.items()}
    in_maps = []
    for core in range(8):
        b, h = core // 2, core % 2
        im = {"x": _rot(x[b], h)}
        im.update(w)
        in_maps.append(im)
    res = run_bass_kernel_spmd(nc, in_maps, core_ids=list(range(8)))
    y = np.empty((4, SEQ, D), dtype=np.float32)
    for core in range(8):
        b, h = core // 2, core % 2
        y[b, h * HALF:(h + 1) * HALF] = res.results[core]["y"]
    return y
